# version 30
# baseline (speedup 1.0000x reference)
import os
import numpy as np
import ml_dtypes
from contextlib import ExitStack
import concourse.bass as bass
import concourse.mybir as mybir
from concourse.bass_utils import run_bass_kernel_spmd

F32 = mybir.dt.float32
BF16 = mybir.dt.bfloat16
AF = mybir.ActivationFunctionType
ALU = mybir.AluOpType
AX = mybir.AxisListType

D = 1024
H = 16
DH = 64
E = 8
DCH = 8
SUB = 128
SPC = 2
T = SUB * SPC
STRICT_ENGINES = set(os.environ.get("KSTRICT", "dve,act,pool").split(",")) - {""}



class Buf:
    __slots__ = ("name", "last_w", "readers", "overlaps")

    def __init__(self, name):
        self.name = name
        self.last_w = None
        self.readers = []
        self.overlaps = []


def link_overlaps(*bufs):
    for b in bufs:
        for b2 in bufs:
            if b2 is not b and b2 not in b.overlaps:
                b.overlaps.append(b2)


class Eng:
    __slots__ = ("name", "obj", "sem", "dma", "n_ops", "incs", "cum", "known",
                 "last_inst", "insts")

    def __init__(self, name, obj, sem, dma):
        self.name = name
        self.obj = obj
        self.sem = sem
        self.dma = dma
        self.n_ops = 0
        self.incs = []
        self.cum = 0
        self.known = {}
        self.last_inst = None
        self.insts = []


class KB:
    def __init__(self, nc, es):
        self.nc = nc
        self.es = es
        self.engs = {}

    def eng(self, name, obj, dma=False):
        sem = self.es.enter_context(self.nc.semaphore(f"sem_{name}"))
        self.engs[name] = Eng(name, obj, sem, dma)
        return self.engs[name]

    def _value_for(self, peng, idx):
        lo = None
        for oi, val in reversed(peng.incs):
            if oi >= idx:
                lo = val
            else:
                break
        if lo is not None:
            return lo
        assert not peng.dma, "dma producers always have eager incs"
        j = min(idx + 1, peng.n_ops - 1)
        peng.cum += 1
        peng.insts[j].then_inc(peng.sem, 1)
        peng.incs.append((j, peng.cum))
        return peng.cum

    def buf_sem(self, buf, qname):
        pname = f"dq_{qname}_{buf.name}"
        if pname not in self.engs:
            sem = self.es.enter_context(self.nc.semaphore(pname))
            pe = Eng(pname, None, sem, True)
            self.engs[pname] = pe
        return pname

    def op(self, ename, fn, reads=(), writes=(), dma_buf=None, exotic=False):
        eng = self.engs[ename]
        assert (dma_buf is None) or eng.dma
        waits = {}

        def need(pname, idx, pstrict=False):
            if (pname == ename and not eng.dma and not (pstrict or exotic)
                    and ename not in STRICT_ENGINES):
                return
            if pname == ename and (eng.dma or ename == "pe"):
                return
            if waits.get(pname, -1) < self._tok_val(pname, idx):
                waits[pname] = self._tok_val(pname, idx)

        seen = set()
        for b in reads:
            for bb in (b, *b.overlaps):
                if id(bb) in seen:
                    continue
                if bb.last_w:
                    need(*bb.last_w)
            seen.add(id(b))
        for b in writes:
            for bb in (b, *b.overlaps):
                if bb.last_w:
                    need(*bb.last_w)
                for r in bb.readers:
                    need(*r)
        for pname, val in waits.items():
            if eng.known.get(pname, -1) < val:
                eng.obj.wait_ge(self.engs[pname].sem, val)
                eng.known[pname] = val
        inst = fn(eng.obj)
        eng.n_ops += 1
        eng.last_inst = inst
        eng.insts.append(inst)
        if eng.dma:
            assert dma_buf is not None, "DMA ops must name their slot buffer"
            pname = self.buf_sem(dma_buf, ename)
            peng = self.engs[pname]
            peng.cum += 16
            inst.then_inc(peng.sem, 16)
            tok = (pname, peng.cum, False)
        else:
            tok = (ename, eng.n_ops - 1, exotic)
        for b in writes:
            b.last_w = tok
            b.readers = []
        for b in reads:
            b.readers.append(tok)
        return inst

    def _tok_val(self, pname, idx):
        peng = self.engs[pname]
        if peng.dma:
            return idx
        return self._value_for(peng, idx)

    def final_wait(self, ename):
        eng = self.engs[ename]
        for pname, peng in self.engs.items():
            if pname == ename or (peng.n_ops == 0 and not peng.dma):
                continue
            if peng.dma:
                if peng.cum == 0:
                    continue
                val = peng.cum
            else:
                val = self._value_for(peng, peng.n_ops - 1)
            if eng.known.get(pname, -1) < val:
                eng.obj.wait_ge(peng.sem, val)
                eng.known[pname] = val



def build_nc(n_tokens=8192, debug=0):
    assert n_tokens % T == 0
    nch = n_tokens // T
    nsub = n_tokens // SUB

    nc = bass.Bass("TRN2", target_bir_lowering=False)

    dx = nc.dram_tensor("x", [n_tokens, D], F32, kind="ExternalInput").ap()
    dy = nc.dram_tensor("y", [n_tokens, D], F32, kind="ExternalOutput").ap()
    dwq = nc.dram_tensor("wq", [128, 8 * D], F32, kind="ExternalInput").ap()
    dwk = nc.dram_tensor("wk", [128, 8 * D], F32, kind="ExternalInput").ap()
    dwv = nc.dram_tensor("wv", [128, 8 * D], F32, kind="ExternalInput").ap()
    dwo = nc.dram_tensor("wo", [128, 8 * D], F32, kind="ExternalInput").ap()
    dwa = nc.dram_tensor("wa", [128, 8 * H], F32, kind="ExternalInput").ap()
    dgw = nc.dram_tensor("gw", [128, 8 * E], F32, kind="ExternalInput").ap()
    dexpw = nc.dram_tensor("expw", [E, 128, 8 * D], F32, kind="ExternalInput").ap()
    dexpb = nc.dram_tensor("expb", [E, D], F32, kind="ExternalInput").ap()
    dwmod = nc.dram_tensor("wmod", [128, 48 * D], F32, kind="ExternalInput").ap()
    dc = nc.dram_tensor("cmod", [128, 8], F32, kind="ExternalInput").ap()
    dba = nc.dram_tensor("ba_bc", [128, H], F32, kind="ExternalInput").ap()
    dgb = nc.dram_tensor("gateb_bc", [128, E], F32, kind="ExternalInput").ap()
    dtri = nc.dram_tensor("tri", [128, 128], F32, kind="ExternalInput").ap()
    dident = nc.dram_tensor("identbf", [128, 128], BF16, kind="ExternalInput").ap()
    donesb = nc.dram_tensor("onesbf", [128, 1], BF16, kind="ExternalInput").ap()
    doneh = nc.dram_tensor("oneh127", [128, 1], F32, kind="ExternalInput").ap()
    dones64 = nc.dram_tensor("ones1x64", [1, 64], F32, kind="ExternalInput").ap()
    dones128 = nc.dram_tensor("ones1x128", [1, 128], F32, kind="ExternalInput").ap()
    dcvec = nc.dram_tensor("cvec", [128, 4], F32, kind="ExternalInput").ap()
    decache = nc.dram_tensor("ecache", [E, 128, 8 * D], BF16, kind="Internal").ap()
    if debug:
        ddbg = nc.dram_tensor("dbg", [debug, 128, D], F32, kind="ExternalOutput").ap()

    es = ExitStack()
    kb = KB(nc, es)
    S = lambda name, shape, dt: es.enter_context(nc.sbuf_tensor(name, shape, dt))
    P = lambda name, shape, dt: es.enter_context(nc.psum_tensor(name, shape, dt))

    xr = [S(f"xr{i}", [128, D], F32) for i in range(4)]
    wq_bf = S("wq_bf", [128, 8 * D], BF16)
    wk_bf = S("wk_bf", [128, 8 * D], BF16)
    wv_bf = S("wv_bf", [128, 8 * D], BF16)
    wo_bf = S("wo_bf", [128, 8 * D], BF16)
    wa_bf = S("wa_bf", [128, 8 * H], BF16)
    gw_bf = S("gw_bf", [128, 8 * E], BF16)
    expb_f = S("expb_f", [E, D], F32)
    expb_bf = S("expb_bf", [E, D], BF16)
    ering = [S(f"ering{i}", [128, 4096], BF16) for i in range(4)]
    wring = [S(f"wring{i}", [128, D], F32) for i in range(4)]
    sf1 = S("sf1", [128, D], F32)
    s1 = S("s1", [128, D], BF16)
    s2 = S("s2", [128, D], BF16)
    h1_bf = S("h1_bf", [128, D], BF16)
    h1_fm = S("h1_fm", [128, D], BF16)
    qt_tm = S("qt_tm", [128, D], BF16)
    kt_tm = S("kt_tm", [128, D], BF16)
    v_tm = S("v_tm", [128, D], BF16)
    qt_fm = S("qt_fm", [128, D], BF16)
    kt_fm = S("kt_fm", [128, D], BF16)
    smt = [S(f"smt{i}", [128, 128], BF16) for i in range(2)]
    ao_tm = S("ao_tm", [128, D], BF16)
    ao_fm = S("ao_fm", [128, D], BF16)
    h2_tm = S("h2_tm", [128, D], BF16)
    h2_fm = S("h2_fm", [128, SPC * D], BF16)
    Sst = S("Sst", [128, D], BF16)
    zst = S("zst", [128, H], BF16)
    kvbf = S("kvbf", [128, D], BF16)
    zvbf = S("zvbf", [128, H], BF16)
    u16 = S("u16", [128, H], F32)
    t16a = S("t16a", [128, H], F32)
    t16b = S("t16b", [128, H], F32)
    lna = S("lna", [128, H], F32)
    el = S("el", [128, H], F32)
    enl = S("enl", [128, H], F32)
    el_bf = S("el_bf", [128, H], BF16)
    enl_bf = S("enl_bf", [128, H], BF16)
    elcrow = S("elcrow", [1, H], F32)
    elc_bf = S("elc_bf", [128, H], BF16)
    den_sb = S("den_sb", [128, H], F32)
    rden = S("rden", [128, H], F32)
    stat = S("stat", [128, 16], F32)
    stat2 = S("stat2", [128, 2], F32)
    snap = S("snap", [128, 8], F32)
    g8 = S("g8", [128, 6 * E], F32)
    mrep = S("mrep", [128, 8], F32)
    m1c = S("m1c", [128, 4], F32)
    combc = S("combc", [128, SPC * E], F32)
    comb_bfx = S("comb_bfx", [128, SPC * E], BF16)
    combT_bf = S("combT_bf", [8, SPC * 128], BF16)
    c_sil = S("c_sil", [128, 8], F32)
    c_t8 = S("c_t8", [128, 16], F32)
    rowtmp = S("rowtmp", [1, D], F32)
    ca1 = S("ca1", [128, D], F32)
    cb1 = S("cb1", [128, D], F32)
    sa_f = S("sa_f", [128, D], F32)
    sb_f = S("sb_f", [128, D], F32)
    ga_exp = S("ga_exp", [128, D], F32)
    gb_exp = S("gb_exp", [128, D], F32)
    babc2 = S("babc2", [128, H], F32)
    gateb_t = S("gateb_t", [128, E], F32)
    tri_sb = S("tri_sb", [128, 128], F32)
    ident_sb = S("ident_sb", [128, 128], BF16)
    onesb_sb = S("onesb_sb", [128, 1], BF16)
    oneh_sb = S("oneh_sb", [128, 1], F32)
    ones64_sb = S("ones64_sb", [1, 64], F32)
    ones128_sb = S("ones128_sb", [1, 128], F32)
    cvec = S("cvec_sb", [128, 4], F32)

    P01 = P("P01", [128, D], F32)
    P23 = P("P23", [128, D], F32)
    P45 = P("P45", [128, D], F32)
    P6 = P("P6", [128, 512], F32)
    P7 = P("P7", [128, D], BF16)

    B = lambda n: Buf(n)
    b_xr = [B(f"xr{i}") for i in range(4)]
    b_wq, b_wk, b_wv, b_wo = B("wq"), B("wk"), B("wv"), B("wo")
    b_wa, b_gw = B("wa"), B("gw")
    b_expbf, b_expbbf = B("expb_f"), B("expb_bf")
    b_ering = [B(f"ering{i}") for i in range(4)]
    b_wring = [B(f"wring{i}") for i in range(4)]
    b_sf1, b_s1, b_s2 = B("sf1"), B("s1"), B("s2")
    b_h1bf = B("h1_bf")
    b_h1fm = [B(f"h1_fm{u}") for u in range(DCH)]
    b_qttm = [B(f"qt_tm{u}") for u in range(DCH)]
    b_kttm = [B(f"kt_tm{u}") for u in range(DCH)]
    b_vtm = B("v_tm")
    b_qtfm, b_ktfm = B("qt_fm"), B("kt_fm")
    b_smt = [B("smt0"), B("smt1")]
    b_aotm = [B(f"ao_tm{u}") for u in range(DCH)]
    b_aofm = B("ao_fm")
    b_h2tm = B("h2_tm")
    b_h2fm = [[B(f"h2_fm{s}_{u}") for u in range(DCH)] for s in range(SPC)]
    b_S, b_z = B("Sst"), B("zst")
    b_kvbf, b_zvbf = B("kvbf"), B("zvbf")
    b_u16, b_t16a, b_t16b, b_lna = B("u16"), B("t16a"), B("t16b"), B("lna")
    b_el, b_enl, b_elcrow, b_elcbf = B("el"), B("enl"), B("elcrow"), B("elc_bf")
    b_elbf, b_enlbf = B("el_bf"), B("enl_bf")
    b_den, b_rden, b_stat = B("den"), B("rden"), B("stat")
    b_stat2 = B("stat2")
    b_snap = B("snap")
    b_g8, b_mrep, b_m1c = B("g8"), B("mrep"), B("m1c")
    b_comb, b_combbfx, b_combT = B("combc"), B("comb_bfx"), B("combT_bf")
    b_csil, b_ct8, b_rowtmp = B("c_sil"), B("c_t8"), B("rowtmp")
    b_ca1, b_cb1, b_saf, b_sbf = B("ca1"), B("cb1"), B("sa_f"), B("sb_f")
    b_gaexp, b_gbexp = B("ga_exp"), B("gb_exp")
    b_babc2, b_gatebt = B("babc2"), B("gateb_t")
    b_tri, b_ident, b_onesb, b_oneh, b_ones64 = B("tri"), B("ident"), B("onesb"), B("oneh"), B("ones64")
    b_ones128 = B("ones128")
    b_cvec = B("cvec")
    b_ecache = [[B(f"ec{e}_{q}") for q in range(2)] for e in range(E)]
    b_y = B("y")

    b_pq, b_scA, b_scB, b_py, b_fe0 = B("pq"), B("scA"), B("scB"), B("py"), B("fe0")
    link_overlaps(b_pq, b_py, b_fe0)
    for bb in (b_pq, b_py, b_fe0):
        link_overlaps(bb, b_scA)
        link_overlaps(bb, b_scB)
    b_pk, b_kvS, b_fe1 = B("pk"), B("kvS"), B("fe1")
    link_overlaps(b_pk, b_kvS, b_fe1)
    b_pv, b_po, b_pbm, b_fe2, b_pmod = B("pv"), B("po"), B("pbm"), B("fe2"), B("pmod")
    link_overlaps(b_pv, b_po, b_pbm, b_fe2, b_pmod)
    b_p6 = B("p6")
    b_p7 = B("p7")
    b_modrow = B("modrow")
    link_overlaps(b_modrow, b_sf1)

    scA = P01[:, 0:128]
    scB = P01[:, 512:640]
    p6a = P6[:, 0:16]
    p6L = P6[:, 16:32]
    p6den = P6[:, 32:48]
    p6kvz = P6[:, 48:64]
    p6er = P6[0:1, 64:80]
    p6eb = P6[:, 80:96]
    p6g = P6[:, 96:104]

    with nc.Block():
        pe = kb.eng("pe", nc.tensor)
        dve = kb.eng("dve", nc.vector)
        act = kb.eng("act", nc.scalar)
        qs = kb.eng("qs", nc.sync, dma=True)
        qg = kb.eng("qg", nc.gpsimd, dma=True)
        pool = kb.eng("pool", nc.gpsimd, dma=False)
        qa = kb.eng("qa", nc.scalar, dma=True)

        OP = kb.op

        def warm(target_ap, rbufs, wbuf):
            OP("pe", lambda e: e.matmul(out=target_ap, lhsT=onesb_sb[:], rhs=onesb_sb[:],
                                        start=True, stop=True),
               reads=list(rbufs), writes=[wbuf])

        nc.scalar.add_instruction(mybir.InstLoadActFuncSet(
            name=nc.get_next_instruction_name(), act_func_set_id=6, ins=[], outs=[]))
        dbg_i = [0]
        dbg_f32 = S("dbg_f32", [128, D], F32) if debug else None
        b_dbgf = B("dbgf")
        b_dbgq = B("dbgq")

        def dump(ap, rbufs, ncols=D, cast=False, nparts=128):
            if not debug or dbg_i[0] >= debug:
                return
            i = dbg_i[0]
            dbg_i[0] += 1
            if cast:
                OP("act", lambda e: e.copy(out=dbg_f32[:, 0:ncols], in_=ap),
                   reads=list(rbufs), writes=[b_dbgf])
                OP("qg", lambda e, i=i: e.dma_start(out=ddbg[i, :, 0:ncols],
                                                    in_=dbg_f32[:, 0:ncols]),
                   reads=[b_dbgf], writes=[b_dbgq], dma_buf=b_dbgq)
            else:
                OP("qg", lambda e, i=i: e.dma_start(out=ddbg[i, 0:nparts, 0:ncols], in_=ap),
                   reads=list(rbufs) + [b_dbgq], writes=[b_dbgq], dma_buf=b_dbgq)
        kb.dump = dump

        OP("qs", lambda e: e.dma_start(out=tri_sb[:], in_=dtri), writes=[b_tri], dma_buf=b_tri)
        OP("qs", lambda e: e.dma_start(out=ident_sb[:], in_=dident), writes=[b_ident], dma_buf=b_ident)
        OP("qs", lambda e: e.dma_start(out=onesb_sb[:], in_=donesb), writes=[b_onesb], dma_buf=b_onesb)
        OP("qs", lambda e: e.dma_start(out=oneh_sb[:], in_=doneh), writes=[b_oneh], dma_buf=b_oneh)
        OP("qs", lambda e: e.dma_start(out=ones64_sb[:], in_=dones64), writes=[b_ones64], dma_buf=b_ones64)
        OP("qs", lambda e: e.dma_start(out=ones128_sb[:], in_=dones128), writes=[b_ones128], dma_buf=b_ones128)
        OP("qs", lambda e: e.dma_start(out=cvec[:], in_=dcvec), writes=[b_cvec], dma_buf=b_cvec)
        C_EPS, C_NHALF, C_NEG1 = cvec[:, 0:1], cvec[:, 1:2], cvec[:, 2:3]
        OP("qs", lambda e: e.dma_start(out=c_sil[:], in_=dc), writes=[b_csil], dma_buf=b_csil)
        OP("qs", lambda e: e.dma_start(out=babc2[:], in_=dba), writes=[b_babc2], dma_buf=b_babc2)
        OP("qs", lambda e: e.dma_start(out=gateb_t[:], in_=dgb), writes=[b_gatebt], dma_buf=b_gatebt)
        OP("qs", lambda e: e.dma_start(out=expb_f[:], in_=dexpb), writes=[b_expbf], dma_buf=b_expbf)
        for g in range(min(4, nsub)):
            OP("qg", lambda e, g=g: e.dma_start(out=xr[g][:], in_=dx[g * 128:(g + 1) * 128, :]),
               writes=[b_xr[g]], dma_buf=b_xr[g])

        OP("dve", lambda e: e.memset(Sst[:], 0.0), writes=[b_S])
        OP("dve", lambda e: e.memset(zst[:], 0.0), writes=[b_z])
        OP("dve", lambda e: e.memset(mrep[:], -1e30), writes=[b_mrep])
        OP("dve", lambda e: e.tensor_scalar(out=babc2[:], in0=babc2[:], scalar1=2.0,
                                            scalar2=None, op0=ALU.add),
           reads=[b_babc2], writes=[b_babc2])
        OP("act", lambda e: e.activation(out=c_t8[:, 0:8], in_=c_sil[:], func=AF.Exp,
                                         scale=C_NEG1),
           reads=[b_csil, b_cvec], writes=[b_ct8])
        OP("dve", lambda e: e.tensor_scalar(out=c_t8[:, 0:8], in0=c_t8[:, 0:8], scalar1=1.0,
                                            scalar2=None, op0=ALU.add),
           reads=[b_ct8], writes=[b_ct8])
        OP("dve", lambda e: e.reciprocal(out=c_t8[:, 8:16], in_=c_t8[:, 0:8]),
           reads=[b_ct8], writes=[b_ct8], exotic=True)
        OP("dve", lambda e: e.tensor_tensor(out=c_sil[:], in0=c_sil[:], in1=c_t8[:, 8:16],
                                            op=ALU.mult),
           reads=[b_csil, b_ct8], writes=[b_csil])

        modrow = sf1[0:1, 0:D]
        wslot = 0

        def stream_piece(dram_ap, ncols=D):
            nonlocal wslot
            sl = wslot % 4
            wslot += 1
            OP("qs", lambda e: e.dma_start(out=wring[sl][:, 0:ncols], in_=dram_ap),
               writes=[b_wring[sl]], dma_buf=b_wring[sl])
            return sl

        def mod_block(m):
            for j in range(DCH):
                sl = stream_piece(dwmod[:, (m * 8 + j) * D:(m * 8 + j + 1) * D])
                for hf in range(2):
                    OP("pe", lambda e, j=j, hf=hf, sl=sl: e.matmul(
                        out=P45[0:1, 512 * hf:512 * (hf + 1)],
                        lhsT=c_sil[:, j:j + 1], rhs=wring[sl][:, 512 * hf:512 * (hf + 1)],
                        start=(j == 0), stop=(j == DCH - 1)),
                       reads=[b_csil, b_wring[sl], b_pmod], writes=[b_pmod])
            OP("act", lambda e: e.copy(out=modrow, in_=P45[0:1, 0:D]),
               reads=[b_pmod], writes=[b_modrow])

        def bcast_row(dst_ap, dstbuf, bias=None, dtype_note=None):
            for hf in range(2):
                OP("pe", lambda e, hf=hf: e.matmul(
                    out=P23[:, 512 * hf:512 * (hf + 1)],
                    lhsT=ones128_sb[:],
                    rhs=modrow[:, 512 * hf:512 * (hf + 1)],
                    start=True, stop=True),
                   reads=[b_ones128, b_modrow, b_pk], writes=[b_pk])
            if bias is None:
                OP("act", lambda e: e.copy(out=dst_ap, in_=P23[:, 0:D]),
                   reads=[b_pk], writes=[dstbuf])
            else:
                OP("act", lambda e: e.activation(out=dst_ap, in_=P23[:, 0:D], func=AF.Copy,
                                                 bias=bias),
                   reads=[b_pk], writes=[dstbuf])

        def gate_transform():
            OP("act", lambda e: e.activation(out=rowtmp[:], in_=modrow, func=AF.Sign),
               reads=[b_modrow], writes=[b_rowtmp])
            OP("act", lambda e: e.activation(out=modrow, in_=modrow, func=AF.Abs),
               reads=[b_modrow], writes=[b_modrow])
            OP("dve", lambda e: e.tensor_scalar(out=modrow, in0=modrow, scalar1=0.25,
                                                scalar2=None, op0=ALU.max),
               reads=[b_modrow], writes=[b_modrow])
            OP("dve", lambda e: e.tensor_tensor(out=modrow, in0=modrow, in1=rowtmp[:], op=ALU.mult),
               reads=[b_modrow, b_rowtmp], writes=[b_modrow])

        mod_block(0)
        kb.dump(c_sil[:], [b_csil], ncols=8)
        kb.dump(modrow, [b_modrow], nparts=1)
        bcast_row(sa_f[:], b_saf)
        mod_block(1)
        bcast_row(ca1[:], b_ca1, bias=1.0)
        mod_block(2)
        gate_transform()
        bcast_row(gb_exp[:], b_gbexp)
        mod_block(3)
        gate_transform()
        bcast_row(ga_exp[:], b_gaexp)
        mod_block(4)
        bcast_row(sb_f[:], b_sbf)
        mod_block(5)
        bcast_row(cb1[:], b_cb1, bias=1.0)
        kb.dump(sa_f[:], [b_saf])
        kb.dump(ca1[:], [b_ca1])
        kb.dump(gb_exp[:], [b_gbexp])
        kb.dump(ga_exp[:], [b_gaexp])
        kb.dump(sb_f[:], [b_sbf])
        kb.dump(cb1[:], [b_cb1])

        def load_weight(dram, dst, dstbuf, ncols, fold=None, foldbuf=None):
            for j in range(ncols // D):
                sl = stream_piece(dram[:, j * D:(j + 1) * D])
                if fold is None:
                    OP("act", lambda e, j=j, sl=sl: e.copy(out=dst[:, j * D:(j + 1) * D],
                                                           in_=wring[sl][:]),
                       reads=[b_wring[sl]], writes=[dstbuf])
                else:
                    OP("dve", lambda e, j=j, sl=sl: e.tensor_tensor(
                        out=dst[:, j * D:(j + 1) * D], in0=wring[sl][:], in1=fold, op=ALU.mult),
                       reads=[b_wring[sl], foldbuf], writes=[dstbuf])

        load_weight(dwq, wq_bf, b_wq, 8 * D)
        load_weight(dwk, wk_bf, b_wk, 8 * D)
        load_weight(dwv, wv_bf, b_wv, 8 * D)
        sl = stream_piece(dwa, ncols=8 * H)
        OP("act", lambda e, sl=sl: e.copy(out=wa_bf[:], in_=wring[sl][:, 0:8 * H]),
           reads=[b_wring[sl]], writes=[b_wa])
        sl = stream_piece(dgw, ncols=8 * E)
        OP("act", lambda e, sl=sl: e.copy(out=gw_bf[:], in_=wring[sl][:, 0:8 * E]),
           reads=[b_wring[sl]], writes=[b_gw])
        load_weight(dwo, wo_bf, b_wo, 8 * D, fold=ga_exp[:], foldbuf=b_gaexp)
        OP("dve", lambda e: e.tensor_tensor(out=expb_bf[:], in0=expb_f[:], in1=gb_exp[0:E, :],
                                            op=ALU.mult),
           reads=[b_expbf, b_gbexp], writes=[b_expbbf])

        tr_cnt = [0]

        def transposes(src_ap, srcbufs, dst_ap, dstbuf):
            blk = srcbufs if isinstance(srcbufs, list) else [srcbufs] * DCH
            for u in range(DCH):
                OP("pe", lambda e, u=u: e.transpose(out=P7[:, 128 * u:128 * (u + 1)],
                                                    in_=src_ap[:, 128 * u:128 * (u + 1)],
                                                    identity=ident_sb[:]),
                   reads=[blk[u], b_ident], writes=[b_p7])
            OP("act", lambda e: e.copy(out=dst_ap, in_=P7[:]), reads=[b_p7], writes=[dstbuf])

        def proj(h_fm_ap, h_fm_buf, w_sb, w_buf, pdst, pbuf, nout):
            nhalf = max(1, nout // 512)
            step = min(nout, 512)
            blk = h_fm_buf if isinstance(h_fm_buf, list) else [h_fm_buf] * DCH
            for j in range(DCH):
                for hf in range(nhalf):
                    OP("pe", lambda e, j=j, hf=hf: e.matmul(
                        out=pdst[:, hf * step:(hf + 1) * step],
                        lhsT=h_fm_ap[:, 128 * j:128 * (j + 1)],
                        rhs=w_sb[:, j * nout + hf * step: j * nout + (hf + 1) * step],
                        start=(j == 0), stop=(j == DCH - 1)),
                       reads=[blk[j], w_buf, pbuf], writes=[pbuf])

        def ln_stats(x_ap, xbuf):
            OP("dve", lambda e: e.bn_stats(out=stat[:, 0:6], in_=x_ap[:, 0:512]),
               reads=[xbuf], writes=[b_stat], exotic=True)
            OP("dve", lambda e: e.bn_stats(out=stat[:, 6:12], in_=x_ap[:, 512:1024]),
               reads=[xbuf], writes=[b_stat], exotic=True)
            OP("dve", lambda e: e.bn_aggr(out=stat[:, 12:14], in_=stat[:, 0:12]),
               reads=[b_stat], writes=[b_stat], exotic=True)
            OP("act", lambda e: e.activation(out=stat2[:, 0:1], in_=stat[:, 13:14], func=AF.Ln,
                                             bias=C_EPS),
               reads=[b_stat, b_cvec], writes=[b_stat2])
            OP("act", lambda e: e.activation(out=stat2[:, 1:2], in_=stat2[:, 0:1], func=AF.Exp,
                                             scale=C_NHALF),
               reads=[b_stat2, b_cvec], writes=[b_stat2])

        def ln_apply(x_ap, xbuf, scale_t, scalebuf, shift_t, shiftbuf, out_bf, outbufs):
            OP("dve", lambda e: e.tensor_scalar(out=sf1[:], in0=x_ap, scalar1=stat[:, 12:13],
                                                scalar2=stat2[:, 1:2], op0=ALU.subtract,
                                                op1=ALU.mult),
               reads=[xbuf, b_stat, b_stat2], writes=[b_sf1])
            OP("pool", lambda e: e.tensor_tensor(out=sf1[:], in0=sf1[:], in1=scale_t,
                                                 op=ALU.mult),
               reads=[b_sf1, scalebuf], writes=[b_sf1])
            OP("pool", lambda e: e.tensor_tensor(out=out_bf, in0=sf1[:], in1=shift_t,
                                                 op=ALU.add),
               reads=[b_sf1, shiftbuf], writes=list(outbufs))

        def elu_scale(psrc, psbuf, elbf_ap, elbf_buf, dst, dstbufs, on_dve=False):
            if on_dve:
                OP("dve", lambda e: e.tensor_scalar(out=s1[:], in0=psrc, scalar1=0.0,
                                                    scalar2=None, op0=ALU.min),
                   reads=[psbuf], writes=[b_s1])
                OP("act", lambda e: e.activation(out=s2[:], in_=s1[:], func=AF.Exp),
                   reads=[b_s1], writes=[b_s2])
                OP("dve", lambda e: e.tensor_scalar(out=s1[:], in0=psrc, scalar1=0.0,
                                                    scalar2=None, op0=ALU.max),
                   reads=[psbuf], writes=[b_s1])
            else:
                OP("act", lambda e: e.activation(out=s1[:], in_=psrc, func=AF.Relu,
                                                 scale=C_NEG1),
                   reads=[psbuf, b_cvec], writes=[b_s1])
                OP("act", lambda e: e.activation(out=s2[:], in_=s1[:], func=AF.Exp,
                                                 scale=C_NEG1),
                   reads=[b_s1, b_cvec], writes=[b_s2])
                OP("act", lambda e: e.activation(out=s1[:], in_=psrc, func=AF.Relu),
                   reads=[psbuf], writes=[b_s1])
            OP("dve", lambda e: e.tensor_tensor(out=dst, in0=s2[:], in1=s1[:], op=ALU.add),
               reads=[b_s1, b_s2], writes=list(dstbufs))
            rep = elbf_ap.unsqueeze(2).to_broadcast([128, H, 64])
            d3 = dst.rearrange("p (h o) -> p h o", h=H)
            OP("dve", lambda e, rep=rep, d3=d3: e.tensor_tensor(out=d3, in0=d3, in1=rep,
                                                                op=ALU.mult),
               reads=list(dstbufs) + [elbf_buf], writes=list(dstbufs))

        ln1_done = set()
        h1t_done = set()
        pe_backlog = []

        def emit_backlog():
            for f in pe_backlog:
                f()
            pe_backlog.clear()

        def ln1(g):
            xa = xr[g % 4][:]
            xb = b_xr[g % 4]
            ln_stats(xa, xb)
            ln_apply(xa, xb, ca1[:], b_ca1, sa_f[:], b_saf, h1_bf[:], [b_h1bf])
            ln1_done.add(g)

        def h1t(g):
            for u in range(DCH):
                OP("pe", lambda e, u=u: e.transpose(out=P7[:, 128 * u:128 * (u + 1)],
                                                    in_=h1_bf[:, 128 * u:128 * (u + 1)],
                                                    identity=ident_sb[:]),
                   reads=[b_h1bf, b_ident], writes=[b_p7])
            OP("act", lambda e: e.copy(out=h1_fm[:], in_=P7[:]),
               reads=[b_p7], writes=list(b_h1fm))
            h1t_done.add(g)

        front_done = set()

        def attn_front(g):
            for j in range(DCH):
                OP("pe", lambda e, j=j: e.matmul(
                    out=p6a, lhsT=h1_fm[:, 128 * j:128 * (j + 1)],
                    rhs=wa_bf[:, j * H:(j + 1) * H],
                    start=(j == 0), stop=(j == DCH - 1)),
                   reads=[b_h1fm[j], b_wa, b_p6], writes=[b_p6])
            OP("dve", lambda e: e.tensor_tensor(out=u16[:], in0=p6a, in1=babc2[:], op=ALU.add),
               reads=[b_p6, b_babc2], writes=[b_u16])
            OP("act", lambda e: e.activation(out=t16a[:], in_=u16[:], func=AF.Abs),
               reads=[b_u16], writes=[b_t16a])
            OP("act", lambda e: e.activation(out=t16b[:], in_=t16a[:], func=AF.Exp, scale=C_NEG1),
               reads=[b_t16a, b_cvec], writes=[b_t16b])
            OP("act", lambda e: e.activation(out=t16a[:], in_=t16b[:], func=AF.Ln, bias=1.0),
               reads=[b_t16b], writes=[b_t16a])
            OP("dve", lambda e: e.tensor_scalar(out=t16b[:], in0=u16[:], scalar1=0.0,
                                                scalar2=None, op0=ALU.min),
               reads=[b_u16], writes=[b_t16b])
            OP("dve", lambda e: e.tensor_tensor(out=lna[:], in0=t16b[:], in1=t16a[:],
                                                op=ALU.subtract),
               reads=[b_t16a, b_t16b], writes=[b_lna])
            front_done.add(g)

        def attn_subchunk(g):
            xa = xr[g % 4][:]
            xb = b_xr[g % 4]
            if g not in ln1_done:
                ln1(g)
            if g not in h1t_done:
                h1t(g)
            if g not in front_done:
                attn_front(g)
            proj(h1_fm[:], b_h1fm, wk_bf, b_wk, P23, b_pk, D)
            OP("pe", lambda e: e.matmul(out=p6L, lhsT=tri_sb[:], rhs=lna[:], start=True,
                                        stop=True),
               reads=[b_tri, b_lna, b_p6], writes=[b_p6])
            OP("act", lambda e: e.activation(out=el[:], in_=p6L, func=AF.Exp),
               reads=[b_p6], writes=[b_el])
            OP("act", lambda e: e.activation(out=enl[:], in_=p6L, func=AF.Exp, scale=C_NEG1),
               reads=[b_p6, b_cvec], writes=[b_enl])
            OP("act", lambda e: e.copy(out=el_bf[:], in_=el[:]), reads=[b_el], writes=[b_elbf])
            OP("act", lambda e: e.copy(out=enl_bf[:], in_=enl[:]), reads=[b_enl], writes=[b_enlbf])
            OP("pe", lambda e: e.matmul(out=p6er, lhsT=oneh_sb[:], rhs=el[:], start=True,
                                        stop=True),
               reads=[b_oneh, b_el, b_p6], writes=[b_p6])
            OP("act", lambda e: e.copy(out=elcrow[:], in_=p6er), reads=[b_p6], writes=[b_elcrow])
            OP("pe", lambda e: e.matmul(out=p6eb, lhsT=ones128_sb[:], rhs=elcrow[:], start=True,
                                        stop=True),
               reads=[b_ones128, b_elcrow, b_p6], writes=[b_p6])
            OP("act", lambda e: e.copy(out=elc_bf[:], in_=p6eb), reads=[b_p6], writes=[b_elcbf])
            proj(h1_fm[:], b_h1fm, wq_bf, b_wq, P01, b_pq, D)
            elu_scale(P23[:], b_pk, enl_bf[:], b_enlbf, kt_tm[:], b_kttm, on_dve=True)
            emit_backlog()
            proj(h1_fm[:], b_h1fm, wv_bf, b_wv, P45, b_pv, D)
            elu_scale(P01[:], b_pq, el_bf[:], b_elbf, qt_tm[:], b_qttm)
            transposes(kt_tm[:], b_kttm, kt_fm[:], b_ktfm)
            transposes(qt_tm[:], b_qttm, qt_fm[:], b_qtfm)
            OP("act", lambda e: e.copy(out=v_tm[:], in_=P45[:]), reads=[b_pv], writes=[b_vtm])

            def em_scores(h):
                u, half = h // 2, h % 2
                base = 64 * half
                qsl = qt_fm[base:base + 64, 128 * u:128 * (u + 1)]
                ksl = kt_fm[base:base + 64, 128 * u:128 * (u + 1)]
                sc = scA if h % 2 == 0 else scB
                bsc = b_scA if h % 2 == 0 else b_scB
                OP("pe", lambda e, sc=sc, ksl=ksl, qsl=qsl: e.matmul(
                    out=sc, lhsT=ksl, rhs=qsl, start=True, stop=True),
                   reads=[b_ktfm, b_qtfm, bsc], writes=[bsc])

            def em_mask(h):
                sc = scA if h % 2 == 0 else scB
                bsc = b_scA if h % 2 == 0 else b_scB
                OP("dve", lambda e, sc=sc, h=h: e.tensor_tensor(
                    out=smt[h % 2][:], in0=sc, in1=tri_sb[:], op=ALU.mult),
                   reads=[bsc, b_tri], writes=[b_smt[h % 2]])

            def em_rest(h):
                u, half = h // 2, h % 2
                base = 64 * half
                qsl = qt_fm[base:base + 64, 128 * u:128 * (u + 1)]
                OP("pe", lambda e, h=h: e.matmul(
                    out=P45[:, 64 * h:64 * (h + 1)], lhsT=smt[h % 2][:],
                    rhs=v_tm[:, 64 * h:64 * (h + 1)], start=True, stop=False),
                   reads=[b_smt[h % 2], b_vtm, b_po], writes=[b_po])
                OP("pe", lambda e, h=h, qsl=qsl, base=base: e.matmul(
                    out=P45[:, 64 * h:64 * (h + 1)], lhsT=qsl,
                    rhs=Sst[base:base + 64, 64 * h:64 * (h + 1)], start=False, stop=True),
                   reads=[b_qtfm, b_S, b_po], writes=[b_po])
                OP("pe", lambda e, h=h: e.matmul(
                    out=p6den[:, h:h + 1], lhsT=smt[h % 2][:], rhs=onesb_sb[:],
                    start=True, stop=False),
                   reads=[b_smt[h % 2], b_onesb, b_p6], writes=[b_p6])
                OP("pe", lambda e, h=h, qsl=qsl, base=base: e.matmul(
                    out=p6den[:, h:h + 1], lhsT=qsl, rhs=zst[base:base + 64, h:h + 1],
                    start=False, stop=True),
                   reads=[b_qtfm, b_z, b_p6], writes=[b_p6])
                OP("pe", lambda e, h=h: e.matmul(
                    out=P23[0:64, 64 * h:64 * (h + 1)],
                    lhsT=kt_tm[:, 64 * h:64 * (h + 1)],
                    rhs=v_tm[:, 64 * h:64 * (h + 1)], start=True, stop=True),
                   reads=[b_kttm[h // 2], b_vtm, b_kvS], writes=[b_kvS])
                OP("pe", lambda e, h=h: e.matmul(
                    out=p6kvz[0:64, h:h + 1], lhsT=kt_tm[:, 64 * h:64 * (h + 1)],
                    rhs=onesb_sb[:], start=True, stop=True),
                   reads=[b_kttm[h // 2], b_onesb, b_p6], writes=[b_p6])

            em_scores(0)
            for h in range(H):
                em_mask(h)
                if h + 1 < H:
                    em_scores(h + 1)
                em_rest(h)
                if h == 2 and g + 1 < nsub:
                    ln1(g + 1)
            OP("dve", lambda e: e.tensor_scalar(out=den_sb[:], in0=p6den, scalar1=1e-6,
                                                scalar2=None, op0=ALU.add),
               reads=[b_p6], writes=[b_den])
            OP("dve", lambda e: e.reciprocal(out=rden[:], in_=den_sb[:]),
               reads=[b_den], writes=[b_rden], exotic=True)
            OP("act", lambda e: e.copy(out=kvbf[0:64, :], in_=P23[0:64, :]),
               reads=[b_kvS], writes=[b_kvbf])
            OP("act", lambda e: e.copy(out=zvbf[0:64, :], in_=p6kvz[0:64, :]),
               reads=[b_p6, b_den], writes=[b_zvbf])
            warm(P01[0:1, 0:1], [b_den], b_pq)
            if g + 1 < nsub:
                h1t(g + 1)
                attn_front(g + 1)
            warm(P01[0:1, 0:1], [b_rden], b_pq)
            warm(P01[0:1, 0:1], [b_kvbf], b_pq)
            OP("pool", lambda e: e.tensor_tensor(out=Sst[0:64, :], in0=Sst[0:64, :],
                                                 in1=kvbf[0:64, :], op=ALU.add),
               reads=[b_S, b_kvbf], writes=[b_S])
            elcrep = elc_bf[0:64, :].unsqueeze(2).to_broadcast([64, H, 64])
            OP("pool", lambda e, elcrep=elcrep: e.tensor_tensor(
                out=Sst[0:64, :].rearrange("p (h o) -> p h o", h=H),
                in0=Sst[0:64, :].rearrange("p (h o) -> p h o", h=H),
                in1=elcrep, op=ALU.mult),
               reads=[b_S, b_elcbf], writes=[b_S])
            OP("pool", lambda e: e.tensor_tensor(out=zst[0:64, :], in0=zst[0:64, :],
                                                 in1=zvbf[0:64, :], op=ALU.add),
               reads=[b_z, b_zvbf], writes=[b_z])
            OP("pool", lambda e: e.tensor_tensor(
                out=zst[0:64, :], in0=zst[0:64, :], in1=elc_bf[0:64, :], op=ALU.mult),
               reads=[b_z, b_elcbf], writes=[b_z])
            OP("pool", lambda e: e.tensor_copy(out=Sst[64:128, :], in_=Sst[0:64, :]),
               reads=[b_S], writes=[b_S])
            OP("pool", lambda e: e.tensor_copy(out=zst[64:128, :], in_=zst[0:64, :]),
               reads=[b_z], writes=[b_z])
            for u in (0, 4, 1, 5, 2, 6, 3, 7):
                for h in (2 * u, 2 * u + 1):
                    if h < 8:
                        OP("act", lambda e, h=h: e.activation(
                            out=ao_tm[:, 64 * h:64 * (h + 1)],
                            in_=P45[:, 64 * h:64 * (h + 1)],
                            func=AF.Copy, scale=rden[:, h:h + 1]),
                           reads=[b_po, b_rden], writes=[b_aotm[u]])
                    else:
                        OP("dve", lambda e, h=h: e.tensor_scalar(
                            out=ao_tm[:, 64 * h:64 * (h + 1)],
                            in0=P45[:, 64 * h:64 * (h + 1)],
                            scalar1=rden[:, h:h + 1], scalar2=None, op0=ALU.mult),
                           reads=[b_po, b_rden], writes=[b_aotm[u]])
                OP("pe", lambda e, u=u: e.transpose(
                    out=P7[:, 128 * u:128 * (u + 1)], in_=ao_tm[:, 128 * u:128 * (u + 1)],
                    identity=ident_sb[:]),
                   reads=[b_aotm[u], b_ident, b_p7], writes=[b_p7])
                if u in (1, 6):
                    warm(P01[0:1, 0:1], [b_aotm[u]], b_py)
            OP("act", lambda e: e.copy(out=ao_fm[:], in_=P7[:]), reads=[b_p7], writes=[b_aofm])
            proj(ao_fm[:], b_aofm, wo_bf, b_wo, P01, b_py, D)
            OP("dve", lambda e, xa=xa: e.tensor_tensor(out=xa, in0=P01[:], in1=xa, op=ALU.add),
               reads=[b_py, xb], writes=[xb])
            s = g % SPC
            ln_stats(xa, xb)
            ln_apply(xa, xb, cb1[:], b_cb1, sb_f[:], b_sbf, h2_tm[:], [b_h2tm])
            def h2t(s=s):
                for u in range(DCH):
                    OP("pe", lambda e, u=u, s=s: e.transpose(
                        out=P7[:, 128 * u:128 * (u + 1)],
                        in_=h2_tm[:, 128 * u:128 * (u + 1)], identity=ident_sb[:]),
                       reads=[b_h2tm, b_ident], writes=[b_p7])
                OP("act", lambda e, s=s: e.copy(out=h2_fm[:, s * D:(s + 1) * D], in_=P7[:]),
                   reads=[b_p7], writes=list(b_h2fm[s]))
            pe_backlog.append(h2t)

        piece_ctr = [0]
        fcvt = [0]

        def gate_comb(s):
            ga_, gb_, gc_, gd_, ge_, gf_ = (g8[:, i * E:(i + 1) * E] for i in range(6))
            OP("dve", lambda e: e.tensor_tensor(out=ga_, in0=p6g, in1=gateb_t[:], op=ALU.add),
               reads=[b_p6, b_gatebt], writes=[b_g8])
            OP("dve", lambda e: e.tensor_scalar(out=ga_, in0=ga_, scalar1=1e-4,
                                                scalar2=None, op0=ALU.max),
               reads=[b_g8], writes=[b_g8])
            OP("dve", lambda e: e.tensor_reduce(out=m1c[:, 0:1], in_=ga_, axis=AX.X,
                                                op=ALU.max),
               reads=[b_g8], writes=[b_m1c], exotic=True)
            OP("dve", lambda e: e.tensor_scalar(out=m1c[:, 1:2], in0=m1c[:, 0:1],
                                                scalar1=-1.0, scalar2=None, op0=ALU.mult),
               reads=[b_m1c], writes=[b_m1c])
            OP("act", lambda e: e.activation(out=gb_, in_=ga_, func=AF.Exp,
                                             bias=m1c[:, 1:2]),
               reads=[b_g8, b_m1c], writes=[b_g8])
            OP("dve", lambda e: e.tensor_reduce(out=m1c[:, 0:1], in_=gb_, axis=AX.X,
                                                op=ALU.add),
               reads=[b_g8], writes=[b_m1c], exotic=True)
            OP("dve", lambda e: e.reciprocal(out=m1c[:, 1:2], in_=m1c[:, 0:1]),
               reads=[b_m1c], writes=[b_m1c], exotic=True)
            OP("dve", lambda e: e.tensor_scalar(out=ga_, in0=gb_, scalar1=m1c[:, 1:2],
                                                scalar2=None, op0=ALU.mult),
               reads=[b_g8, b_m1c], writes=[b_g8])
            OP("dve", lambda e: e.tensor_reduce(out=mrep[:, 0:1], in_=ga_, axis=AX.X,
                                                op=ALU.max),
               reads=[b_g8], writes=[b_mrep], exotic=True)
            OP("dve", lambda e: e.match_replace(out=gb_, in_to_replace=mrep[:],
                                                in_values=ga_, imm_value=-1e30),
               reads=[b_g8, b_mrep], writes=[b_g8], exotic=True)
            OP("dve", lambda e: e.tensor_tensor(out=gc_, in0=ga_, in1=gb_, op=ALU.not_equal),
               reads=[b_g8], writes=[b_g8])
            OP("dve", lambda e: e.tensor_reduce(out=mrep[:, 0:1], in_=gb_, axis=AX.X,
                                                op=ALU.max),
               reads=[b_g8], writes=[b_mrep], exotic=True)
            OP("dve", lambda e: e.match_replace(out=gd_, in_to_replace=mrep[:],
                                                in_values=gb_, imm_value=-1e30),
               reads=[b_g8, b_mrep], writes=[b_g8], exotic=True)
            OP("dve", lambda e: e.tensor_tensor(out=ge_, in0=gb_, in1=gd_, op=ALU.not_equal),
               reads=[b_g8], writes=[b_g8])
            OP("dve", lambda e: e.tensor_tensor(out=gc_, in0=gc_, in1=ge_, op=ALU.add),
               reads=[b_g8], writes=[b_g8])
            OP("dve", lambda e: e.tensor_scalar(out=gf_, in0=ga_, scalar1=0.1,
                                                scalar2=None, op0=ALU.max),
               reads=[b_g8], writes=[b_g8])
            OP("dve", lambda e: e.tensor_tensor(out=gf_, in0=gf_, in1=gc_, op=ALU.mult),
               reads=[b_g8], writes=[b_g8])
            OP("dve", lambda e: e.tensor_reduce(out=m1c[:, 0:1], in_=gf_, axis=AX.X,
                                                op=ALU.add),
               reads=[b_g8], writes=[b_m1c], exotic=True)
            OP("dve", lambda e: e.reciprocal(out=m1c[:, 1:2], in_=m1c[:, 0:1]),
               reads=[b_m1c], writes=[b_m1c], exotic=True)
            OP("dve", lambda e, s=s: e.tensor_scalar(
                out=combc[:, s * E:(s + 1) * E], in0=gf_, scalar1=m1c[:, 1:2],
                scalar2=0.5, op0=ALU.mult, op1=ALU.mult),
               reads=[b_g8, b_m1c], writes=[b_comb])
            OP("dve", lambda e, s=s: e.tensor_copy(out=comb_bfx[:, s * E:(s + 1) * E],
                                                   in_=combc[:, s * E:(s + 1) * E]),
               reads=[b_comb], writes=[b_combbfx])

        def combT_biasmix(s, xa, xb):
            OP("pe", lambda e, s=s: e.transpose(out=P7[0:8, 0:128],
                                                in_=comb_bfx[:, s * E:(s + 1) * E],
                                                identity=ident_sb[:]),
               reads=[b_combbfx, b_ident, b_p7], writes=[b_p7])
            OP("act", lambda e, s=s: e.copy(out=combT_bf[:, s * 128:(s + 1) * 128],
                                            in_=P7[0:8, 0:128]),
               reads=[b_p7], writes=[b_combT])
            for hf in range(2):
                OP("pe", lambda e, hf=hf, s=s: e.matmul(
                    out=P45[:, 512 * hf:512 * (hf + 1)],
                    lhsT=combT_bf[:, s * 128:(s + 1) * 128],
                    rhs=expb_bf[:, 512 * hf:512 * (hf + 1)], start=True, stop=True),
                   reads=[b_combT, b_expbbf, b_pbm], writes=[b_pbm])
            OP("dve", lambda e, xa=xa: e.tensor_tensor(out=xa, in0=P45[:], in1=xa,
                                                       op=ALU.add),
               reads=[b_pbm, xb], writes=[xb])

        def ffn_chunk(c):
            g0 = c * SPC

            def gates(s):
                for j in range(DCH):
                    OP("pe", lambda e, j=j, s=s: e.matmul(
                        out=p6g, lhsT=h2_fm[:, s * D + 128 * j:s * D + 128 * (j + 1)],
                        rhs=gw_bf[:, j * E:(j + 1) * E],
                        start=(j == 0), stop=(j == DCH - 1)),
                       reads=[b_h2fm[s][j], b_gw, b_p6], writes=[b_p6])

            ptiles = (P01, P23, P45)
            pbufs = (b_fe0, b_fe1, b_fe2)

            def piece_dma(e_, q2):
                pc = piece_ctr[0]
                piece_ctr[0] += 1
                sl = pc % 4
                if c == 0:
                    for w_ in range(4):
                        j = 4 * q2 + w_
                        fsl = fcvt[0] % 4
                        fcvt[0] += 1
                        OP("qs", lambda e, e_=e_, j=j, fsl=fsl: e.dma_start(
                            out=wring[fsl][:], in_=dexpw[e_, :, j * D:(j + 1) * D]),
                           writes=[b_wring[fsl]], dma_buf=b_wring[fsl])
                        OP("dve", lambda e, sl=sl, w_=w_, fsl=fsl: e.tensor_tensor(
                            out=ering[sl][:, w_ * D:(w_ + 1) * D], in0=wring[fsl][:],
                            in1=gb_exp[:], op=ALU.mult),
                           reads=[b_wring[fsl], b_gbexp], writes=[b_ering[sl]])
                    OP("qg", lambda e, e_=e_, q2=q2, sl=sl: e.dma_start(
                        out=decache[e_, :, q2 * 4096:(q2 + 1) * 4096], in_=ering[sl][:]),
                       reads=[b_ering[sl]], writes=[b_ecache[e_][q2]],
                       dma_buf=b_ering[sl])
                else:
                    OP("qs", lambda e, e_=e_, q2=q2, sl=sl: e.dma_start(
                        out=ering[sl][:], in_=decache[e_, :, q2 * 4096:(q2 + 1) * 4096]),
                       reads=[b_ecache[e_][q2]], writes=[b_ering[sl]],
                       dma_buf=b_ering[sl])
                return sl

            def piece_mms(e_, q2, sl, subs, hfs=(0, 1)):
                for w_ in range(4):
                    j = 4 * q2 + w_
                    for s in subs:
                        pt = ptiles[(2 * e_ + s) % 3]
                        pb = pbufs[(2 * e_ + s) % 3]
                        for hf in hfs:
                            OP("pe", lambda e, j=j, s=s, w_=w_, hf=hf, sl=sl, pt=pt: e.matmul(
                                out=pt[:, 512 * hf:512 * (hf + 1)],
                                lhsT=h2_fm[:, s * D + 128 * j:s * D + 128 * (j + 1)],
                                rhs=ering[sl][:, w_ * D + 512 * hf:w_ * D + 512 * (hf + 1)],
                                start=(j == 0), stop=(j == DCH - 1)),
                               reads=[b_h2fm[s][j], b_ering[sl], pb], writes=[pb])

            def expert_mms(e_, sls, subs):
                for q2 in range(2):
                    piece_mms(e_, q2, sls[q2], subs)
                for s in subs:
                    g = g0 + s
                    xa = xr[g % 4][:]
                    xb = b_xr[g % 4]
                    pt = ptiles[(2 * e_ + s) % 3]
                    pb = pbufs[(2 * e_ + s) % 3]
                    OP("dve", lambda e, xa=xa, pt=pt, s=s, e_=e_: e.scalar_tensor_tensor(
                        out=xa, in0=pt[:], scalar=combc[:, s * E + e_:s * E + e_ + 1],
                        in1=xa, op0=ALU.mult, op1=ALU.add),
                       reads=[pb, b_comb, xb], writes=[xb])

            def stts(e_):
                pass

            gates(0)
            gate_comb(0)
            sls = [piece_dma(0, 0)]
            piece_mms(0, 0, sls[0], [0])
            warm(P23[0:1, 0:1], [b_stat2], b_fe1)
            warm(P23[0:1, 0:1], [b_h2tm], b_fe1)
            emit_backlog()
            gates(1)
            gate_comb(1)
            sls += [piece_dma(0, 1)]
            piece_mms(0, 0, sls[0], [1])
            piece_mms(0, 1, sls[1], [0, 1])
            for s in range(SPC):
                g = g0 + s
                combT_biasmix(s, xr[g % 4][:], b_xr[g % 4])
            for s in range(SPC):
                g = g0 + s
                xa = xr[g % 4][:]
                xb = b_xr[g % 4]
                pt = ptiles[s % 3]
                pb = pbufs[s % 3]
                OP("dve", lambda e, xa=xa, pt=pt, s=s: e.scalar_tensor_tensor(
                    out=xa, in0=pt[:], scalar=combc[:, s * E:s * E + 1],
                    in1=xa, op0=ALU.mult, op1=ALU.add),
                   reads=[pb, b_comb, xb], writes=[xb])
            for e_ in range(1, E):
                sls = [piece_dma(e_, q2) for q2 in range(2)]
                expert_mms(e_, sls, [0, 1])

        for c in range(nch):
            for s in range(SPC):
                attn_subchunk(c * SPC + s)
            ffn_chunk(c)
            for s in range(SPC):
                g = c * SPC + s
                OP("qg", lambda e, g=g: e.dma_start(
                    out=dy[g * 128:(g + 1) * 128, :], in_=xr[g % 4][:]),
                   reads=[b_xr[g % 4]], writes=[b_y], dma_buf=b_xr[g % 4])
            for s in range(SPC):
                g = (c + 2) * SPC + s
                if g < nsub:
                    OP("qg", lambda e, g=g: e.dma_start(
                        out=xr[g % 4][:], in_=dx[g * 128:(g + 1) * 128, :]),
                       writes=[b_xr[g % 4]], dma_buf=b_xr[g % 4])

        kb.final_wait("qs")

    es.close()
    return nc



def _wlayout(w):
    fdim = w.shape[1]
    return np.ascontiguousarray(
        w.reshape(8, 128, fdim).transpose(1, 0, 2).reshape(128, 8 * fdim))


def kernel(L0_attn_x, L0_attn_Wq, L0_attn_Wk, L0_attn_Wv, L0_attn_Wo, L0_attn_Wa,
           L0_attn_ba, L0_ffn_gate_W, L0_ffn_gate_b, L0_ffn_expert_W, L0_ffn_expert_b,
           L0_c, L0_wmod):
    n_tokens = L0_attn_x.shape[1]
    nc = build_nc(n_tokens)

    wq = _wlayout(np.asarray(L0_attn_Wq, np.float32))
    wk = _wlayout(np.asarray(L0_attn_Wk, np.float32))
    wv = _wlayout(np.asarray(L0_attn_Wv, np.float32))
    wo = _wlayout(np.asarray(L0_attn_Wo, np.float32))
    wa = _wlayout(np.asarray(L0_attn_Wa, np.float32))
    gw = _wlayout(np.asarray(L0_ffn_gate_W, np.float32))
    expw = np.stack([_wlayout(np.asarray(L0_ffn_expert_W[e], np.float32)) for e in range(E)])
    expb = np.ascontiguousarray(np.asarray(L0_ffn_expert_b, np.float32))
    wm = np.asarray(L0_wmod, np.float32).reshape(1024, 6, 1024)[:, [0, 1, 5, 2, 3, 4], :]
    wm = np.ascontiguousarray(
        wm.reshape(8, 128, 6, 1024).transpose(1, 2, 0, 3).reshape(128, 48 * 1024))
    ba_bc = np.ascontiguousarray(
        np.broadcast_to(np.asarray(L0_attn_ba, np.float32), (128, H)))
    gateb_bc = np.ascontiguousarray(
        np.broadcast_to(np.asarray(L0_ffn_gate_b, np.float32), (128, E)))
    tri = np.triu(np.ones((128, 128), dtype=np.float32))
    identbf = np.eye(128, dtype=np.float32).astype(ml_dtypes.bfloat16)
    onesbf = np.ones((128, 1), dtype=np.float32).astype(ml_dtypes.bfloat16)
    oneh127 = np.zeros((128, 1), dtype=np.float32)
    oneh127[127, 0] = 1.0
    ones1x64 = np.ones((1, 64), dtype=np.float32)
    ones1x128 = np.ones((1, 128), dtype=np.float32)
    cvec_h = np.zeros((128, 4), dtype=np.float32)
    cvec_h[:, 0] = 1e-5
    cvec_h[:, 1] = -0.5
    cvec_h[:, 2] = -1.0

    shared = dict(wq=wq, wk=wk, wv=wv, wo=wo, wa=wa, gw=gw, expw=expw, expb=expb,
                  wmod=wm, ba_bc=ba_bc, gateb_bc=gateb_bc, tri=tri, identbf=identbf,
                  onesbf=onesbf, oneh127=oneh127, ones1x64=ones1x64,
                  ones1x128=ones1x128, cvec=cvec_h)
    in_maps = []
    xf = np.asarray(L0_attn_x, np.float32)
    for b in range(8):
        m = dict(shared)
        m["x"] = np.ascontiguousarray(xf[b, :n_tokens])
        m["cmod"] = np.ascontiguousarray(
            np.asarray(L0_c[b], np.float32).reshape(8, 128).T)
        in_maps.append(m)

    res = run_bass_kernel_spmd(nc, in_maps, core_ids=list(range(8)))
    out = np.stack([r["y"] for r in res.results])
    return out.astype(np.float32)


# version 31
# speedup vs baseline: 1.0049x; 1.0049x over previous
import os
import numpy as np
import ml_dtypes
from contextlib import ExitStack
import concourse.bass as bass
import concourse.mybir as mybir
from concourse.bass_utils import run_bass_kernel_spmd

F32 = mybir.dt.float32
BF16 = mybir.dt.bfloat16
AF = mybir.ActivationFunctionType
ALU = mybir.AluOpType
AX = mybir.AxisListType

D = 1024
H = 16
DH = 64
E = 8
DCH = 8
SUB = 128
SPC = 2
T = SUB * SPC
STRICT_ENGINES = set(os.environ.get("KSTRICT", "dve,act,pool").split(",")) - {""}



class Buf:
    __slots__ = ("name", "last_w", "readers", "overlaps")

    def __init__(self, name):
        self.name = name
        self.last_w = None
        self.readers = []
        self.overlaps = []


def link_overlaps(*bufs):
    for b in bufs:
        for b2 in bufs:
            if b2 is not b and b2 not in b.overlaps:
                b.overlaps.append(b2)


class Eng:
    __slots__ = ("name", "obj", "sem", "dma", "n_ops", "incs", "cum", "known",
                 "last_inst", "insts")

    def __init__(self, name, obj, sem, dma):
        self.name = name
        self.obj = obj
        self.sem = sem
        self.dma = dma
        self.n_ops = 0
        self.incs = []
        self.cum = 0
        self.known = {}
        self.last_inst = None
        self.insts = []


class KB:
    def __init__(self, nc, es):
        self.nc = nc
        self.es = es
        self.engs = {}

    def eng(self, name, obj, dma=False):
        sem = self.es.enter_context(self.nc.semaphore(f"sem_{name}"))
        self.engs[name] = Eng(name, obj, sem, dma)
        return self.engs[name]

    def _value_for(self, peng, idx):
        lo = None
        for oi, val in reversed(peng.incs):
            if oi >= idx:
                lo = val
            else:
                break
        if lo is not None:
            return lo
        assert not peng.dma, "dma producers always have eager incs"
        j = min(idx + 1, peng.n_ops - 1)
        peng.cum += 1
        peng.insts[j].then_inc(peng.sem, 1)
        peng.incs.append((j, peng.cum))
        return peng.cum

    def buf_sem(self, buf, qname):
        pname = f"dq_{qname}_{buf.name}"
        if pname not in self.engs:
            sem = self.es.enter_context(self.nc.semaphore(pname))
            pe = Eng(pname, None, sem, True)
            self.engs[pname] = pe
        return pname

    def op(self, ename, fn, reads=(), writes=(), dma_buf=None, exotic=False):
        eng = self.engs[ename]
        assert (dma_buf is None) or eng.dma
        waits = {}

        def need(pname, idx, pstrict=False):
            if (pname == ename and not eng.dma and not (pstrict or exotic)
                    and ename not in STRICT_ENGINES):
                return
            if pname == ename and (eng.dma or ename == "pe"):
                return
            if waits.get(pname, -1) < self._tok_val(pname, idx):
                waits[pname] = self._tok_val(pname, idx)

        seen = set()
        for b in reads:
            for bb in (b, *b.overlaps):
                if id(bb) in seen:
                    continue
                if bb.last_w:
                    need(*bb.last_w)
            seen.add(id(b))
        for b in writes:
            for bb in (b, *b.overlaps):
                if bb.last_w:
                    need(*bb.last_w)
                for r in bb.readers:
                    need(*r)
        for pname, val in waits.items():
            if eng.known.get(pname, -1) < val:
                eng.obj.wait_ge(self.engs[pname].sem, val)
                eng.known[pname] = val
        inst = fn(eng.obj)
        eng.n_ops += 1
        eng.last_inst = inst
        eng.insts.append(inst)
        if eng.dma:
            assert dma_buf is not None, "DMA ops must name their slot buffer"
            pname = self.buf_sem(dma_buf, ename)
            peng = self.engs[pname]
            peng.cum += 16
            inst.then_inc(peng.sem, 16)
            tok = (pname, peng.cum, False)
        else:
            tok = (ename, eng.n_ops - 1, exotic)
        for b in writes:
            b.last_w = tok
            b.readers = []
        for b in reads:
            b.readers.append(tok)
        return inst

    def _tok_val(self, pname, idx):
        peng = self.engs[pname]
        if peng.dma:
            return idx
        return self._value_for(peng, idx)

    def final_wait(self, ename):
        eng = self.engs[ename]
        for pname, peng in self.engs.items():
            if pname == ename or (peng.n_ops == 0 and not peng.dma):
                continue
            if peng.dma:
                if peng.cum == 0:
                    continue
                val = peng.cum
            else:
                val = self._value_for(peng, peng.n_ops - 1)
            if eng.known.get(pname, -1) < val:
                eng.obj.wait_ge(peng.sem, val)
                eng.known[pname] = val



def build_nc(n_tokens=8192, debug=0):
    assert n_tokens % T == 0
    nch = n_tokens // T
    nsub = n_tokens // SUB

    nc = bass.Bass("TRN2", target_bir_lowering=False)

    dx = nc.dram_tensor("x", [n_tokens, D], F32, kind="ExternalInput").ap()
    dy = nc.dram_tensor("y", [n_tokens, D], F32, kind="ExternalOutput").ap()
    dwq = nc.dram_tensor("wq", [128, 8 * D], F32, kind="ExternalInput").ap()
    dwk = nc.dram_tensor("wk", [128, 8 * D], F32, kind="ExternalInput").ap()
    dwv = nc.dram_tensor("wv", [128, 8 * D], F32, kind="ExternalInput").ap()
    dwo = nc.dram_tensor("wo", [128, 8 * D], F32, kind="ExternalInput").ap()
    dwa = nc.dram_tensor("wa", [128, 8 * H], F32, kind="ExternalInput").ap()
    dgw = nc.dram_tensor("gw", [128, 8 * E], F32, kind="ExternalInput").ap()
    dexpw = nc.dram_tensor("expw", [E, 128, 8 * D], F32, kind="ExternalInput").ap()
    dexpb = nc.dram_tensor("expb", [E, D], F32, kind="ExternalInput").ap()
    dwmod = nc.dram_tensor("wmod", [128, 48 * D], F32, kind="ExternalInput").ap()
    dc = nc.dram_tensor("cmod", [128, 8], F32, kind="ExternalInput").ap()
    dba = nc.dram_tensor("ba_bc", [128, H], F32, kind="ExternalInput").ap()
    dgb = nc.dram_tensor("gateb_bc", [128, E], F32, kind="ExternalInput").ap()
    dtri = nc.dram_tensor("tri", [128, 128], F32, kind="ExternalInput").ap()
    dident = nc.dram_tensor("identbf", [128, 128], BF16, kind="ExternalInput").ap()
    donesb = nc.dram_tensor("onesbf", [128, 1], BF16, kind="ExternalInput").ap()
    doneh = nc.dram_tensor("oneh127", [128, 1], F32, kind="ExternalInput").ap()
    dones64 = nc.dram_tensor("ones1x64", [1, 64], F32, kind="ExternalInput").ap()
    dones128 = nc.dram_tensor("ones1x128", [1, 128], F32, kind="ExternalInput").ap()
    dcvec = nc.dram_tensor("cvec", [128, 4], F32, kind="ExternalInput").ap()
    decache = nc.dram_tensor("ecache", [E, 128, 8 * D], BF16, kind="Internal").ap()
    if debug:
        ddbg = nc.dram_tensor("dbg", [debug, 128, D], F32, kind="ExternalOutput").ap()

    es = ExitStack()
    kb = KB(nc, es)
    S = lambda name, shape, dt: es.enter_context(nc.sbuf_tensor(name, shape, dt))
    P = lambda name, shape, dt: es.enter_context(nc.psum_tensor(name, shape, dt))

    xr = [S(f"xr{i}", [128, D], F32) for i in range(4)]
    wq_bf = S("wq_bf", [128, 8 * D], BF16)
    wk_bf = S("wk_bf", [128, 8 * D], BF16)
    wv_bf = S("wv_bf", [128, 8 * D], BF16)
    wo_bf = S("wo_bf", [128, 8 * D], BF16)
    wa_bf = S("wa_bf", [128, 8 * H], BF16)
    gw_bf = S("gw_bf", [128, 8 * E], BF16)
    expb_f = S("expb_f", [E, D], F32)
    expb_bf = S("expb_bf", [E, D], BF16)
    ering = [S(f"ering{i}", [128, 4096], BF16) for i in range(4)]
    wring = [S(f"wring{i}", [128, D], F32) for i in range(4)]
    sf1 = S("sf1", [128, D], F32)
    s1 = S("s1", [128, D], BF16)
    s2 = S("s2", [128, D], BF16)
    h1_bf = S("h1_bf", [128, D], BF16)
    h1_fm = S("h1_fm", [128, D], BF16)
    qt_tm = S("qt_tm", [128, D], BF16)
    kt_tm = S("kt_tm", [128, D], BF16)
    v_tm = S("v_tm", [128, D], BF16)
    qt_fm = S("qt_fm", [128, D], BF16)
    kt_fm = S("kt_fm", [128, D], BF16)
    smt = [S(f"smt{i}", [128, 128], BF16) for i in range(2)]
    ao_tm = S("ao_tm", [128, D], BF16)
    ao_fm = S("ao_fm", [128, D], BF16)
    h2_tm = S("h2_tm", [128, D], BF16)
    h2_fm = S("h2_fm", [128, SPC * D], BF16)
    Sst = S("Sst", [128, D], BF16)
    zst = S("zst", [128, H], BF16)
    kvbf = S("kvbf", [128, D], BF16)
    zvbf = S("zvbf", [128, H], BF16)
    u16 = S("u16", [128, H], F32)
    t16a = S("t16a", [128, H], F32)
    t16b = S("t16b", [128, H], F32)
    lna = S("lna", [128, H], F32)
    el = S("el", [128, H], F32)
    enl = S("enl", [128, H], F32)
    el_bf = S("el_bf", [128, H], BF16)
    enl_bf = S("enl_bf", [128, H], BF16)
    elcrow = S("elcrow", [1, H], F32)
    elc_bf = S("elc_bf", [128, H], BF16)
    den_sb = S("den_sb", [128, H], F32)
    rden = S("rden", [128, H], F32)
    stat = S("stat", [128, 16], F32)
    stat2 = S("stat2", [128, 4], F32)
    snap = S("snap", [128, 8], F32)
    g8 = S("g8", [128, 6 * E], F32)
    mrep = S("mrep", [128, 8], F32)
    m1c = S("m1c", [128, 4], F32)
    combc = S("combc", [128, SPC * E], F32)
    comb_bfx = S("comb_bfx", [128, SPC * E], BF16)
    combT_bf = S("combT_bf", [8, SPC * 128], BF16)
    c_sil = S("c_sil", [128, 8], F32)
    c_t8 = S("c_t8", [128, 16], F32)
    rowtmp = S("rowtmp", [1, D], F32)
    caT = S("caT", [128, DCH], F32)
    cbT = S("cbT", [128, DCH], F32)
    saT = S("saT", [128, DCH], F32)
    sbT = S("sbT", [128, DCH], F32)
    ga_exp = S("ga_exp", [128, D], F32)
    gb_exp = S("gb_exp", [128, D], F32)
    babc2 = S("babc2", [128, H], F32)
    gateb_t = S("gateb_t", [128, E], F32)
    tri_sb = S("tri_sb", [128, 128], F32)
    ident_sb = S("ident_sb", [128, 128], BF16)
    onesb_sb = S("onesb_sb", [128, 1], BF16)
    oneh_sb = S("oneh_sb", [128, 1], F32)
    ones64_sb = S("ones64_sb", [1, 64], F32)
    ones128_sb = S("ones128_sb", [1, 128], F32)
    cvec = S("cvec_sb", [128, 4], F32)

    P01 = P("P01", [128, D], F32)
    P23 = P("P23", [128, D], F32)
    P45 = P("P45", [128, D], F32)
    P6 = P("P6", [128, 512], F32)
    P7 = P("P7", [128, D], BF16)

    B = lambda n: Buf(n)
    b_xr = [B(f"xr{i}") for i in range(4)]
    b_wq, b_wk, b_wv, b_wo = B("wq"), B("wk"), B("wv"), B("wo")
    b_wa, b_gw = B("wa"), B("gw")
    b_expbf, b_expbbf = B("expb_f"), B("expb_bf")
    b_ering = [B(f"ering{i}") for i in range(4)]
    b_wring = [B(f"wring{i}") for i in range(4)]
    b_sf1, b_s1, b_s2 = B("sf1"), B("s1"), B("s2")
    b_h1bf = B("h1_bf")
    b_h1fm = [B(f"h1_fm{u}") for u in range(DCH)]
    b_qttm = [B(f"qt_tm{u}") for u in range(DCH)]
    b_kttm = [B(f"kt_tm{u}") for u in range(DCH)]
    b_vtm = B("v_tm")
    b_qtfm, b_ktfm = B("qt_fm"), B("kt_fm")
    b_smt = [B("smt0"), B("smt1")]
    b_aotm = [B(f"ao_tm{u}") for u in range(DCH)]
    b_aofm = B("ao_fm")
    b_h2tm = B("h2_tm")
    b_h2fm = [[B(f"h2_fm{s}_{u}") for u in range(DCH)] for s in range(SPC)]
    b_S, b_z = B("Sst"), B("zst")
    b_kvbf, b_zvbf = B("kvbf"), B("zvbf")
    b_u16, b_t16a, b_t16b, b_lna = B("u16"), B("t16a"), B("t16b"), B("lna")
    b_el, b_enl, b_elcrow, b_elcbf = B("el"), B("enl"), B("elcrow"), B("elc_bf")
    b_elbf, b_enlbf = B("el_bf"), B("enl_bf")
    b_den, b_rden, b_stat = B("den"), B("rden"), B("stat")
    b_stat2 = B("stat2")
    b_snap = B("snap")
    b_g8, b_mrep, b_m1c = B("g8"), B("mrep"), B("m1c")
    b_comb, b_combbfx, b_combT = B("combc"), B("comb_bfx"), B("combT_bf")
    b_csil, b_ct8, b_rowtmp = B("c_sil"), B("c_t8"), B("rowtmp")
    b_ca1, b_cb1, b_saf, b_sbf = B("ca1"), B("cb1"), B("sa_f"), B("sb_f")
    b_gaexp, b_gbexp = B("ga_exp"), B("gb_exp")
    b_babc2, b_gatebt = B("babc2"), B("gateb_t")
    b_tri, b_ident, b_onesb, b_oneh, b_ones64 = B("tri"), B("ident"), B("onesb"), B("oneh"), B("ones64")
    b_ones128 = B("ones128")
    b_cvec = B("cvec")
    b_ecache = [[B(f"ec{e}_{q}") for q in range(2)] for e in range(E)]
    b_y = B("y")

    b_pq, b_scA, b_scB, b_py, b_fe0 = B("pq"), B("scA"), B("scB"), B("py"), B("fe0")
    link_overlaps(b_pq, b_py, b_fe0)
    for bb in (b_pq, b_py, b_fe0):
        link_overlaps(bb, b_scA)
        link_overlaps(bb, b_scB)
    b_pk, b_kvS, b_fe1 = B("pk"), B("kvS"), B("fe1")
    link_overlaps(b_pk, b_kvS, b_fe1)
    b_pv, b_po, b_pbm, b_fe2, b_pmod = B("pv"), B("po"), B("pbm"), B("fe2"), B("pmod")
    link_overlaps(b_pv, b_po, b_pbm, b_fe2, b_pmod)
    b_p6 = B("p6")
    b_p7 = B("p7")
    b_modrow = B("modrow")
    link_overlaps(b_modrow, b_sf1)

    scA = P01[:, 0:128]
    scB = P01[:, 512:640]
    p6a = P6[:, 0:16]
    p6L = P6[:, 16:32]
    p6den = P6[:, 32:48]
    p6kvz = P6[:, 48:64]
    p6er = P6[0:1, 64:80]
    p6eb = P6[:, 80:96]
    p6g = P6[:, 96:104]
    p6t = P6[:, 104:112]

    with nc.Block():
        pe = kb.eng("pe", nc.tensor)
        dve = kb.eng("dve", nc.vector)
        act = kb.eng("act", nc.scalar)
        qs = kb.eng("qs", nc.sync, dma=True)
        qg = kb.eng("qg", nc.gpsimd, dma=True)
        pool = kb.eng("pool", nc.gpsimd, dma=False)
        qa = kb.eng("qa", nc.scalar, dma=True)

        OP = kb.op

        def warm(target_ap, rbufs, wbuf):
            OP("pe", lambda e: e.matmul(out=target_ap, lhsT=onesb_sb[:], rhs=onesb_sb[:],
                                        start=True, stop=True),
               reads=list(rbufs), writes=[wbuf])

        nc.scalar.add_instruction(mybir.InstLoadActFuncSet(
            name=nc.get_next_instruction_name(), act_func_set_id=6, ins=[], outs=[]))
        dbg_i = [0]
        dbg_f32 = S("dbg_f32", [128, D], F32) if debug else None
        b_dbgf = B("dbgf")
        b_dbgq = B("dbgq")

        def dump(ap, rbufs, ncols=D, cast=False, nparts=128):
            if not debug or dbg_i[0] >= debug:
                return
            i = dbg_i[0]
            dbg_i[0] += 1
            if cast:
                OP("act", lambda e: e.copy(out=dbg_f32[:, 0:ncols], in_=ap),
                   reads=list(rbufs), writes=[b_dbgf])
                OP("qg", lambda e, i=i: e.dma_start(out=ddbg[i, :, 0:ncols],
                                                    in_=dbg_f32[:, 0:ncols]),
                   reads=[b_dbgf], writes=[b_dbgq], dma_buf=b_dbgq)
            else:
                OP("qg", lambda e, i=i: e.dma_start(out=ddbg[i, 0:nparts, 0:ncols], in_=ap),
                   reads=list(rbufs) + [b_dbgq], writes=[b_dbgq], dma_buf=b_dbgq)
        kb.dump = dump

        OP("qs", lambda e: e.dma_start(out=tri_sb[:], in_=dtri), writes=[b_tri], dma_buf=b_tri)
        OP("qs", lambda e: e.dma_start(out=ident_sb[:], in_=dident), writes=[b_ident], dma_buf=b_ident)
        OP("qs", lambda e: e.dma_start(out=onesb_sb[:], in_=donesb), writes=[b_onesb], dma_buf=b_onesb)
        OP("qs", lambda e: e.dma_start(out=oneh_sb[:], in_=doneh), writes=[b_oneh], dma_buf=b_oneh)
        OP("qs", lambda e: e.dma_start(out=ones64_sb[:], in_=dones64), writes=[b_ones64], dma_buf=b_ones64)
        OP("qs", lambda e: e.dma_start(out=ones128_sb[:], in_=dones128), writes=[b_ones128], dma_buf=b_ones128)
        OP("qs", lambda e: e.dma_start(out=cvec[:], in_=dcvec), writes=[b_cvec], dma_buf=b_cvec)
        C_EPS, C_NHALF, C_NEG1 = cvec[:, 0:1], cvec[:, 1:2], cvec[:, 2:3]
        OP("qs", lambda e: e.dma_start(out=c_sil[:], in_=dc), writes=[b_csil], dma_buf=b_csil)
        OP("qs", lambda e: e.dma_start(out=babc2[:], in_=dba), writes=[b_babc2], dma_buf=b_babc2)
        OP("qs", lambda e: e.dma_start(out=gateb_t[:], in_=dgb), writes=[b_gatebt], dma_buf=b_gatebt)
        OP("qs", lambda e: e.dma_start(out=expb_f[:], in_=dexpb), writes=[b_expbf], dma_buf=b_expbf)
        for g in range(min(4, nsub)):
            OP("qg", lambda e, g=g: e.dma_start(out=xr[g][:], in_=dx[g * 128:(g + 1) * 128, :]),
               writes=[b_xr[g]], dma_buf=b_xr[g])

        OP("dve", lambda e: e.memset(Sst[:], 0.0), writes=[b_S])
        OP("dve", lambda e: e.memset(zst[:], 0.0), writes=[b_z])
        OP("dve", lambda e: e.memset(mrep[:], -1e30), writes=[b_mrep])
        OP("dve", lambda e: e.tensor_scalar(out=babc2[:], in0=babc2[:], scalar1=2.0,
                                            scalar2=None, op0=ALU.add),
           reads=[b_babc2], writes=[b_babc2])
        OP("act", lambda e: e.activation(out=c_t8[:, 0:8], in_=c_sil[:], func=AF.Exp,
                                         scale=C_NEG1),
           reads=[b_csil, b_cvec], writes=[b_ct8])
        OP("dve", lambda e: e.tensor_scalar(out=c_t8[:, 0:8], in0=c_t8[:, 0:8], scalar1=1.0,
                                            scalar2=None, op0=ALU.add),
           reads=[b_ct8], writes=[b_ct8])
        OP("dve", lambda e: e.reciprocal(out=c_t8[:, 8:16], in_=c_t8[:, 0:8]),
           reads=[b_ct8], writes=[b_ct8], exotic=True)
        OP("dve", lambda e: e.tensor_tensor(out=c_sil[:], in0=c_sil[:], in1=c_t8[:, 8:16],
                                            op=ALU.mult),
           reads=[b_csil, b_ct8], writes=[b_csil])

        modrow = sf1[0:1, 0:D]
        wslot = 0

        def stream_piece(dram_ap, ncols=D):
            nonlocal wslot
            sl = wslot % 4
            wslot += 1
            OP("qs", lambda e: e.dma_start(out=wring[sl][:, 0:ncols], in_=dram_ap),
               writes=[b_wring[sl]], dma_buf=b_wring[sl])
            return sl

        def mod_block(m):
            for j in range(DCH):
                sl = stream_piece(dwmod[:, (m * 8 + j) * D:(m * 8 + j + 1) * D])
                for hf in range(2):
                    OP("pe", lambda e, j=j, hf=hf, sl=sl: e.matmul(
                        out=P45[0:1, 512 * hf:512 * (hf + 1)],
                        lhsT=c_sil[:, j:j + 1], rhs=wring[sl][:, 512 * hf:512 * (hf + 1)],
                        start=(j == 0), stop=(j == DCH - 1)),
                       reads=[b_csil, b_wring[sl], b_pmod], writes=[b_pmod])
            OP("act", lambda e: e.copy(out=modrow, in_=P45[0:1, 0:D]),
               reads=[b_pmod], writes=[b_modrow])

        def bcast_row(dst_ap, dstbuf, bias=None, dtype_note=None):
            for hf in range(2):
                OP("pe", lambda e, hf=hf: e.matmul(
                    out=P23[:, 512 * hf:512 * (hf + 1)],
                    lhsT=ones128_sb[:],
                    rhs=modrow[:, 512 * hf:512 * (hf + 1)],
                    start=True, stop=True),
                   reads=[b_ones128, b_modrow, b_pk], writes=[b_pk])
            if bias is None:
                OP("act", lambda e: e.copy(out=dst_ap, in_=P23[:, 0:D]),
                   reads=[b_pk], writes=[dstbuf])
            else:
                OP("act", lambda e: e.activation(out=dst_ap, in_=P23[:, 0:D], func=AF.Copy,
                                                 bias=bias),
                   reads=[b_pk], writes=[dstbuf])

        def colT(dst_ap, dstbuf, bias=None):
            for b2 in range(DCH):
                OP("pe", lambda e, b2=b2: e.matmul(
                    out=p6t[:, b2:b2 + 1], lhsT=modrow[:, 128 * b2:128 * (b2 + 1)],
                    rhs=ones128_sb[0:1, 0:1], start=True, stop=True),
                   reads=[b_modrow, b_ones128, b_p6], writes=[b_p6])
            if bias is None:
                OP("act", lambda e: e.copy(out=dst_ap, in_=p6t[:]),
                   reads=[b_p6], writes=[dstbuf])
            else:
                OP("act", lambda e: e.activation(out=dst_ap, in_=p6t[:], func=AF.Copy,
                                                 bias=bias),
                   reads=[b_p6], writes=[dstbuf])

        def gate_transform():
            OP("act", lambda e: e.activation(out=rowtmp[:], in_=modrow, func=AF.Sign),
               reads=[b_modrow], writes=[b_rowtmp])
            OP("act", lambda e: e.activation(out=modrow, in_=modrow, func=AF.Abs),
               reads=[b_modrow], writes=[b_modrow])
            OP("dve", lambda e: e.tensor_scalar(out=modrow, in0=modrow, scalar1=0.25,
                                                scalar2=None, op0=ALU.max),
               reads=[b_modrow], writes=[b_modrow])
            OP("dve", lambda e: e.tensor_tensor(out=modrow, in0=modrow, in1=rowtmp[:], op=ALU.mult),
               reads=[b_modrow, b_rowtmp], writes=[b_modrow])

        mod_block(0)
        kb.dump(c_sil[:], [b_csil], ncols=8)
        kb.dump(modrow, [b_modrow], nparts=1)
        colT(saT[:], b_saf)
        mod_block(1)
        colT(caT[:], b_ca1, bias=1.0)
        mod_block(2)
        gate_transform()
        bcast_row(gb_exp[:], b_gbexp)
        mod_block(3)
        gate_transform()
        bcast_row(ga_exp[:], b_gaexp)
        mod_block(4)
        colT(sbT[:], b_sbf)
        mod_block(5)
        colT(cbT[:], b_cb1, bias=1.0)
        kb.dump(saT[:], [b_saf], ncols=DCH)
        kb.dump(caT[:], [b_ca1], ncols=DCH)
        kb.dump(gb_exp[:], [b_gbexp])
        kb.dump(ga_exp[:], [b_gaexp])
        kb.dump(sbT[:], [b_sbf], ncols=DCH)
        kb.dump(cbT[:], [b_cb1], ncols=DCH)

        def load_weight(dram, dst, dstbuf, ncols, fold=None, foldbuf=None):
            for j in range(ncols // D):
                sl = stream_piece(dram[:, j * D:(j + 1) * D])
                if fold is None:
                    OP("act", lambda e, j=j, sl=sl: e.copy(out=dst[:, j * D:(j + 1) * D],
                                                           in_=wring[sl][:]),
                       reads=[b_wring[sl]], writes=[dstbuf])
                else:
                    OP("dve", lambda e, j=j, sl=sl: e.tensor_tensor(
                        out=dst[:, j * D:(j + 1) * D], in0=wring[sl][:], in1=fold, op=ALU.mult),
                       reads=[b_wring[sl], foldbuf], writes=[dstbuf])

        load_weight(dwq, wq_bf, b_wq, 8 * D)
        load_weight(dwk, wk_bf, b_wk, 8 * D)
        load_weight(dwv, wv_bf, b_wv, 8 * D)
        sl = stream_piece(dwa, ncols=8 * H)
        OP("act", lambda e, sl=sl: e.copy(out=wa_bf[:], in_=wring[sl][:, 0:8 * H]),
           reads=[b_wring[sl]], writes=[b_wa])
        sl = stream_piece(dgw, ncols=8 * E)
        OP("act", lambda e, sl=sl: e.copy(out=gw_bf[:], in_=wring[sl][:, 0:8 * E]),
           reads=[b_wring[sl]], writes=[b_gw])
        load_weight(dwo, wo_bf, b_wo, 8 * D, fold=ga_exp[:], foldbuf=b_gaexp)
        OP("dve", lambda e: e.tensor_tensor(out=expb_bf[:], in0=expb_f[:], in1=gb_exp[0:E, :],
                                            op=ALU.mult),
           reads=[b_expbf, b_gbexp], writes=[b_expbbf])

        tr_cnt = [0]

        def transposes(src_ap, srcbufs, dst_ap, dstbuf):
            blk = srcbufs if isinstance(srcbufs, list) else [srcbufs] * DCH
            for u in range(DCH):
                OP("pe", lambda e, u=u: e.transpose(out=P7[:, 128 * u:128 * (u + 1)],
                                                    in_=src_ap[:, 128 * u:128 * (u + 1)],
                                                    identity=ident_sb[:]),
                   reads=[blk[u], b_ident], writes=[b_p7])
            OP("act", lambda e: e.copy(out=dst_ap, in_=P7[:]), reads=[b_p7], writes=[dstbuf])

        def proj(h_fm_ap, h_fm_buf, w_sb, w_buf, pdst, pbuf, nout):
            nhalf = max(1, nout // 512)
            step = min(nout, 512)
            blk = h_fm_buf if isinstance(h_fm_buf, list) else [h_fm_buf] * DCH
            for j in range(DCH):
                for hf in range(nhalf):
                    OP("pe", lambda e, j=j, hf=hf: e.matmul(
                        out=pdst[:, hf * step:(hf + 1) * step],
                        lhsT=h_fm_ap[:, 128 * j:128 * (j + 1)],
                        rhs=w_sb[:, j * nout + hf * step: j * nout + (hf + 1) * step],
                        start=(j == 0), stop=(j == DCH - 1)),
                       reads=[blk[j], w_buf, pbuf], writes=[pbuf])

        def ln_stats(x_ap, xbuf):
            OP("dve", lambda e: e.bn_stats(out=stat[:, 0:6], in_=x_ap[:, 0:512]),
               reads=[xbuf], writes=[b_stat], exotic=True)
            OP("dve", lambda e: e.bn_stats(out=stat[:, 6:12], in_=x_ap[:, 512:1024]),
               reads=[xbuf], writes=[b_stat], exotic=True)
            OP("dve", lambda e: e.bn_aggr(out=stat[:, 12:14], in_=stat[:, 0:12]),
               reads=[b_stat], writes=[b_stat], exotic=True)
            OP("act", lambda e: e.activation(out=stat2[:, 0:1], in_=stat[:, 13:14], func=AF.Ln,
                                             bias=C_EPS),
               reads=[b_stat, b_cvec], writes=[b_stat2])
            OP("act", lambda e: e.activation(out=stat2[:, 1:2], in_=stat2[:, 0:1], func=AF.Exp,
                                             scale=C_NHALF),
               reads=[b_stat2, b_cvec], writes=[b_stat2])

        def ln_norm(x_ap, xbuf, dst_tm, dstbuf):
            OP("dve", lambda e: e.tensor_scalar(out=dst_tm, in0=x_ap,
                                                scalar1=stat[:, 12:13],
                                                scalar2=stat2[:, 1:2],
                                                op0=ALU.subtract, op1=ALU.mult),
               reads=[xbuf, b_stat, b_stat2], writes=[dstbuf])

        def elu_scale(psrc, psbuf, elbf_ap, elbf_buf, dst, dstbufs, on_dve=False):
            if on_dve:
                OP("dve", lambda e: e.tensor_scalar(out=s1[:], in0=psrc, scalar1=0.0,
                                                    scalar2=None, op0=ALU.min),
                   reads=[psbuf], writes=[b_s1])
                OP("act", lambda e: e.activation(out=s2[:], in_=s1[:], func=AF.Exp),
                   reads=[b_s1], writes=[b_s2])
                OP("dve", lambda e: e.tensor_scalar(out=s1[:], in0=psrc, scalar1=0.0,
                                                    scalar2=None, op0=ALU.max),
                   reads=[psbuf], writes=[b_s1])
            else:
                OP("act", lambda e: e.activation(out=s1[:], in_=psrc, func=AF.Relu,
                                                 scale=C_NEG1),
                   reads=[psbuf, b_cvec], writes=[b_s1])
                OP("act", lambda e: e.activation(out=s2[:], in_=s1[:], func=AF.Exp,
                                                 scale=C_NEG1),
                   reads=[b_s1, b_cvec], writes=[b_s2])
                OP("act", lambda e: e.activation(out=s1[:], in_=psrc, func=AF.Relu),
                   reads=[psbuf], writes=[b_s1])
            OP("dve", lambda e: e.tensor_tensor(out=dst, in0=s2[:], in1=s1[:], op=ALU.add),
               reads=[b_s1, b_s2], writes=list(dstbufs))
            rep = elbf_ap.unsqueeze(2).to_broadcast([128, H, 64])
            d3 = dst.rearrange("p (h o) -> p h o", h=H)
            OP("dve", lambda e, rep=rep, d3=d3: e.tensor_tensor(out=d3, in0=d3, in1=rep,
                                                                op=ALU.mult),
               reads=list(dstbufs) + [elbf_buf], writes=list(dstbufs))

        ln1_done = set()
        h1t_done = set()
        stts_done = set()
        pe_backlog = []

        def emit_backlog():
            for f in pe_backlog:
                f()
            pe_backlog.clear()

        def ln1_parts(g):
            xa = xr[g % 4][:]
            xb = b_xr[g % 4]

            def p1():
                OP("dve", lambda e: e.bn_stats(out=stat[:, 0:6], in_=xa[:, 0:512]),
                   reads=[xb], writes=[b_stat], exotic=True)

            def p2():
                OP("dve", lambda e: e.bn_stats(out=stat[:, 6:12], in_=xa[:, 512:1024]),
                   reads=[xb], writes=[b_stat], exotic=True)

            def p3():
                OP("dve", lambda e: e.bn_aggr(out=stat[:, 12:14], in_=stat[:, 0:12]),
                   reads=[b_stat], writes=[b_stat], exotic=True)
                OP("act", lambda e: e.activation(out=stat2[:, 0:1], in_=stat[:, 13:14],
                                                 func=AF.Ln, bias=C_EPS),
                   reads=[b_stat, b_cvec], writes=[b_stat2])
                OP("act", lambda e: e.activation(out=stat2[:, 1:2], in_=stat2[:, 0:1],
                                                 func=AF.Exp, scale=C_NHALF),
                   reads=[b_stat2, b_cvec], writes=[b_stat2])

            def p4():
                ln_norm(xa, xb, h1_bf[:], b_h1bf)
                ln1_done.add(g)

            return [p1, p2, p3, p4]

        def ln1(g):
            for p in ln1_parts(g):
                p()

        def h1t(g):
            for u in range(DCH):
                OP("pe", lambda e, u=u: e.transpose(out=P7[:, 128 * u:128 * (u + 1)],
                                                    in_=h1_bf[:, 128 * u:128 * (u + 1)],
                                                    identity=ident_sb[:]),
                   reads=[b_h1bf, b_ident], writes=[b_p7])
            OP("act", lambda e: e.copy(out=h1_bf[:], in_=P7[:]),
               reads=[b_p7], writes=[b_h1bf])
            h1t_done.add(g)

        def h1_stts(g):
            for u in range(DCH):
                sa_b = saT[:, u:u + 1].to_broadcast([128, 128])
                OP("dve", lambda e, u=u, sa_b=sa_b: e.scalar_tensor_tensor(
                    out=h1_fm[:, 128 * u:128 * (u + 1)],
                    in0=h1_bf[:, 128 * u:128 * (u + 1)],
                    scalar=caT[:, u:u + 1], in1=sa_b, op0=ALU.mult, op1=ALU.add),
                   reads=[b_h1bf, b_ca1, b_saf], writes=[b_h1fm[u]])
            stts_done.add(g)

        front_done = set()

        def attn_front(g):
            for j in range(DCH):
                OP("pe", lambda e, j=j: e.matmul(
                    out=p6a, lhsT=h1_fm[:, 128 * j:128 * (j + 1)],
                    rhs=wa_bf[:, j * H:(j + 1) * H],
                    start=(j == 0), stop=(j == DCH - 1)),
                   reads=[b_h1fm[j], b_wa, b_p6], writes=[b_p6])
            OP("dve", lambda e: e.tensor_tensor(out=u16[:], in0=p6a, in1=babc2[:], op=ALU.add),
               reads=[b_p6, b_babc2], writes=[b_u16])
            OP("act", lambda e: e.activation(out=t16a[:], in_=u16[:], func=AF.Abs),
               reads=[b_u16], writes=[b_t16a])
            OP("act", lambda e: e.activation(out=t16b[:], in_=t16a[:], func=AF.Exp, scale=C_NEG1),
               reads=[b_t16a, b_cvec], writes=[b_t16b])
            OP("act", lambda e: e.activation(out=t16a[:], in_=t16b[:], func=AF.Ln, bias=1.0),
               reads=[b_t16b], writes=[b_t16a])
            OP("dve", lambda e: e.tensor_scalar(out=t16b[:], in0=u16[:], scalar1=0.0,
                                                scalar2=None, op0=ALU.min),
               reads=[b_u16], writes=[b_t16b])
            OP("dve", lambda e: e.tensor_tensor(out=lna[:], in0=t16b[:], in1=t16a[:],
                                                op=ALU.subtract),
               reads=[b_t16a, b_t16b], writes=[b_lna])
            front_done.add(g)

        def attn_subchunk(g):
            xa = xr[g % 4][:]
            xb = b_xr[g % 4]
            if g not in ln1_done:
                ln1(g)
            if g not in h1t_done:
                h1t(g)
            if g not in stts_done:
                h1_stts(g)
            if g not in front_done:
                attn_front(g)
            proj(h1_fm[:], b_h1fm, wk_bf, b_wk, P23, b_pk, D)
            OP("pe", lambda e: e.matmul(out=p6L, lhsT=tri_sb[:], rhs=lna[:], start=True,
                                        stop=True),
               reads=[b_tri, b_lna, b_p6], writes=[b_p6])
            OP("act", lambda e: e.activation(out=el[:], in_=p6L, func=AF.Exp),
               reads=[b_p6], writes=[b_el])
            OP("act", lambda e: e.activation(out=enl[:], in_=p6L, func=AF.Exp, scale=C_NEG1),
               reads=[b_p6, b_cvec], writes=[b_enl])
            OP("act", lambda e: e.copy(out=el_bf[:], in_=el[:]), reads=[b_el], writes=[b_elbf])
            OP("act", lambda e: e.copy(out=enl_bf[:], in_=enl[:]), reads=[b_enl], writes=[b_enlbf])
            OP("pe", lambda e: e.matmul(out=p6er, lhsT=oneh_sb[:], rhs=el[:], start=True,
                                        stop=True),
               reads=[b_oneh, b_el, b_p6], writes=[b_p6])
            OP("act", lambda e: e.copy(out=elcrow[:], in_=p6er), reads=[b_p6], writes=[b_elcrow])
            OP("pe", lambda e: e.matmul(out=p6eb, lhsT=ones128_sb[:], rhs=elcrow[:], start=True,
                                        stop=True),
               reads=[b_ones128, b_elcrow, b_p6], writes=[b_p6])
            OP("act", lambda e: e.copy(out=elc_bf[:], in_=p6eb), reads=[b_p6], writes=[b_elcbf])
            proj(h1_fm[:], b_h1fm, wq_bf, b_wq, P01, b_pq, D)
            elu_scale(P23[:], b_pk, enl_bf[:], b_enlbf, kt_tm[:], b_kttm, on_dve=True)
            emit_backlog()
            proj(h1_fm[:], b_h1fm, wv_bf, b_wv, P45, b_pv, D)
            elu_scale(P01[:], b_pq, el_bf[:], b_elbf, qt_tm[:], b_qttm)
            transposes(kt_tm[:], b_kttm, kt_fm[:], b_ktfm)
            transposes(qt_tm[:], b_qttm, qt_fm[:], b_qtfm)
            OP("act", lambda e: e.copy(out=v_tm[:], in_=P45[:]), reads=[b_pv], writes=[b_vtm])

            def em_scores(h):
                u, half = h // 2, h % 2
                base = 64 * half
                qsl = qt_fm[base:base + 64, 128 * u:128 * (u + 1)]
                ksl = kt_fm[base:base + 64, 128 * u:128 * (u + 1)]
                sc = scA if h % 2 == 0 else scB
                bsc = b_scA if h % 2 == 0 else b_scB
                OP("pe", lambda e, sc=sc, ksl=ksl, qsl=qsl: e.matmul(
                    out=sc, lhsT=ksl, rhs=qsl, start=True, stop=True),
                   reads=[b_ktfm, b_qtfm, bsc], writes=[bsc])

            def em_mask(h):
                sc = scA if h % 2 == 0 else scB
                bsc = b_scA if h % 2 == 0 else b_scB
                OP("dve", lambda e, sc=sc, h=h: e.tensor_tensor(
                    out=smt[h % 2][:], in0=sc, in1=tri_sb[:], op=ALU.mult),
                   reads=[bsc, b_tri], writes=[b_smt[h % 2]])

            def em_rest(h):
                u, half = h // 2, h % 2
                base = 64 * half
                qsl = qt_fm[base:base + 64, 128 * u:128 * (u + 1)]
                OP("pe", lambda e, h=h: e.matmul(
                    out=P45[:, 64 * h:64 * (h + 1)], lhsT=smt[h % 2][:],
                    rhs=v_tm[:, 64 * h:64 * (h + 1)], start=True, stop=False),
                   reads=[b_smt[h % 2], b_vtm, b_po], writes=[b_po])
                OP("pe", lambda e, h=h, qsl=qsl, base=base: e.matmul(
                    out=P45[:, 64 * h:64 * (h + 1)], lhsT=qsl,
                    rhs=Sst[base:base + 64, 64 * h:64 * (h + 1)], start=False, stop=True),
                   reads=[b_qtfm, b_S, b_po], writes=[b_po])
                OP("pe", lambda e, h=h: e.matmul(
                    out=p6den[:, h:h + 1], lhsT=smt[h % 2][:], rhs=onesb_sb[:],
                    start=True, stop=False),
                   reads=[b_smt[h % 2], b_onesb, b_p6], writes=[b_p6])
                OP("pe", lambda e, h=h, qsl=qsl, base=base: e.matmul(
                    out=p6den[:, h:h + 1], lhsT=qsl, rhs=zst[base:base + 64, h:h + 1],
                    start=False, stop=True),
                   reads=[b_qtfm, b_z, b_p6], writes=[b_p6])
                OP("pe", lambda e, h=h: e.matmul(
                    out=P23[0:64, 64 * h:64 * (h + 1)],
                    lhsT=kt_tm[:, 64 * h:64 * (h + 1)],
                    rhs=v_tm[:, 64 * h:64 * (h + 1)], start=True, stop=True),
                   reads=[b_kttm[h // 2], b_vtm, b_kvS], writes=[b_kvS])
                OP("pe", lambda e, h=h: e.matmul(
                    out=p6kvz[0:64, h:h + 1], lhsT=kt_tm[:, 64 * h:64 * (h + 1)],
                    rhs=onesb_sb[:], start=True, stop=True),
                   reads=[b_kttm[h // 2], b_onesb, b_p6], writes=[b_p6])

            ln1_next = ln1_parts(g + 1) if g + 1 < nsub else None
            em_scores(0)
            for h in range(H):
                em_mask(h)
                if h + 1 < H:
                    em_scores(h + 1)
                em_rest(h)
                if h in (2, 5, 8, 11) and g + 1 < nsub:
                    ln1_next[(h - 2) // 3]()
            OP("dve", lambda e: e.tensor_scalar(out=den_sb[:], in0=p6den, scalar1=1e-6,
                                                scalar2=None, op0=ALU.add),
               reads=[b_p6], writes=[b_den])
            OP("dve", lambda e: e.reciprocal(out=rden[:], in_=den_sb[:]),
               reads=[b_den], writes=[b_rden], exotic=True)
            OP("act", lambda e: e.copy(out=kvbf[0:64, :], in_=P23[0:64, :]),
               reads=[b_kvS], writes=[b_kvbf])
            OP("act", lambda e: e.copy(out=zvbf[0:64, :], in_=p6kvz[0:64, :]),
               reads=[b_p6, b_den], writes=[b_zvbf])
            warm(P01[0:1, 0:1], [b_den], b_pq)
            if g + 1 < nsub:
                h1t(g + 1)
            warm(P01[0:1, 0:1], [b_rden], b_pq)
            warm(P01[0:1, 0:1], [b_kvbf], b_pq)
            OP("pool", lambda e: e.tensor_tensor(out=Sst[0:64, :], in0=Sst[0:64, :],
                                                 in1=kvbf[0:64, :], op=ALU.add),
               reads=[b_S, b_kvbf], writes=[b_S])
            elcrep = elc_bf[0:64, :].unsqueeze(2).to_broadcast([64, H, 64])
            OP("pool", lambda e, elcrep=elcrep: e.tensor_tensor(
                out=Sst[0:64, :].rearrange("p (h o) -> p h o", h=H),
                in0=Sst[0:64, :].rearrange("p (h o) -> p h o", h=H),
                in1=elcrep, op=ALU.mult),
               reads=[b_S, b_elcbf], writes=[b_S])
            OP("pool", lambda e: e.tensor_tensor(out=zst[0:64, :], in0=zst[0:64, :],
                                                 in1=zvbf[0:64, :], op=ALU.add),
               reads=[b_z, b_zvbf], writes=[b_z])
            OP("pool", lambda e: e.tensor_tensor(
                out=zst[0:64, :], in0=zst[0:64, :], in1=elc_bf[0:64, :], op=ALU.mult),
               reads=[b_z, b_elcbf], writes=[b_z])
            OP("pool", lambda e: e.tensor_copy(out=Sst[64:128, :], in_=Sst[0:64, :]),
               reads=[b_S], writes=[b_S])
            OP("pool", lambda e: e.tensor_copy(out=zst[64:128, :], in_=zst[0:64, :]),
               reads=[b_z], writes=[b_z])
            for u in (0, 4, 1, 5, 2, 6, 3, 7):
                for h in (2 * u, 2 * u + 1):
                    if h < 8:
                        OP("act", lambda e, h=h: e.activation(
                            out=ao_tm[:, 64 * h:64 * (h + 1)],
                            in_=P45[:, 64 * h:64 * (h + 1)],
                            func=AF.Copy, scale=rden[:, h:h + 1]),
                           reads=[b_po, b_rden], writes=[b_aotm[u]])
                    else:
                        OP("dve", lambda e, h=h: e.tensor_scalar(
                            out=ao_tm[:, 64 * h:64 * (h + 1)],
                            in0=P45[:, 64 * h:64 * (h + 1)],
                            scalar1=rden[:, h:h + 1], scalar2=None, op0=ALU.mult),
                           reads=[b_po, b_rden], writes=[b_aotm[u]])
                OP("pe", lambda e, u=u: e.transpose(
                    out=P7[:, 128 * u:128 * (u + 1)], in_=ao_tm[:, 128 * u:128 * (u + 1)],
                    identity=ident_sb[:]),
                   reads=[b_aotm[u], b_ident, b_p7], writes=[b_p7])
                if u in (1, 6):
                    warm(P01[0:1, 0:1], [b_aotm[u]], b_py)
            OP("act", lambda e: e.copy(out=ao_fm[:], in_=P7[:]), reads=[b_p7], writes=[b_aofm])
            proj(ao_fm[:], b_aofm, wo_bf, b_wo, P01, b_py, D)
            OP("dve", lambda e, xa=xa: e.tensor_tensor(out=xa, in0=P01[:], in1=xa, op=ALU.add),
               reads=[b_py, xb], writes=[xb])
            s = g % SPC
            ln_stats(xa, xb)
            ln_norm(xa, xb, h2_tm[:], b_h2tm)
            if g + 1 < nsub:
                h1_stts(g + 1)
                attn_front(g + 1)
            def h2t(s=s):
                for u in range(DCH):
                    OP("pe", lambda e, u=u, s=s: e.transpose(
                        out=P7[:, 128 * u:128 * (u + 1)],
                        in_=h2_tm[:, 128 * u:128 * (u + 1)], identity=ident_sb[:]),
                       reads=[b_h2tm, b_ident], writes=[b_p7])
                OP("act", lambda e, s=s: e.copy(out=h2_tm[:], in_=P7[:]),
                   reads=[b_p7], writes=[b_h2tm])
                for u in range(DCH):
                    sb_b = sbT[:, u:u + 1].to_broadcast([128, 128])
                    OP("dve", lambda e, u=u, s=s, sb_b=sb_b: e.scalar_tensor_tensor(
                        out=h2_fm[:, s * D + 128 * u:s * D + 128 * (u + 1)],
                        in0=h2_tm[:, 128 * u:128 * (u + 1)],
                        scalar=cbT[:, u:u + 1], in1=sb_b, op0=ALU.mult, op1=ALU.add),
                       reads=[b_h2tm, b_cb1, b_sbf], writes=[b_h2fm[s][u]])
            pe_backlog.append(h2t)

        piece_ctr = [0]
        fcvt = [0]

        def gate_comb(s):
            ga_, gb_, gc_, gd_, ge_, gf_ = (g8[:, i * E:(i + 1) * E] for i in range(6))
            OP("dve", lambda e: e.tensor_tensor(out=ga_, in0=p6g, in1=gateb_t[:], op=ALU.add),
               reads=[b_p6, b_gatebt], writes=[b_g8])
            OP("dve", lambda e: e.tensor_scalar(out=ga_, in0=ga_, scalar1=1e-4,
                                                scalar2=None, op0=ALU.max),
               reads=[b_g8], writes=[b_g8])
            OP("dve", lambda e: e.tensor_reduce(out=m1c[:, 0:1], in_=ga_, axis=AX.X,
                                                op=ALU.max),
               reads=[b_g8], writes=[b_m1c], exotic=True)
            OP("dve", lambda e: e.tensor_scalar(out=m1c[:, 1:2], in0=m1c[:, 0:1],
                                                scalar1=-1.0, scalar2=None, op0=ALU.mult),
               reads=[b_m1c], writes=[b_m1c])
            OP("act", lambda e: e.activation(out=gb_, in_=ga_, func=AF.Exp,
                                             bias=m1c[:, 1:2]),
               reads=[b_g8, b_m1c], writes=[b_g8])
            OP("dve", lambda e: e.tensor_reduce(out=m1c[:, 0:1], in_=gb_, axis=AX.X,
                                                op=ALU.add),
               reads=[b_g8], writes=[b_m1c], exotic=True)
            OP("dve", lambda e: e.reciprocal(out=m1c[:, 1:2], in_=m1c[:, 0:1]),
               reads=[b_m1c], writes=[b_m1c], exotic=True)
            OP("dve", lambda e: e.tensor_scalar(out=ga_, in0=gb_, scalar1=m1c[:, 1:2],
                                                scalar2=None, op0=ALU.mult),
               reads=[b_g8, b_m1c], writes=[b_g8])
            OP("dve", lambda e: e.tensor_reduce(out=mrep[:, 0:1], in_=ga_, axis=AX.X,
                                                op=ALU.max),
               reads=[b_g8], writes=[b_mrep], exotic=True)
            OP("dve", lambda e: e.match_replace(out=gb_, in_to_replace=mrep[:],
                                                in_values=ga_, imm_value=-1e30),
               reads=[b_g8, b_mrep], writes=[b_g8], exotic=True)
            OP("dve", lambda e: e.tensor_tensor(out=gc_, in0=ga_, in1=gb_, op=ALU.not_equal),
               reads=[b_g8], writes=[b_g8])
            OP("dve", lambda e: e.tensor_reduce(out=mrep[:, 0:1], in_=gb_, axis=AX.X,
                                                op=ALU.max),
               reads=[b_g8], writes=[b_mrep], exotic=True)
            OP("dve", lambda e: e.match_replace(out=gd_, in_to_replace=mrep[:],
                                                in_values=gb_, imm_value=-1e30),
               reads=[b_g8, b_mrep], writes=[b_g8], exotic=True)
            OP("dve", lambda e: e.tensor_tensor(out=ge_, in0=gb_, in1=gd_, op=ALU.not_equal),
               reads=[b_g8], writes=[b_g8])
            OP("dve", lambda e: e.tensor_tensor(out=gc_, in0=gc_, in1=ge_, op=ALU.add),
               reads=[b_g8], writes=[b_g8])
            OP("dve", lambda e: e.tensor_scalar(out=gf_, in0=ga_, scalar1=0.1,
                                                scalar2=None, op0=ALU.max),
               reads=[b_g8], writes=[b_g8])
            OP("dve", lambda e: e.tensor_tensor(out=gf_, in0=gf_, in1=gc_, op=ALU.mult),
               reads=[b_g8], writes=[b_g8])
            OP("dve", lambda e: e.tensor_reduce(out=m1c[:, 0:1], in_=gf_, axis=AX.X,
                                                op=ALU.add),
               reads=[b_g8], writes=[b_m1c], exotic=True)
            OP("dve", lambda e: e.reciprocal(out=m1c[:, 1:2], in_=m1c[:, 0:1]),
               reads=[b_m1c], writes=[b_m1c], exotic=True)
            OP("dve", lambda e, s=s: e.tensor_scalar(
                out=combc[:, s * E:(s + 1) * E], in0=gf_, scalar1=m1c[:, 1:2],
                scalar2=0.5, op0=ALU.mult, op1=ALU.mult),
               reads=[b_g8, b_m1c], writes=[b_comb])
            OP("dve", lambda e, s=s: e.tensor_copy(out=comb_bfx[:, s * E:(s + 1) * E],
                                                   in_=combc[:, s * E:(s + 1) * E]),
               reads=[b_comb], writes=[b_combbfx])

        def combT_biasmix(s, xa, xb):
            OP("pe", lambda e, s=s: e.transpose(out=P7[0:8, 0:128],
                                                in_=comb_bfx[:, s * E:(s + 1) * E],
                                                identity=ident_sb[:]),
               reads=[b_combbfx, b_ident, b_p7], writes=[b_p7])
            OP("act", lambda e, s=s: e.copy(out=combT_bf[:, s * 128:(s + 1) * 128],
                                            in_=P7[0:8, 0:128]),
               reads=[b_p7], writes=[b_combT])
            for hf in range(2):
                OP("pe", lambda e, hf=hf, s=s: e.matmul(
                    out=P45[:, 512 * hf:512 * (hf + 1)],
                    lhsT=combT_bf[:, s * 128:(s + 1) * 128],
                    rhs=expb_bf[:, 512 * hf:512 * (hf + 1)], start=True, stop=True),
                   reads=[b_combT, b_expbbf, b_pbm], writes=[b_pbm])
            OP("dve", lambda e, xa=xa: e.tensor_tensor(out=xa, in0=P45[:], in1=xa,
                                                       op=ALU.add),
               reads=[b_pbm, xb], writes=[xb])

        def ffn_chunk(c):
            g0 = c * SPC

            def gates(s):
                for j in range(DCH):
                    OP("pe", lambda e, j=j, s=s: e.matmul(
                        out=p6g, lhsT=h2_fm[:, s * D + 128 * j:s * D + 128 * (j + 1)],
                        rhs=gw_bf[:, j * E:(j + 1) * E],
                        start=(j == 0), stop=(j == DCH - 1)),
                       reads=[b_h2fm[s][j], b_gw, b_p6], writes=[b_p6])

            ptiles = (P01, P23, P45)
            pbufs = (b_fe0, b_fe1, b_fe2)

            def piece_dma(e_, q2):
                pc = piece_ctr[0]
                piece_ctr[0] += 1
                sl = pc % 4
                if c == 0:
                    for w_ in range(4):
                        j = 4 * q2 + w_
                        fsl = fcvt[0] % 4
                        fcvt[0] += 1
                        OP("qs", lambda e, e_=e_, j=j, fsl=fsl: e.dma_start(
                            out=wring[fsl][:], in_=dexpw[e_, :, j * D:(j + 1) * D]),
                           writes=[b_wring[fsl]], dma_buf=b_wring[fsl])
                        OP("dve", lambda e, sl=sl, w_=w_, fsl=fsl: e.tensor_tensor(
                            out=ering[sl][:, w_ * D:(w_ + 1) * D], in0=wring[fsl][:],
                            in1=gb_exp[:], op=ALU.mult),
                           reads=[b_wring[fsl], b_gbexp], writes=[b_ering[sl]])
                    OP("qg", lambda e, e_=e_, q2=q2, sl=sl: e.dma_start(
                        out=decache[e_, :, q2 * 4096:(q2 + 1) * 4096], in_=ering[sl][:]),
                       reads=[b_ering[sl]], writes=[b_ecache[e_][q2]],
                       dma_buf=b_ering[sl])
                else:
                    OP("qs", lambda e, e_=e_, q2=q2, sl=sl: e.dma_start(
                        out=ering[sl][:], in_=decache[e_, :, q2 * 4096:(q2 + 1) * 4096]),
                       reads=[b_ecache[e_][q2]], writes=[b_ering[sl]],
                       dma_buf=b_ering[sl])
                return sl

            def piece_mms(e_, q2, sl, subs, hfs=(0, 1)):
                for w_ in range(4):
                    j = 4 * q2 + w_
                    for s in subs:
                        pt = ptiles[(2 * e_ + s) % 3]
                        pb = pbufs[(2 * e_ + s) % 3]
                        for hf in hfs:
                            OP("pe", lambda e, j=j, s=s, w_=w_, hf=hf, sl=sl, pt=pt: e.matmul(
                                out=pt[:, 512 * hf:512 * (hf + 1)],
                                lhsT=h2_fm[:, s * D + 128 * j:s * D + 128 * (j + 1)],
                                rhs=ering[sl][:, w_ * D + 512 * hf:w_ * D + 512 * (hf + 1)],
                                start=(j == 0), stop=(j == DCH - 1)),
                               reads=[b_h2fm[s][j], b_ering[sl], pb], writes=[pb])

            def expert_mms(e_, sls, subs):
                for q2 in range(2):
                    piece_mms(e_, q2, sls[q2], subs)
                for s in subs:
                    g = g0 + s
                    xa = xr[g % 4][:]
                    xb = b_xr[g % 4]
                    pt = ptiles[(2 * e_ + s) % 3]
                    pb = pbufs[(2 * e_ + s) % 3]
                    OP("dve", lambda e, xa=xa, pt=pt, s=s, e_=e_: e.scalar_tensor_tensor(
                        out=xa, in0=pt[:], scalar=combc[:, s * E + e_:s * E + e_ + 1],
                        in1=xa, op0=ALU.mult, op1=ALU.add),
                       reads=[pb, b_comb, xb], writes=[xb])

            def stts(e_):
                pass

            gates(0)
            gate_comb(0)
            sls = [piece_dma(0, 0)]
            piece_mms(0, 0, sls[0], [0])
            warm(P23[0:1, 0:1], [b_stat2], b_fe1)
            warm(P23[0:1, 0:1], [b_h2tm], b_fe1)
            emit_backlog()
            gates(1)
            gate_comb(1)
            sls += [piece_dma(0, 1)]
            piece_mms(0, 0, sls[0], [1])
            piece_mms(0, 1, sls[1], [0, 1])
            for s in range(SPC):
                g = g0 + s
                combT_biasmix(s, xr[g % 4][:], b_xr[g % 4])
            for s in range(SPC):
                g = g0 + s
                xa = xr[g % 4][:]
                xb = b_xr[g % 4]
                pt = ptiles[s % 3]
                pb = pbufs[s % 3]
                OP("dve", lambda e, xa=xa, pt=pt, s=s: e.scalar_tensor_tensor(
                    out=xa, in0=pt[:], scalar=combc[:, s * E:s * E + 1],
                    in1=xa, op0=ALU.mult, op1=ALU.add),
                   reads=[pb, b_comb, xb], writes=[xb])
            for e_ in range(1, E):
                sls = [piece_dma(e_, q2) for q2 in range(2)]
                expert_mms(e_, sls, [0, 1])

        for c in range(nch):
            for s in range(SPC):
                attn_subchunk(c * SPC + s)
            ffn_chunk(c)
            for s in range(SPC):
                g = c * SPC + s
                OP("qg", lambda e, g=g: e.dma_start(
                    out=dy[g * 128:(g + 1) * 128, :], in_=xr[g % 4][:]),
                   reads=[b_xr[g % 4]], writes=[b_y], dma_buf=b_xr[g % 4])
            for s in range(SPC):
                g = (c + 2) * SPC + s
                if g < nsub:
                    OP("qg", lambda e, g=g: e.dma_start(
                        out=xr[g % 4][:], in_=dx[g * 128:(g + 1) * 128, :]),
                       writes=[b_xr[g % 4]], dma_buf=b_xr[g % 4])

        kb.final_wait("qs")

    es.close()
    return nc



def _wlayout(w):
    fdim = w.shape[1]
    return np.ascontiguousarray(
        w.reshape(8, 128, fdim).transpose(1, 0, 2).reshape(128, 8 * fdim))


def kernel(L0_attn_x, L0_attn_Wq, L0_attn_Wk, L0_attn_Wv, L0_attn_Wo, L0_attn_Wa,
           L0_attn_ba, L0_ffn_gate_W, L0_ffn_gate_b, L0_ffn_expert_W, L0_ffn_expert_b,
           L0_c, L0_wmod):
    n_tokens = L0_attn_x.shape[1]
    nc = build_nc(n_tokens)

    wq = _wlayout(np.asarray(L0_attn_Wq, np.float32))
    wk = _wlayout(np.asarray(L0_attn_Wk, np.float32))
    wv = _wlayout(np.asarray(L0_attn_Wv, np.float32))
    wo = _wlayout(np.asarray(L0_attn_Wo, np.float32))
    wa = _wlayout(np.asarray(L0_attn_Wa, np.float32))
    gw = _wlayout(np.asarray(L0_ffn_gate_W, np.float32))
    expw = np.stack([_wlayout(np.asarray(L0_ffn_expert_W[e], np.float32)) for e in range(E)])
    expb = np.ascontiguousarray(np.asarray(L0_ffn_expert_b, np.float32))
    wm = np.asarray(L0_wmod, np.float32).reshape(1024, 6, 1024)[:, [0, 1, 5, 2, 3, 4], :]
    wm = np.ascontiguousarray(
        wm.reshape(8, 128, 6, 1024).transpose(1, 2, 0, 3).reshape(128, 48 * 1024))
    ba_bc = np.ascontiguousarray(
        np.broadcast_to(np.asarray(L0_attn_ba, np.float32), (128, H)))
    gateb_bc = np.ascontiguousarray(
        np.broadcast_to(np.asarray(L0_ffn_gate_b, np.float32), (128, E)))
    tri = np.triu(np.ones((128, 128), dtype=np.float32))
    identbf = np.eye(128, dtype=np.float32).astype(ml_dtypes.bfloat16)
    onesbf = np.ones((128, 1), dtype=np.float32).astype(ml_dtypes.bfloat16)
    oneh127 = np.zeros((128, 1), dtype=np.float32)
    oneh127[127, 0] = 1.0
    ones1x64 = np.ones((1, 64), dtype=np.float32)
    ones1x128 = np.ones((1, 128), dtype=np.float32)
    cvec_h = np.zeros((128, 4), dtype=np.float32)
    cvec_h[:, 0] = 1e-5
    cvec_h[:, 1] = -0.5
    cvec_h[:, 2] = -1.0

    shared = dict(wq=wq, wk=wk, wv=wv, wo=wo, wa=wa, gw=gw, expw=expw, expb=expb,
                  wmod=wm, ba_bc=ba_bc, gateb_bc=gateb_bc, tri=tri, identbf=identbf,
                  onesbf=onesbf, oneh127=oneh127, ones1x64=ones1x64,
                  ones1x128=ones1x128, cvec=cvec_h)
    in_maps = []
    xf = np.asarray(L0_attn_x, np.float32)
    for b in range(8):
        m = dict(shared)
        m["x"] = np.ascontiguousarray(xf[b, :n_tokens])
        m["cmod"] = np.ascontiguousarray(
            np.asarray(L0_c[b], np.float32).reshape(8, 128).T)
        in_maps.append(m)

    res = run_bass_kernel_spmd(nc, in_maps, core_ids=list(range(8)))
    out = np.stack([r["y"] for r in res.results])
    return out.astype(np.float32)


# version 32
# speedup vs baseline: 1.0470x; 1.0419x over previous
import os
import numpy as np
import ml_dtypes
from contextlib import ExitStack
import concourse.bass as bass
import concourse.mybir as mybir
from concourse.bass_utils import run_bass_kernel_spmd

F32 = mybir.dt.float32
BF16 = mybir.dt.bfloat16
AF = mybir.ActivationFunctionType
ALU = mybir.AluOpType
AX = mybir.AxisListType

D = 1024
H = 16
DH = 64
E = 8
DCH = 8
SUB = 128
SPC = 2
T = SUB * SPC
STRICT_ENGINES = set(os.environ.get("KSTRICT", "dve,act,pool").split(",")) - {""}



class Buf:
    __slots__ = ("name", "last_w", "readers", "overlaps")

    def __init__(self, name):
        self.name = name
        self.last_w = None
        self.readers = []
        self.overlaps = []


def link_overlaps(*bufs):
    for b in bufs:
        for b2 in bufs:
            if b2 is not b and b2 not in b.overlaps:
                b.overlaps.append(b2)


class Eng:
    __slots__ = ("name", "obj", "sem", "dma", "n_ops", "incs", "cum", "known",
                 "last_inst", "insts")

    def __init__(self, name, obj, sem, dma):
        self.name = name
        self.obj = obj
        self.sem = sem
        self.dma = dma
        self.n_ops = 0
        self.incs = []
        self.cum = 0
        self.known = {}
        self.last_inst = None
        self.insts = []


class KB:
    def __init__(self, nc, es):
        self.nc = nc
        self.es = es
        self.engs = {}

    def eng(self, name, obj, dma=False):
        sem = self.es.enter_context(self.nc.semaphore(f"sem_{name}"))
        self.engs[name] = Eng(name, obj, sem, dma)
        return self.engs[name]

    def _value_for(self, peng, idx):
        lo = None
        for oi, val in reversed(peng.incs):
            if oi >= idx:
                lo = val
            else:
                break
        if lo is not None:
            return lo
        assert not peng.dma, "dma producers always have eager incs"
        j = min(idx + 1, peng.n_ops - 1)
        peng.cum += 1
        peng.insts[j].then_inc(peng.sem, 1)
        peng.incs.append((j, peng.cum))
        return peng.cum

    def buf_sem(self, buf, qname):
        pname = f"dq_{qname}_{buf.name}"
        if pname not in self.engs:
            sem = self.es.enter_context(self.nc.semaphore(pname))
            pe = Eng(pname, None, sem, True)
            self.engs[pname] = pe
        return pname

    def op(self, ename, fn, reads=(), writes=(), dma_buf=None, exotic=False):
        eng = self.engs[ename]
        assert (dma_buf is None) or eng.dma
        waits = {}

        def need(pname, idx, pstrict=False):
            if (pname == ename and not eng.dma and not (pstrict or exotic)
                    and ename not in STRICT_ENGINES):
                return
            if pname == ename and (eng.dma or ename == "pe"):
                return
            if waits.get(pname, -1) < self._tok_val(pname, idx):
                waits[pname] = self._tok_val(pname, idx)

        seen = set()
        for b in reads:
            for bb in (b, *b.overlaps):
                if id(bb) in seen:
                    continue
                if bb.last_w:
                    need(*bb.last_w)
            seen.add(id(b))
        for b in writes:
            for bb in (b, *b.overlaps):
                if bb.last_w:
                    need(*bb.last_w)
                for r in bb.readers:
                    need(*r)
        for pname, val in waits.items():
            if eng.known.get(pname, -1) < val:
                eng.obj.wait_ge(self.engs[pname].sem, val)
                eng.known[pname] = val
        inst = fn(eng.obj)
        eng.n_ops += 1
        eng.last_inst = inst
        eng.insts.append(inst)
        if eng.dma:
            assert dma_buf is not None, "DMA ops must name their slot buffer"
            pname = self.buf_sem(dma_buf, ename)
            peng = self.engs[pname]
            peng.cum += 16
            inst.then_inc(peng.sem, 16)
            tok = (pname, peng.cum, False)
        else:
            tok = (ename, eng.n_ops - 1, exotic)
        for b in writes:
            b.last_w = tok
            b.readers = []
        for b in reads:
            b.readers.append(tok)
        return inst

    def _tok_val(self, pname, idx):
        peng = self.engs[pname]
        if peng.dma:
            return idx
        return self._value_for(peng, idx)

    def final_wait(self, ename):
        eng = self.engs[ename]
        for pname, peng in self.engs.items():
            if pname == ename or (peng.n_ops == 0 and not peng.dma):
                continue
            if peng.dma:
                if peng.cum == 0:
                    continue
                val = peng.cum
            else:
                val = self._value_for(peng, peng.n_ops - 1)
            if eng.known.get(pname, -1) < val:
                eng.obj.wait_ge(peng.sem, val)
                eng.known[pname] = val



def build_nc(n_tokens=8192, debug=0):
    assert n_tokens % T == 0
    nch = n_tokens // T
    nsub = n_tokens // SUB

    nc = bass.Bass("TRN2", target_bir_lowering=False)

    dx = nc.dram_tensor("x", [n_tokens, D], F32, kind="ExternalInput").ap()
    dy = nc.dram_tensor("y", [n_tokens, D], F32, kind="ExternalOutput").ap()
    dwq = nc.dram_tensor("wq", [128, 8 * D], F32, kind="ExternalInput").ap()
    dwk = nc.dram_tensor("wk", [128, 8 * D], F32, kind="ExternalInput").ap()
    dwv = nc.dram_tensor("wv", [128, 8 * D], F32, kind="ExternalInput").ap()
    dwo = nc.dram_tensor("wo", [128, 8 * D], F32, kind="ExternalInput").ap()
    dwa = nc.dram_tensor("wa", [128, 8 * H], F32, kind="ExternalInput").ap()
    dgw = nc.dram_tensor("gw", [128, 8 * E], F32, kind="ExternalInput").ap()
    dexpw = nc.dram_tensor("expw", [E, 128, 8 * D], F32, kind="ExternalInput").ap()
    dexpb = nc.dram_tensor("expb", [E, D], F32, kind="ExternalInput").ap()
    dwmod = nc.dram_tensor("wmod", [128, 48 * D], F32, kind="ExternalInput").ap()
    dc = nc.dram_tensor("cmod", [128, 8], F32, kind="ExternalInput").ap()
    dba = nc.dram_tensor("ba_bc", [128, H], F32, kind="ExternalInput").ap()
    dgb = nc.dram_tensor("gateb_bc", [128, E], F32, kind="ExternalInput").ap()
    dtri = nc.dram_tensor("tri", [128, 128], F32, kind="ExternalInput").ap()
    dident = nc.dram_tensor("identbf", [128, 128], BF16, kind="ExternalInput").ap()
    donesb = nc.dram_tensor("onesbf", [128, 1], BF16, kind="ExternalInput").ap()
    doneh = nc.dram_tensor("oneh127", [128, 1], F32, kind="ExternalInput").ap()
    dones64 = nc.dram_tensor("ones1x64", [1, 64], F32, kind="ExternalInput").ap()
    dones128 = nc.dram_tensor("ones1x128", [1, 128], F32, kind="ExternalInput").ap()
    dcvec = nc.dram_tensor("cvec", [128, 4], F32, kind="ExternalInput").ap()
    decache = nc.dram_tensor("ecache", [E, 128, 8 * D], BF16, kind="Internal").ap()
    if debug:
        ddbg = nc.dram_tensor("dbg", [debug, 128, D], F32, kind="ExternalOutput").ap()

    es = ExitStack()
    kb = KB(nc, es)
    S = lambda name, shape, dt: es.enter_context(nc.sbuf_tensor(name, shape, dt))
    P = lambda name, shape, dt: es.enter_context(nc.psum_tensor(name, shape, dt))

    xr = [S(f"xr{i}", [128, D], F32) for i in range(4)]
    wq_bf = S("wq_bf", [128, 8 * D], BF16)
    wk_bf = S("wk_bf", [128, 8 * D], BF16)
    wv_bf = S("wv_bf", [128, 8 * D], BF16)
    wo_bf = S("wo_bf", [128, 8 * D], BF16)
    wa_bf = S("wa_bf", [128, 8 * H], BF16)
    gw_bf = S("gw_bf", [128, 8 * E], BF16)
    expb_f = S("expb_f", [E, D], F32)
    expb_bf = S("expb_bf", [E, D], BF16)
    ering = [S(f"ering{i}", [128, 4096], BF16) for i in range(4)]
    wring = [S(f"wring{i}", [128, D], F32) for i in range(4)]
    sf1 = S("sf1", [128, D], F32)
    s1 = S("s1", [128, D], BF16)
    s2 = S("s2", [128, D], BF16)
    h1_bf = S("h1_bf", [128, D], BF16)
    h1_fm = S("h1_fm", [128, D], BF16)
    qt_tm = S("qt_tm", [128, D], BF16)
    kt_tm = S("kt_tm", [128, D], BF16)
    v_tm = S("v_tm", [128, D], BF16)
    qt_fm = S("qt_fm", [128, D], BF16)
    kt_fm = S("kt_fm", [128, D], BF16)
    smt = [S(f"smt{i}", [128, 128], BF16) for i in range(2)]
    ao_tm = S("ao_tm", [128, D], BF16)
    ao_fm = S("ao_fm", [128, D], BF16)
    h2_tm = S("h2_tm", [128, D], BF16)
    h2_fm = S("h2_fm", [128, SPC * D], BF16)
    Sst = S("Sst", [128, D], BF16)
    zst = S("zst", [128, H], BF16)
    kvbf = S("kvbf", [128, D], BF16)
    zvbf = S("zvbf", [128, H], BF16)
    u16 = S("u16", [128, H], F32)
    t16a = S("t16a", [128, H], F32)
    t16b = S("t16b", [128, H], F32)
    lna = S("lna", [128, H], F32)
    el = S("el", [128, H], F32)
    enl = S("enl", [128, H], F32)
    el_bf = S("el_bf", [128, H], BF16)
    enl_bf = S("enl_bf", [128, H], BF16)
    elcrow = S("elcrow", [1, H], F32)
    elc_bf = S("elc_bf", [128, H], BF16)
    den_sb = S("den_sb", [128, H], F32)
    rden = S("rden", [128, H], F32)
    stat = S("stat", [128, 16], F32)
    stat2 = S("stat2", [128, 4], F32)
    snap = S("snap", [128, 8], F32)
    g8 = S("g8", [128, 6 * E], F32)
    mrep = S("mrep", [128, 8], F32)
    m1c = S("m1c", [128, 4], F32)
    combc = S("combc", [128, SPC * E], F32)
    comb_bfx = S("comb_bfx", [128, SPC * E], BF16)
    combT_bf = S("combT_bf", [8, SPC * 128], BF16)
    c_sil = S("c_sil", [128, 8], F32)
    c_t8 = S("c_t8", [128, 16], F32)
    rowtmp = S("rowtmp", [1, D], F32)
    caT = S("caT", [128, DCH], F32)
    cbT = S("cbT", [128, DCH], F32)
    saT = S("saT", [128, DCH], F32)
    sbT = S("sbT", [128, DCH], F32)
    ga_exp = S("ga_exp", [128, D], F32)
    gb_exp = S("gb_exp", [128, D], F32)
    babc2 = S("babc2", [128, H], F32)
    gateb_t = S("gateb_t", [128, E], F32)
    tri_sb = S("tri_sb", [128, 128], F32)
    ident_sb = S("ident_sb", [128, 128], BF16)
    onesb_sb = S("onesb_sb", [128, 1], BF16)
    oneh_sb = S("oneh_sb", [128, 1], F32)
    ones64_sb = S("ones64_sb", [1, 64], F32)
    ones128_sb = S("ones128_sb", [1, 128], F32)
    cvec = S("cvec_sb", [128, 4], F32)

    P01 = P("P01", [128, D], F32)
    P23 = P("P23", [128, D], F32)
    P45 = P("P45", [128, D], F32)
    P6 = P("P6", [128, 512], F32)
    P7 = P("P7", [128, D], BF16)

    B = lambda n: Buf(n)
    b_xr = [B(f"xr{i}") for i in range(4)]
    b_wq, b_wk, b_wv, b_wo = B("wq"), B("wk"), B("wv"), B("wo")
    b_wa, b_gw = B("wa"), B("gw")
    b_expbf, b_expbbf = B("expb_f"), B("expb_bf")
    b_ering = [B(f"ering{i}") for i in range(4)]
    b_wring = [B(f"wring{i}") for i in range(4)]
    b_sf1, b_s1, b_s2 = B("sf1"), B("s1"), B("s2")
    b_h1bf = B("h1_bf")
    b_h1fm = [B(f"h1_fm{u}") for u in range(DCH)]
    b_qttm = [B(f"qt_tm{u}") for u in range(DCH)]
    b_kttm = [B(f"kt_tm{u}") for u in range(DCH)]
    b_vtm = B("v_tm")
    b_qtfm, b_ktfm = B("qt_fm"), B("kt_fm")
    b_smt = [B("smt0"), B("smt1")]
    b_aotm = [B(f"ao_tm{u}") for u in range(DCH)]
    b_aofm = B("ao_fm")
    b_h2tm = B("h2_tm")
    b_h2fm = [[B(f"h2_fm{s}_{u}") for u in range(DCH)] for s in range(SPC)]
    b_S, b_z = B("Sst"), B("zst")
    b_kvbf, b_zvbf = B("kvbf"), B("zvbf")
    b_u16, b_t16a, b_t16b, b_lna = B("u16"), B("t16a"), B("t16b"), B("lna")
    b_el, b_enl, b_elcrow, b_elcbf = B("el"), B("enl"), B("elcrow"), B("elc_bf")
    b_elbf, b_enlbf = B("el_bf"), B("enl_bf")
    b_den, b_rden, b_stat = B("den"), B("rden"), B("stat")
    b_stat2 = B("stat2")
    b_snap = B("snap")
    b_g8, b_mrep, b_m1c = B("g8"), B("mrep"), B("m1c")
    b_comb, b_combbfx, b_combT = B("combc"), B("comb_bfx"), B("combT_bf")
    b_csil, b_ct8, b_rowtmp = B("c_sil"), B("c_t8"), B("rowtmp")
    b_ca1, b_cb1, b_saf, b_sbf = B("ca1"), B("cb1"), B("sa_f"), B("sb_f")
    b_gaexp, b_gbexp = B("ga_exp"), B("gb_exp")
    b_babc2, b_gatebt = B("babc2"), B("gateb_t")
    b_tri, b_ident, b_onesb, b_oneh, b_ones64 = B("tri"), B("ident"), B("onesb"), B("oneh"), B("ones64")
    b_ones128 = B("ones128")
    b_cvec = B("cvec")
    b_ecache = [[B(f"ec{e}_{q}") for q in range(2)] for e in range(E)]
    b_y = B("y")

    b_pq, b_scA, b_scB, b_py, b_fe0 = B("pq"), B("scA"), B("scB"), B("py"), B("fe0")
    link_overlaps(b_pq, b_py, b_fe0)
    for bb in (b_pq, b_py, b_fe0):
        link_overlaps(bb, b_scA)
        link_overlaps(bb, b_scB)
    b_pk, b_kvS, b_fe1 = B("pk"), B("kvS"), B("fe1")
    link_overlaps(b_pk, b_kvS, b_fe1)
    b_pv, b_po, b_pbm, b_fe2, b_pmod = B("pv"), B("po"), B("pbm"), B("fe2"), B("pmod")
    link_overlaps(b_pv, b_po, b_pbm, b_fe2, b_pmod)
    b_p6 = B("p6")
    b_p7 = B("p7")
    b_modrow = B("modrow")
    link_overlaps(b_modrow, b_sf1)

    scA = P01[:, 0:128]
    scB = P01[:, 512:640]
    p6a = P6[:, 0:16]
    p6L = P6[:, 16:32]
    p6den = P6[:, 32:48]
    p6kvz = P6[:, 48:64]
    p6er = P6[0:1, 64:80]
    p6eb = P6[:, 80:96]
    p6g = P6[:, 96:104]
    p6t = P6[:, 104:112]

    with nc.Block():
        pe = kb.eng("pe", nc.tensor)
        dve = kb.eng("dve", nc.vector)
        act = kb.eng("act", nc.scalar)
        qs = kb.eng("qs", nc.sync, dma=True)
        qg = kb.eng("qg", nc.gpsimd, dma=True)
        pool = kb.eng("pool", nc.gpsimd, dma=False)
        qa = kb.eng("qa", nc.scalar, dma=True)

        OP = kb.op

        def warm(target_ap, rbufs, wbuf):
            OP("pe", lambda e: e.matmul(out=target_ap, lhsT=onesb_sb[:], rhs=onesb_sb[:],
                                        start=True, stop=True),
               reads=list(rbufs), writes=[wbuf])

        nc.scalar.add_instruction(mybir.InstLoadActFuncSet(
            name=nc.get_next_instruction_name(), act_func_set_id=6, ins=[], outs=[]))
        dbg_i = [0]
        dbg_f32 = S("dbg_f32", [128, D], F32) if debug else None
        b_dbgf = B("dbgf")
        b_dbgq = B("dbgq")

        def dump(ap, rbufs, ncols=D, cast=False, nparts=128):
            if not debug or dbg_i[0] >= debug:
                return
            i = dbg_i[0]
            dbg_i[0] += 1
            if cast:
                OP("act", lambda e: e.copy(out=dbg_f32[:, 0:ncols], in_=ap),
                   reads=list(rbufs), writes=[b_dbgf])
                OP("qg", lambda e, i=i: e.dma_start(out=ddbg[i, :, 0:ncols],
                                                    in_=dbg_f32[:, 0:ncols]),
                   reads=[b_dbgf], writes=[b_dbgq], dma_buf=b_dbgq)
            else:
                OP("qg", lambda e, i=i: e.dma_start(out=ddbg[i, 0:nparts, 0:ncols], in_=ap),
                   reads=list(rbufs) + [b_dbgq], writes=[b_dbgq], dma_buf=b_dbgq)
        kb.dump = dump

        OP("qs", lambda e: e.dma_start(out=tri_sb[:], in_=dtri), writes=[b_tri], dma_buf=b_tri)
        OP("qs", lambda e: e.dma_start(out=ident_sb[:], in_=dident), writes=[b_ident], dma_buf=b_ident)
        OP("qs", lambda e: e.dma_start(out=onesb_sb[:], in_=donesb), writes=[b_onesb], dma_buf=b_onesb)
        OP("qs", lambda e: e.dma_start(out=oneh_sb[:], in_=doneh), writes=[b_oneh], dma_buf=b_oneh)
        OP("qs", lambda e: e.dma_start(out=ones64_sb[:], in_=dones64), writes=[b_ones64], dma_buf=b_ones64)
        OP("qs", lambda e: e.dma_start(out=ones128_sb[:], in_=dones128), writes=[b_ones128], dma_buf=b_ones128)
        OP("qs", lambda e: e.dma_start(out=cvec[:], in_=dcvec), writes=[b_cvec], dma_buf=b_cvec)
        C_EPS, C_NHALF, C_NEG1 = cvec[:, 0:1], cvec[:, 1:2], cvec[:, 2:3]
        OP("qs", lambda e: e.dma_start(out=c_sil[:], in_=dc), writes=[b_csil], dma_buf=b_csil)
        OP("qs", lambda e: e.dma_start(out=babc2[:], in_=dba), writes=[b_babc2], dma_buf=b_babc2)
        OP("qs", lambda e: e.dma_start(out=gateb_t[:], in_=dgb), writes=[b_gatebt], dma_buf=b_gatebt)
        OP("qs", lambda e: e.dma_start(out=expb_f[:], in_=dexpb), writes=[b_expbf], dma_buf=b_expbf)
        for g in range(min(4, nsub)):
            OP("qg", lambda e, g=g: e.dma_start(out=xr[g][:], in_=dx[g * 128:(g + 1) * 128, :]),
               writes=[b_xr[g]], dma_buf=b_xr[g])

        OP("dve", lambda e: e.memset(Sst[:], 0.0), writes=[b_S])
        OP("dve", lambda e: e.memset(zst[:], 0.0), writes=[b_z])
        OP("dve", lambda e: e.memset(mrep[:], -1e30), writes=[b_mrep])
        OP("dve", lambda e: e.tensor_scalar(out=babc2[:], in0=babc2[:], scalar1=2.0,
                                            scalar2=None, op0=ALU.add),
           reads=[b_babc2], writes=[b_babc2])
        OP("act", lambda e: e.activation(out=c_t8[:, 0:8], in_=c_sil[:], func=AF.Exp,
                                         scale=C_NEG1),
           reads=[b_csil, b_cvec], writes=[b_ct8])
        OP("dve", lambda e: e.tensor_scalar(out=c_t8[:, 0:8], in0=c_t8[:, 0:8], scalar1=1.0,
                                            scalar2=None, op0=ALU.add),
           reads=[b_ct8], writes=[b_ct8])
        OP("dve", lambda e: e.reciprocal(out=c_t8[:, 8:16], in_=c_t8[:, 0:8]),
           reads=[b_ct8], writes=[b_ct8], exotic=True)
        OP("dve", lambda e: e.tensor_tensor(out=c_sil[:], in0=c_sil[:], in1=c_t8[:, 8:16],
                                            op=ALU.mult),
           reads=[b_csil, b_ct8], writes=[b_csil])

        modrow = sf1[0:1, 0:D]
        wslot = 0

        def stream_piece(dram_ap, ncols=D):
            nonlocal wslot
            sl = wslot % 4
            wslot += 1
            OP("qs", lambda e: e.dma_start(out=wring[sl][:, 0:ncols], in_=dram_ap),
               writes=[b_wring[sl]], dma_buf=b_wring[sl])
            return sl

        def mod_block(m):
            for j in range(DCH):
                sl = stream_piece(dwmod[:, (m * 8 + j) * D:(m * 8 + j + 1) * D])
                for hf in range(2):
                    OP("pe", lambda e, j=j, hf=hf, sl=sl: e.matmul(
                        out=P45[0:1, 512 * hf:512 * (hf + 1)],
                        lhsT=c_sil[:, j:j + 1], rhs=wring[sl][:, 512 * hf:512 * (hf + 1)],
                        start=(j == 0), stop=(j == DCH - 1)),
                       reads=[b_csil, b_wring[sl], b_pmod], writes=[b_pmod])
            OP("act", lambda e: e.copy(out=modrow, in_=P45[0:1, 0:D]),
               reads=[b_pmod], writes=[b_modrow])

        def bcast_row(dst_ap, dstbuf, bias=None, dtype_note=None):
            for hf in range(2):
                OP("pe", lambda e, hf=hf: e.matmul(
                    out=P23[:, 512 * hf:512 * (hf + 1)],
                    lhsT=ones128_sb[:],
                    rhs=modrow[:, 512 * hf:512 * (hf + 1)],
                    start=True, stop=True),
                   reads=[b_ones128, b_modrow, b_pk], writes=[b_pk])
            if bias is None:
                OP("act", lambda e: e.copy(out=dst_ap, in_=P23[:, 0:D]),
                   reads=[b_pk], writes=[dstbuf])
            else:
                OP("act", lambda e: e.activation(out=dst_ap, in_=P23[:, 0:D], func=AF.Copy,
                                                 bias=bias),
                   reads=[b_pk], writes=[dstbuf])

        def colT(dst_ap, dstbuf, bias=None):
            for b2 in range(DCH):
                OP("pe", lambda e, b2=b2: e.matmul(
                    out=p6t[:, b2:b2 + 1], lhsT=modrow[:, 128 * b2:128 * (b2 + 1)],
                    rhs=ones128_sb[0:1, 0:1], start=True, stop=True),
                   reads=[b_modrow, b_ones128, b_p6], writes=[b_p6])
            if bias is None:
                OP("act", lambda e: e.copy(out=dst_ap, in_=p6t[:]),
                   reads=[b_p6], writes=[dstbuf])
            else:
                OP("act", lambda e: e.activation(out=dst_ap, in_=p6t[:], func=AF.Copy,
                                                 bias=bias),
                   reads=[b_p6], writes=[dstbuf])

        def gate_transform():
            OP("act", lambda e: e.activation(out=rowtmp[:], in_=modrow, func=AF.Sign),
               reads=[b_modrow], writes=[b_rowtmp])
            OP("act", lambda e: e.activation(out=modrow, in_=modrow, func=AF.Abs),
               reads=[b_modrow], writes=[b_modrow])
            OP("dve", lambda e: e.tensor_scalar(out=modrow, in0=modrow, scalar1=0.25,
                                                scalar2=None, op0=ALU.max),
               reads=[b_modrow], writes=[b_modrow])
            OP("dve", lambda e: e.tensor_tensor(out=modrow, in0=modrow, in1=rowtmp[:], op=ALU.mult),
               reads=[b_modrow, b_rowtmp], writes=[b_modrow])

        mod_block(0)
        kb.dump(c_sil[:], [b_csil], ncols=8)
        kb.dump(modrow, [b_modrow], nparts=1)
        colT(saT[:], b_saf)
        mod_block(1)
        colT(caT[:], b_ca1, bias=1.0)
        mod_block(2)
        gate_transform()
        bcast_row(gb_exp[:], b_gbexp)
        mod_block(3)
        gate_transform()
        bcast_row(ga_exp[:], b_gaexp)
        mod_block(4)
        colT(sbT[:], b_sbf)
        mod_block(5)
        colT(cbT[:], b_cb1, bias=1.0)
        kb.dump(saT[:], [b_saf], ncols=DCH)
        kb.dump(caT[:], [b_ca1], ncols=DCH)
        kb.dump(gb_exp[:], [b_gbexp])
        kb.dump(ga_exp[:], [b_gaexp])
        kb.dump(sbT[:], [b_sbf], ncols=DCH)
        kb.dump(cbT[:], [b_cb1], ncols=DCH)

        def load_weight(dram, dst, dstbuf, ncols, fold=None, foldbuf=None):
            for j in range(ncols // D):
                sl = stream_piece(dram[:, j * D:(j + 1) * D])
                if fold is None:
                    OP("act", lambda e, j=j, sl=sl: e.copy(out=dst[:, j * D:(j + 1) * D],
                                                           in_=wring[sl][:]),
                       reads=[b_wring[sl]], writes=[dstbuf])
                else:
                    OP("dve", lambda e, j=j, sl=sl: e.tensor_tensor(
                        out=dst[:, j * D:(j + 1) * D], in0=wring[sl][:], in1=fold, op=ALU.mult),
                       reads=[b_wring[sl], foldbuf], writes=[dstbuf])

        load_weight(dwq, wq_bf, b_wq, 8 * D)
        load_weight(dwk, wk_bf, b_wk, 8 * D)
        load_weight(dwv, wv_bf, b_wv, 8 * D)
        sl = stream_piece(dwa, ncols=8 * H)
        OP("act", lambda e, sl=sl: e.copy(out=wa_bf[:], in_=wring[sl][:, 0:8 * H]),
           reads=[b_wring[sl]], writes=[b_wa])
        sl = stream_piece(dgw, ncols=8 * E)
        OP("act", lambda e, sl=sl: e.copy(out=gw_bf[:], in_=wring[sl][:, 0:8 * E]),
           reads=[b_wring[sl]], writes=[b_gw])
        load_weight(dwo, wo_bf, b_wo, 8 * D, fold=ga_exp[:], foldbuf=b_gaexp)
        OP("dve", lambda e: e.tensor_tensor(out=expb_bf[:], in0=expb_f[:], in1=gb_exp[0:E, :],
                                            op=ALU.mult),
           reads=[b_expbf, b_gbexp], writes=[b_expbbf])

        tr_cnt = [0]

        def transposes(src_ap, srcbufs, dst_ap, dstbuf):
            blk = srcbufs if isinstance(srcbufs, list) else [srcbufs] * DCH
            for u in range(DCH):
                OP("pe", lambda e, u=u: e.transpose(out=P7[:, 128 * u:128 * (u + 1)],
                                                    in_=src_ap[:, 128 * u:128 * (u + 1)],
                                                    identity=ident_sb[:]),
                   reads=[blk[u], b_ident], writes=[b_p7])
            OP("act", lambda e: e.copy(out=dst_ap, in_=P7[:]), reads=[b_p7], writes=[dstbuf])

        def proj(h_fm_ap, h_fm_buf, w_sb, w_buf, pdst, pbuf, nout):
            nhalf = max(1, nout // 512)
            step = min(nout, 512)
            blk = h_fm_buf if isinstance(h_fm_buf, list) else [h_fm_buf] * DCH
            for j in range(DCH):
                for hf in range(nhalf):
                    OP("pe", lambda e, j=j, hf=hf: e.matmul(
                        out=pdst[:, hf * step:(hf + 1) * step],
                        lhsT=h_fm_ap[:, 128 * j:128 * (j + 1)],
                        rhs=w_sb[:, j * nout + hf * step: j * nout + (hf + 1) * step],
                        start=(j == 0), stop=(j == DCH - 1)),
                       reads=[blk[j], w_buf, pbuf], writes=[pbuf])

        def ln_stats(x_ap, xbuf):
            OP("dve", lambda e: e.bn_stats(out=stat[:, 0:6], in_=x_ap[:, 0:512]),
               reads=[xbuf], writes=[b_stat], exotic=True)
            OP("dve", lambda e: e.bn_stats(out=stat[:, 6:12], in_=x_ap[:, 512:1024]),
               reads=[xbuf], writes=[b_stat], exotic=True)
            OP("dve", lambda e: e.bn_aggr(out=stat[:, 12:14], in_=stat[:, 0:12]),
               reads=[b_stat], writes=[b_stat], exotic=True)
            OP("act", lambda e: e.activation(out=stat2[:, 0:1], in_=stat[:, 13:14], func=AF.Ln,
                                             bias=C_EPS),
               reads=[b_stat, b_cvec], writes=[b_stat2])
            OP("act", lambda e: e.activation(out=stat2[:, 1:2], in_=stat2[:, 0:1], func=AF.Exp,
                                             scale=C_NHALF),
               reads=[b_stat2, b_cvec], writes=[b_stat2])

        def ln_norm(x_ap, xbuf, dst_tm, dstbuf):
            OP("dve", lambda e: e.tensor_scalar(out=dst_tm, in0=x_ap,
                                                scalar1=stat[:, 12:13],
                                                scalar2=stat2[:, 1:2],
                                                op0=ALU.subtract, op1=ALU.mult),
               reads=[xbuf, b_stat, b_stat2], writes=[dstbuf])

        def elu_scale(psrc, psbuf, elbf_ap, elbf_buf, dst, dstbufs, on_dve=False):
            if on_dve:
                OP("dve", lambda e: e.tensor_scalar(out=s1[:], in0=psrc, scalar1=0.0,
                                                    scalar2=None, op0=ALU.min),
                   reads=[psbuf], writes=[b_s1])
                OP("act", lambda e: e.activation(out=s2[:], in_=s1[:], func=AF.Exp),
                   reads=[b_s1], writes=[b_s2])
                OP("dve", lambda e: e.tensor_scalar(out=s1[:], in0=psrc, scalar1=0.0,
                                                    scalar2=None, op0=ALU.max),
                   reads=[psbuf], writes=[b_s1])
            else:
                OP("act", lambda e: e.activation(out=s1[:], in_=psrc, func=AF.Relu,
                                                 scale=C_NEG1),
                   reads=[psbuf, b_cvec], writes=[b_s1])
                OP("act", lambda e: e.activation(out=s2[:], in_=s1[:], func=AF.Exp,
                                                 scale=C_NEG1),
                   reads=[b_s1, b_cvec], writes=[b_s2])
                OP("act", lambda e: e.activation(out=s1[:], in_=psrc, func=AF.Relu),
                   reads=[psbuf], writes=[b_s1])
            OP("dve", lambda e: e.tensor_tensor(out=dst, in0=s2[:], in1=s1[:], op=ALU.add),
               reads=[b_s1, b_s2], writes=list(dstbufs))
            rep = elbf_ap.unsqueeze(2).to_broadcast([128, H, 64])
            d3 = dst.rearrange("p (h o) -> p h o", h=H)
            OP("dve", lambda e, rep=rep, d3=d3: e.tensor_tensor(out=d3, in0=d3, in1=rep,
                                                                op=ALU.mult),
               reads=list(dstbufs) + [elbf_buf], writes=list(dstbufs))

        ln1_done = set()
        h1t_done = set()
        stts_done = set()
        pe_backlog = []

        def emit_backlog():
            for f in pe_backlog:
                f()
            pe_backlog.clear()

        def ln1_parts(g):
            xa = xr[g % 4][:]
            xb = b_xr[g % 4]

            def p1():
                OP("dve", lambda e: e.bn_stats(out=stat[:, 0:6], in_=xa[:, 0:512]),
                   reads=[xb], writes=[b_stat], exotic=True)

            def p2():
                OP("dve", lambda e: e.bn_stats(out=stat[:, 6:12], in_=xa[:, 512:1024]),
                   reads=[xb], writes=[b_stat], exotic=True)

            def p3():
                OP("dve", lambda e: e.bn_aggr(out=stat[:, 12:14], in_=stat[:, 0:12]),
                   reads=[b_stat], writes=[b_stat], exotic=True)
                OP("act", lambda e: e.activation(out=stat2[:, 0:1], in_=stat[:, 13:14],
                                                 func=AF.Ln, bias=C_EPS),
                   reads=[b_stat, b_cvec], writes=[b_stat2])
                OP("act", lambda e: e.activation(out=stat2[:, 1:2], in_=stat2[:, 0:1],
                                                 func=AF.Exp, scale=C_NHALF),
                   reads=[b_stat2, b_cvec], writes=[b_stat2])

            def p4():
                ln_norm(xa, xb, h1_bf[:], b_h1bf)
                ln1_done.add(g)

            return [p1, p2, p3, p4]

        def ln1(g):
            for p in ln1_parts(g):
                p()

        def h1t(g):
            for u in range(DCH):
                OP("pe", lambda e, u=u: e.transpose(out=P7[:, 128 * u:128 * (u + 1)],
                                                    in_=h1_bf[:, 128 * u:128 * (u + 1)],
                                                    identity=ident_sb[:]),
                   reads=[b_h1bf, b_ident], writes=[b_p7])
            OP("dve", lambda e: e.tensor_copy(out=h1_bf[:], in_=P7[:]),
               reads=[b_p7], writes=[b_h1bf])
            h1t_done.add(g)

        def h1_stts(g):
            for u in range(DCH):
                sa_b = saT[:, u:u + 1].to_broadcast([128, 128])
                OP("dve", lambda e, u=u, sa_b=sa_b: e.scalar_tensor_tensor(
                    out=h1_fm[:, 128 * u:128 * (u + 1)],
                    in0=h1_bf[:, 128 * u:128 * (u + 1)],
                    scalar=caT[:, u:u + 1], in1=sa_b, op0=ALU.mult, op1=ALU.add),
                   reads=[b_h1bf, b_ca1, b_saf], writes=[b_h1fm[u]])
            stts_done.add(g)

        front_done = set()

        def attn_front(g):
            for j in range(DCH):
                OP("pe", lambda e, j=j: e.matmul(
                    out=p6a, lhsT=h1_fm[:, 128 * j:128 * (j + 1)],
                    rhs=wa_bf[:, j * H:(j + 1) * H],
                    start=(j == 0), stop=(j == DCH - 1)),
                   reads=[b_h1fm[j], b_wa, b_p6], writes=[b_p6])
            OP("dve", lambda e: e.tensor_tensor(out=u16[:], in0=p6a, in1=babc2[:], op=ALU.add),
               reads=[b_p6, b_babc2], writes=[b_u16])
            OP("act", lambda e: e.activation(out=t16a[:], in_=u16[:], func=AF.Abs),
               reads=[b_u16], writes=[b_t16a])
            OP("act", lambda e: e.activation(out=t16b[:], in_=t16a[:], func=AF.Exp, scale=C_NEG1),
               reads=[b_t16a, b_cvec], writes=[b_t16b])
            OP("act", lambda e: e.activation(out=t16a[:], in_=t16b[:], func=AF.Ln, bias=1.0),
               reads=[b_t16b], writes=[b_t16a])
            OP("dve", lambda e: e.tensor_scalar(out=t16b[:], in0=u16[:], scalar1=0.0,
                                                scalar2=None, op0=ALU.min),
               reads=[b_u16], writes=[b_t16b])
            OP("dve", lambda e: e.tensor_tensor(out=lna[:], in0=t16b[:], in1=t16a[:],
                                                op=ALU.subtract),
               reads=[b_t16a, b_t16b], writes=[b_lna])
            front_done.add(g)

        def attn_subchunk(g):
            xa = xr[g % 4][:]
            xb = b_xr[g % 4]
            if g not in ln1_done:
                ln1(g)
            if g not in h1t_done:
                h1t(g)
            if g not in stts_done:
                h1_stts(g)
            if g not in front_done:
                attn_front(g)
            proj(h1_fm[:], b_h1fm, wk_bf, b_wk, P23, b_pk, D)
            OP("pe", lambda e: e.matmul(out=p6L, lhsT=tri_sb[:], rhs=lna[:], start=True,
                                        stop=True),
               reads=[b_tri, b_lna, b_p6], writes=[b_p6])
            OP("act", lambda e: e.activation(out=el[:], in_=p6L, func=AF.Exp),
               reads=[b_p6], writes=[b_el])
            OP("act", lambda e: e.activation(out=enl[:], in_=p6L, func=AF.Exp, scale=C_NEG1),
               reads=[b_p6, b_cvec], writes=[b_enl])
            OP("act", lambda e: e.copy(out=el_bf[:], in_=el[:]), reads=[b_el], writes=[b_elbf])
            OP("act", lambda e: e.copy(out=enl_bf[:], in_=enl[:]), reads=[b_enl], writes=[b_enlbf])
            OP("pe", lambda e: e.matmul(out=p6er, lhsT=oneh_sb[:], rhs=el[:], start=True,
                                        stop=True),
               reads=[b_oneh, b_el, b_p6], writes=[b_p6])
            OP("act", lambda e: e.copy(out=elcrow[:], in_=p6er), reads=[b_p6], writes=[b_elcrow])
            OP("pe", lambda e: e.matmul(out=p6eb, lhsT=ones128_sb[:], rhs=elcrow[:], start=True,
                                        stop=True),
               reads=[b_ones128, b_elcrow, b_p6], writes=[b_p6])
            OP("act", lambda e: e.copy(out=elc_bf[:], in_=p6eb), reads=[b_p6], writes=[b_elcbf])
            proj(h1_fm[:], b_h1fm, wq_bf, b_wq, P01, b_pq, D)
            elu_scale(P23[:], b_pk, enl_bf[:], b_enlbf, kt_tm[:], b_kttm, on_dve=True)
            emit_backlog()
            proj(h1_fm[:], b_h1fm, wv_bf, b_wv, P45, b_pv, D)
            elu_scale(P01[:], b_pq, el_bf[:], b_elbf, qt_tm[:], b_qttm)
            transposes(kt_tm[:], b_kttm, kt_fm[:], b_ktfm)
            transposes(qt_tm[:], b_qttm, qt_fm[:], b_qtfm)
            OP("act", lambda e: e.copy(out=v_tm[:], in_=P45[:]), reads=[b_pv], writes=[b_vtm])

            def em_scores(h):
                u, half = h // 2, h % 2
                base = 64 * half
                qsl = qt_fm[base:base + 64, 128 * u:128 * (u + 1)]
                ksl = kt_fm[base:base + 64, 128 * u:128 * (u + 1)]
                sc = scA if h % 2 == 0 else scB
                bsc = b_scA if h % 2 == 0 else b_scB
                OP("pe", lambda e, sc=sc, ksl=ksl, qsl=qsl: e.matmul(
                    out=sc, lhsT=ksl, rhs=qsl, start=True, stop=True),
                   reads=[b_ktfm, b_qtfm, bsc], writes=[bsc])

            def em_mask(h):
                sc = scA if h % 2 == 0 else scB
                bsc = b_scA if h % 2 == 0 else b_scB
                OP("dve", lambda e, sc=sc, h=h: e.tensor_tensor(
                    out=smt[h % 2][:], in0=sc, in1=tri_sb[:], op=ALU.mult),
                   reads=[bsc, b_tri], writes=[b_smt[h % 2]])

            def em_rest(h):
                u, half = h // 2, h % 2
                base = 64 * half
                qsl = qt_fm[base:base + 64, 128 * u:128 * (u + 1)]
                OP("pe", lambda e, h=h: e.matmul(
                    out=P45[:, 64 * h:64 * (h + 1)], lhsT=smt[h % 2][:],
                    rhs=v_tm[:, 64 * h:64 * (h + 1)], start=True, stop=False),
                   reads=[b_smt[h % 2], b_vtm, b_po], writes=[b_po])
                OP("pe", lambda e, h=h, qsl=qsl, base=base: e.matmul(
                    out=P45[:, 64 * h:64 * (h + 1)], lhsT=qsl,
                    rhs=Sst[base:base + 64, 64 * h:64 * (h + 1)], start=False, stop=True),
                   reads=[b_qtfm, b_S, b_po], writes=[b_po])
                OP("pe", lambda e, h=h: e.matmul(
                    out=p6den[:, h:h + 1], lhsT=smt[h % 2][:], rhs=onesb_sb[:],
                    start=True, stop=False),
                   reads=[b_smt[h % 2], b_onesb, b_p6], writes=[b_p6])
                OP("pe", lambda e, h=h, qsl=qsl, base=base: e.matmul(
                    out=p6den[:, h:h + 1], lhsT=qsl, rhs=zst[base:base + 64, h:h + 1],
                    start=False, stop=True),
                   reads=[b_qtfm, b_z, b_p6], writes=[b_p6])
                OP("pe", lambda e, h=h: e.matmul(
                    out=P23[0:64, 64 * h:64 * (h + 1)],
                    lhsT=kt_tm[:, 64 * h:64 * (h + 1)],
                    rhs=v_tm[:, 64 * h:64 * (h + 1)], start=True, stop=True),
                   reads=[b_kttm[h // 2], b_vtm, b_kvS], writes=[b_kvS])
                OP("pe", lambda e, h=h: e.matmul(
                    out=p6kvz[0:64, h:h + 1], lhsT=kt_tm[:, 64 * h:64 * (h + 1)],
                    rhs=onesb_sb[:], start=True, stop=True),
                   reads=[b_kttm[h // 2], b_onesb, b_p6], writes=[b_p6])

            ln1_next = ln1_parts(g + 1) if g + 1 < nsub else None
            em_scores(0)
            for h in range(H):
                em_mask(h)
                if h + 1 < H:
                    em_scores(h + 1)
                em_rest(h)
                if h in (2, 5, 8, 11) and g + 1 < nsub:
                    ln1_next[(h - 2) // 3]()
            OP("dve", lambda e: e.tensor_scalar(out=den_sb[:], in0=p6den, scalar1=1e-6,
                                                scalar2=None, op0=ALU.add),
               reads=[b_p6], writes=[b_den])
            OP("dve", lambda e: e.reciprocal(out=rden[:], in_=den_sb[:]),
               reads=[b_den], writes=[b_rden], exotic=True)
            OP("act", lambda e: e.copy(out=kvbf[0:64, :], in_=P23[0:64, :]),
               reads=[b_kvS], writes=[b_kvbf])
            OP("act", lambda e: e.copy(out=zvbf[0:64, :], in_=p6kvz[0:64, :]),
               reads=[b_p6, b_den], writes=[b_zvbf])
            warm(P01[0:1, 0:1], [b_den], b_pq)
            if g + 1 < nsub:
                h1t(g + 1)
            warm(P01[0:1, 0:1], [b_rden], b_pq)
            warm(P01[0:1, 0:1], [b_kvbf], b_pq)
            OP("pool", lambda e: e.tensor_tensor(out=Sst[0:64, :], in0=Sst[0:64, :],
                                                 in1=kvbf[0:64, :], op=ALU.add),
               reads=[b_S, b_kvbf], writes=[b_S])
            elcrep = elc_bf[0:64, :].unsqueeze(2).to_broadcast([64, H, 64])
            OP("pool", lambda e, elcrep=elcrep: e.tensor_tensor(
                out=Sst[0:64, :].rearrange("p (h o) -> p h o", h=H),
                in0=Sst[0:64, :].rearrange("p (h o) -> p h o", h=H),
                in1=elcrep, op=ALU.mult),
               reads=[b_S, b_elcbf], writes=[b_S])
            OP("pool", lambda e: e.tensor_tensor(out=zst[0:64, :], in0=zst[0:64, :],
                                                 in1=zvbf[0:64, :], op=ALU.add),
               reads=[b_z, b_zvbf], writes=[b_z])
            OP("pool", lambda e: e.tensor_tensor(
                out=zst[0:64, :], in0=zst[0:64, :], in1=elc_bf[0:64, :], op=ALU.mult),
               reads=[b_z, b_elcbf], writes=[b_z])
            OP("pool", lambda e: e.tensor_copy(out=Sst[64:128, :], in_=Sst[0:64, :]),
               reads=[b_S], writes=[b_S])
            OP("pool", lambda e: e.tensor_copy(out=zst[64:128, :], in_=zst[0:64, :]),
               reads=[b_z], writes=[b_z])
            for u in (0, 4, 1, 5, 2, 6, 3, 7):
                for h in (2 * u, 2 * u + 1):
                    if h < 8:
                        OP("act", lambda e, h=h: e.activation(
                            out=ao_tm[:, 64 * h:64 * (h + 1)],
                            in_=P45[:, 64 * h:64 * (h + 1)],
                            func=AF.Copy, scale=rden[:, h:h + 1]),
                           reads=[b_po, b_rden], writes=[b_aotm[u]])
                    else:
                        OP("dve", lambda e, h=h: e.tensor_scalar(
                            out=ao_tm[:, 64 * h:64 * (h + 1)],
                            in0=P45[:, 64 * h:64 * (h + 1)],
                            scalar1=rden[:, h:h + 1], scalar2=None, op0=ALU.mult),
                           reads=[b_po, b_rden], writes=[b_aotm[u]])
                OP("pe", lambda e, u=u: e.transpose(
                    out=P7[:, 128 * u:128 * (u + 1)], in_=ao_tm[:, 128 * u:128 * (u + 1)],
                    identity=ident_sb[:]),
                   reads=[b_aotm[u], b_ident, b_p7], writes=[b_p7])
                if u in (1, 6):
                    warm(P01[0:1, 0:1], [b_aotm[u]], b_py)
            if g + 1 < nsub:
                h1_stts(g + 1)
            OP("act", lambda e: e.copy(out=ao_fm[:], in_=P7[:]), reads=[b_p7], writes=[b_aofm])
            proj(ao_fm[:], b_aofm, wo_bf, b_wo, P01, b_py, D)
            OP("dve", lambda e, xa=xa: e.tensor_tensor(out=xa, in0=P01[:], in1=xa, op=ALU.add),
               reads=[b_py, xb], writes=[xb])
            s = g % SPC
            ln_stats(xa, xb)
            ln_norm(xa, xb, h2_tm[:], b_h2tm)
            if g + 1 < nsub:
                attn_front(g + 1)
            def h2t(s=s):
                for u in range(DCH):
                    OP("pe", lambda e, u=u, s=s: e.transpose(
                        out=P7[:, 128 * u:128 * (u + 1)],
                        in_=h2_tm[:, 128 * u:128 * (u + 1)], identity=ident_sb[:]),
                       reads=[b_h2tm, b_ident], writes=[b_p7])
                OP("act", lambda e, s=s: e.copy(out=h2_tm[:], in_=P7[:]),
                   reads=[b_p7], writes=[b_h2tm])
                for u in range(DCH):
                    sb_b = sbT[:, u:u + 1].to_broadcast([128, 128])
                    OP("dve", lambda e, u=u, s=s, sb_b=sb_b: e.scalar_tensor_tensor(
                        out=h2_fm[:, s * D + 128 * u:s * D + 128 * (u + 1)],
                        in0=h2_tm[:, 128 * u:128 * (u + 1)],
                        scalar=cbT[:, u:u + 1], in1=sb_b, op0=ALU.mult, op1=ALU.add),
                       reads=[b_h2tm, b_cb1, b_sbf], writes=[b_h2fm[s][u]])
            pe_backlog.append(h2t)

        piece_ctr = [0]
        fcvt = [0]

        def gate_comb(s):
            ga_, gb_, gc_, gd_, ge_, gf_ = (g8[:, i * E:(i + 1) * E] for i in range(6))
            OP("dve", lambda e: e.tensor_tensor(out=ga_, in0=p6g, in1=gateb_t[:], op=ALU.add),
               reads=[b_p6, b_gatebt], writes=[b_g8])
            OP("dve", lambda e: e.tensor_scalar(out=ga_, in0=ga_, scalar1=1e-4,
                                                scalar2=None, op0=ALU.max),
               reads=[b_g8], writes=[b_g8])
            OP("dve", lambda e: e.tensor_reduce(out=m1c[:, 0:1], in_=ga_, axis=AX.X,
                                                op=ALU.max),
               reads=[b_g8], writes=[b_m1c], exotic=True)
            OP("dve", lambda e: e.tensor_scalar(out=m1c[:, 1:2], in0=m1c[:, 0:1],
                                                scalar1=-1.0, scalar2=None, op0=ALU.mult),
               reads=[b_m1c], writes=[b_m1c])
            OP("act", lambda e: e.activation(out=gb_, in_=ga_, func=AF.Exp,
                                             bias=m1c[:, 1:2]),
               reads=[b_g8, b_m1c], writes=[b_g8])
            OP("dve", lambda e: e.tensor_reduce(out=m1c[:, 0:1], in_=gb_, axis=AX.X,
                                                op=ALU.add),
               reads=[b_g8], writes=[b_m1c], exotic=True)
            OP("dve", lambda e: e.reciprocal(out=m1c[:, 1:2], in_=m1c[:, 0:1]),
               reads=[b_m1c], writes=[b_m1c], exotic=True)
            OP("dve", lambda e: e.tensor_scalar(out=ga_, in0=gb_, scalar1=m1c[:, 1:2],
                                                scalar2=None, op0=ALU.mult),
               reads=[b_g8, b_m1c], writes=[b_g8])
            OP("dve", lambda e: e.tensor_reduce(out=mrep[:, 0:1], in_=ga_, axis=AX.X,
                                                op=ALU.max),
               reads=[b_g8], writes=[b_mrep], exotic=True)
            OP("dve", lambda e: e.match_replace(out=gb_, in_to_replace=mrep[:],
                                                in_values=ga_, imm_value=-1e30),
               reads=[b_g8, b_mrep], writes=[b_g8], exotic=True)
            OP("dve", lambda e: e.tensor_tensor(out=gc_, in0=ga_, in1=gb_, op=ALU.not_equal),
               reads=[b_g8], writes=[b_g8])
            OP("dve", lambda e: e.tensor_reduce(out=mrep[:, 0:1], in_=gb_, axis=AX.X,
                                                op=ALU.max),
               reads=[b_g8], writes=[b_mrep], exotic=True)
            OP("dve", lambda e: e.match_replace(out=gd_, in_to_replace=mrep[:],
                                                in_values=gb_, imm_value=-1e30),
               reads=[b_g8, b_mrep], writes=[b_g8], exotic=True)
            OP("dve", lambda e: e.tensor_tensor(out=ge_, in0=gb_, in1=gd_, op=ALU.not_equal),
               reads=[b_g8], writes=[b_g8])
            OP("dve", lambda e: e.tensor_tensor(out=gc_, in0=gc_, in1=ge_, op=ALU.add),
               reads=[b_g8], writes=[b_g8])
            OP("dve", lambda e: e.tensor_scalar(out=gf_, in0=ga_, scalar1=0.1,
                                                scalar2=None, op0=ALU.max),
               reads=[b_g8], writes=[b_g8])
            OP("dve", lambda e: e.tensor_tensor(out=gf_, in0=gf_, in1=gc_, op=ALU.mult),
               reads=[b_g8], writes=[b_g8])
            OP("dve", lambda e: e.tensor_reduce(out=m1c[:, 0:1], in_=gf_, axis=AX.X,
                                                op=ALU.add),
               reads=[b_g8], writes=[b_m1c], exotic=True)
            OP("dve", lambda e: e.reciprocal(out=m1c[:, 1:2], in_=m1c[:, 0:1]),
               reads=[b_m1c], writes=[b_m1c], exotic=True)
            OP("dve", lambda e, s=s: e.tensor_scalar(
                out=combc[:, s * E:(s + 1) * E], in0=gf_, scalar1=m1c[:, 1:2],
                scalar2=0.5, op0=ALU.mult, op1=ALU.mult),
               reads=[b_g8, b_m1c], writes=[b_comb])
            OP("dve", lambda e, s=s: e.tensor_copy(out=comb_bfx[:, s * E:(s + 1) * E],
                                                   in_=combc[:, s * E:(s + 1) * E]),
               reads=[b_comb], writes=[b_combbfx])

        def combT_biasmix(s, xa, xb):
            OP("pe", lambda e, s=s: e.transpose(out=P7[0:8, 0:128],
                                                in_=comb_bfx[:, s * E:(s + 1) * E],
                                                identity=ident_sb[:]),
               reads=[b_combbfx, b_ident, b_p7], writes=[b_p7])
            OP("act", lambda e, s=s: e.copy(out=combT_bf[:, s * 128:(s + 1) * 128],
                                            in_=P7[0:8, 0:128]),
               reads=[b_p7], writes=[b_combT])
            for hf in range(2):
                OP("pe", lambda e, hf=hf, s=s: e.matmul(
                    out=P45[:, 512 * hf:512 * (hf + 1)],
                    lhsT=combT_bf[:, s * 128:(s + 1) * 128],
                    rhs=expb_bf[:, 512 * hf:512 * (hf + 1)], start=True, stop=True),
                   reads=[b_combT, b_expbbf, b_pbm], writes=[b_pbm])
            OP("dve", lambda e, xa=xa: e.tensor_tensor(out=xa, in0=P45[:], in1=xa,
                                                       op=ALU.add),
               reads=[b_pbm, xb], writes=[xb])

        def ffn_chunk(c):
            g0 = c * SPC

            def gates(s):
                for j in range(DCH):
                    OP("pe", lambda e, j=j, s=s: e.matmul(
                        out=p6g, lhsT=h2_fm[:, s * D + 128 * j:s * D + 128 * (j + 1)],
                        rhs=gw_bf[:, j * E:(j + 1) * E],
                        start=(j == 0), stop=(j == DCH - 1)),
                       reads=[b_h2fm[s][j], b_gw, b_p6], writes=[b_p6])

            ptiles = (P01, P23, P45)
            pbufs = (b_fe0, b_fe1, b_fe2)

            def piece_dma(e_, q2):
                pc = piece_ctr[0]
                piece_ctr[0] += 1
                sl = pc % 4
                if c == 0:
                    for w_ in range(4):
                        j = 4 * q2 + w_
                        fsl = fcvt[0] % 4
                        fcvt[0] += 1
                        OP("qs", lambda e, e_=e_, j=j, fsl=fsl: e.dma_start(
                            out=wring[fsl][:], in_=dexpw[e_, :, j * D:(j + 1) * D]),
                           writes=[b_wring[fsl]], dma_buf=b_wring[fsl])
                        OP("dve", lambda e, sl=sl, w_=w_, fsl=fsl: e.tensor_tensor(
                            out=ering[sl][:, w_ * D:(w_ + 1) * D], in0=wring[fsl][:],
                            in1=gb_exp[:], op=ALU.mult),
                           reads=[b_wring[fsl], b_gbexp], writes=[b_ering[sl]])
                    OP("qg", lambda e, e_=e_, q2=q2, sl=sl: e.dma_start(
                        out=decache[e_, :, q2 * 4096:(q2 + 1) * 4096], in_=ering[sl][:]),
                       reads=[b_ering[sl]], writes=[b_ecache[e_][q2]],
                       dma_buf=b_ering[sl])
                else:
                    OP("qs", lambda e, e_=e_, q2=q2, sl=sl: e.dma_start(
                        out=ering[sl][:], in_=decache[e_, :, q2 * 4096:(q2 + 1) * 4096]),
                       reads=[b_ecache[e_][q2]], writes=[b_ering[sl]],
                       dma_buf=b_ering[sl])
                return sl

            def piece_mms(e_, q2, sl, subs, hfs=(0, 1)):
                for w_ in range(4):
                    j = 4 * q2 + w_
                    for s in subs:
                        pt = ptiles[(2 * e_ + s) % 3]
                        pb = pbufs[(2 * e_ + s) % 3]
                        for hf in hfs:
                            OP("pe", lambda e, j=j, s=s, w_=w_, hf=hf, sl=sl, pt=pt: e.matmul(
                                out=pt[:, 512 * hf:512 * (hf + 1)],
                                lhsT=h2_fm[:, s * D + 128 * j:s * D + 128 * (j + 1)],
                                rhs=ering[sl][:, w_ * D + 512 * hf:w_ * D + 512 * (hf + 1)],
                                start=(j == 0), stop=(j == DCH - 1)),
                               reads=[b_h2fm[s][j], b_ering[sl], pb], writes=[pb])

            def expert_mms(e_, sls, subs):
                for q2 in range(2):
                    piece_mms(e_, q2, sls[q2], subs)
                for s in subs:
                    g = g0 + s
                    xa = xr[g % 4][:]
                    xb = b_xr[g % 4]
                    pt = ptiles[(2 * e_ + s) % 3]
                    pb = pbufs[(2 * e_ + s) % 3]
                    OP("dve", lambda e, xa=xa, pt=pt, s=s, e_=e_: e.scalar_tensor_tensor(
                        out=xa, in0=pt[:], scalar=combc[:, s * E + e_:s * E + e_ + 1],
                        in1=xa, op0=ALU.mult, op1=ALU.add),
                       reads=[pb, b_comb, xb], writes=[xb])

            def stts(e_):
                pass

            gates(0)
            gate_comb(0)
            sls = [piece_dma(0, 0)]
            piece_mms(0, 0, sls[0], [0])
            warm(P23[0:1, 0:1], [b_stat2], b_fe1)
            warm(P23[0:1, 0:1], [b_h2tm], b_fe1)
            emit_backlog()
            gates(1)
            gate_comb(1)
            sls += [piece_dma(0, 1)]
            piece_mms(0, 0, sls[0], [1])
            piece_mms(0, 1, sls[1], [0, 1])
            for s in range(SPC):
                g = g0 + s
                combT_biasmix(s, xr[g % 4][:], b_xr[g % 4])
            for s in range(SPC):
                g = g0 + s
                xa = xr[g % 4][:]
                xb = b_xr[g % 4]
                pt = ptiles[s % 3]
                pb = pbufs[s % 3]
                OP("dve", lambda e, xa=xa, pt=pt, s=s: e.scalar_tensor_tensor(
                    out=xa, in0=pt[:], scalar=combc[:, s * E:s * E + 1],
                    in1=xa, op0=ALU.mult, op1=ALU.add),
                   reads=[pb, b_comb, xb], writes=[xb])
            for e_ in range(1, E):
                sls = [piece_dma(e_, q2) for q2 in range(2)]
                expert_mms(e_, sls, [0, 1])

        for c in range(nch):
            for s in range(SPC):
                attn_subchunk(c * SPC + s)
            ffn_chunk(c)
            for s in range(SPC):
                g = c * SPC + s
                OP("qg", lambda e, g=g: e.dma_start(
                    out=dy[g * 128:(g + 1) * 128, :], in_=xr[g % 4][:]),
                   reads=[b_xr[g % 4]], writes=[b_y], dma_buf=b_xr[g % 4])
            for s in range(SPC):
                g = (c + 2) * SPC + s
                if g < nsub:
                    OP("qg", lambda e, g=g: e.dma_start(
                        out=xr[g % 4][:], in_=dx[g * 128:(g + 1) * 128, :]),
                       writes=[b_xr[g % 4]], dma_buf=b_xr[g % 4])

        kb.final_wait("qs")

    es.close()
    return nc



def _wlayout(w):
    fdim = w.shape[1]
    return np.ascontiguousarray(
        w.reshape(8, 128, fdim).transpose(1, 0, 2).reshape(128, 8 * fdim))


def kernel(L0_attn_x, L0_attn_Wq, L0_attn_Wk, L0_attn_Wv, L0_attn_Wo, L0_attn_Wa,
           L0_attn_ba, L0_ffn_gate_W, L0_ffn_gate_b, L0_ffn_expert_W, L0_ffn_expert_b,
           L0_c, L0_wmod):
    n_tokens = L0_attn_x.shape[1]
    nc = build_nc(n_tokens)

    wq = _wlayout(np.asarray(L0_attn_Wq, np.float32))
    wk = _wlayout(np.asarray(L0_attn_Wk, np.float32))
    wv = _wlayout(np.asarray(L0_attn_Wv, np.float32))
    wo = _wlayout(np.asarray(L0_attn_Wo, np.float32))
    wa = _wlayout(np.asarray(L0_attn_Wa, np.float32))
    gw = _wlayout(np.asarray(L0_ffn_gate_W, np.float32))
    expw = np.stack([_wlayout(np.asarray(L0_ffn_expert_W[e], np.float32)) for e in range(E)])
    expb = np.ascontiguousarray(np.asarray(L0_ffn_expert_b, np.float32))
    wm = np.asarray(L0_wmod, np.float32).reshape(1024, 6, 1024)[:, [0, 1, 5, 2, 3, 4], :]
    wm = np.ascontiguousarray(
        wm.reshape(8, 128, 6, 1024).transpose(1, 2, 0, 3).reshape(128, 48 * 1024))
    ba_bc = np.ascontiguousarray(
        np.broadcast_to(np.asarray(L0_attn_ba, np.float32), (128, H)))
    gateb_bc = np.ascontiguousarray(
        np.broadcast_to(np.asarray(L0_ffn_gate_b, np.float32), (128, E)))
    tri = np.triu(np.ones((128, 128), dtype=np.float32))
    identbf = np.eye(128, dtype=np.float32).astype(ml_dtypes.bfloat16)
    onesbf = np.ones((128, 1), dtype=np.float32).astype(ml_dtypes.bfloat16)
    oneh127 = np.zeros((128, 1), dtype=np.float32)
    oneh127[127, 0] = 1.0
    ones1x64 = np.ones((1, 64), dtype=np.float32)
    ones1x128 = np.ones((1, 128), dtype=np.float32)
    cvec_h = np.zeros((128, 4), dtype=np.float32)
    cvec_h[:, 0] = 1e-5
    cvec_h[:, 1] = -0.5
    cvec_h[:, 2] = -1.0

    shared = dict(wq=wq, wk=wk, wv=wv, wo=wo, wa=wa, gw=gw, expw=expw, expb=expb,
                  wmod=wm, ba_bc=ba_bc, gateb_bc=gateb_bc, tri=tri, identbf=identbf,
                  onesbf=onesbf, oneh127=oneh127, ones1x64=ones1x64,
                  ones1x128=ones1x128, cvec=cvec_h)
    in_maps = []
    xf = np.asarray(L0_attn_x, np.float32)
    for b in range(8):
        m = dict(shared)
        m["x"] = np.ascontiguousarray(xf[b, :n_tokens])
        m["cmod"] = np.ascontiguousarray(
            np.asarray(L0_c[b], np.float32).reshape(8, 128).T)
        in_maps.append(m)

    res = run_bass_kernel_spmd(nc, in_maps, core_ids=list(range(8)))
    out = np.stack([r["y"] for r in res.results])
    return out.astype(np.float32)


# version 33
# speedup vs baseline: 1.0503x; 1.0032x over previous
import os
import numpy as np
import ml_dtypes
from contextlib import ExitStack
import concourse.bass as bass
import concourse.mybir as mybir
from concourse.bass_utils import run_bass_kernel_spmd

F32 = mybir.dt.float32
BF16 = mybir.dt.bfloat16
AF = mybir.ActivationFunctionType
ALU = mybir.AluOpType
AX = mybir.AxisListType

D = 1024
H = 16
DH = 64
E = 8
DCH = 8
SUB = 128
SPC = 2
T = SUB * SPC
AO_SEQ = (4, 0, 5, 1, 6, 2, 7, 3)
STRICT_ENGINES = set(os.environ.get("KSTRICT", "dve,act,pool").split(",")) - {""}



class Buf:
    __slots__ = ("name", "last_w", "readers", "overlaps")

    def __init__(self, name):
        self.name = name
        self.last_w = None
        self.readers = []
        self.overlaps = []


def link_overlaps(*bufs):
    for b in bufs:
        for b2 in bufs:
            if b2 is not b and b2 not in b.overlaps:
                b.overlaps.append(b2)


class Eng:
    __slots__ = ("name", "obj", "sem", "dma", "n_ops", "incs", "cum", "known",
                 "last_inst", "insts")

    def __init__(self, name, obj, sem, dma):
        self.name = name
        self.obj = obj
        self.sem = sem
        self.dma = dma
        self.n_ops = 0
        self.incs = []
        self.cum = 0
        self.known = {}
        self.last_inst = None
        self.insts = []


class KB:
    def __init__(self, nc, es):
        self.nc = nc
        self.es = es
        self.engs = {}

    def eng(self, name, obj, dma=False):
        sem = self.es.enter_context(self.nc.semaphore(f"sem_{name}"))
        self.engs[name] = Eng(name, obj, sem, dma)
        return self.engs[name]

    def _value_for(self, peng, idx):
        lo = None
        for oi, val in reversed(peng.incs):
            if oi >= idx:
                lo = val
            else:
                break
        if lo is not None:
            return lo
        assert not peng.dma, "dma producers always have eager incs"
        j = min(idx + 1, peng.n_ops - 1)
        peng.cum += 1
        peng.insts[j].then_inc(peng.sem, 1)
        peng.incs.append((j, peng.cum))
        return peng.cum

    def buf_sem(self, buf, qname):
        pname = f"dq_{qname}_{buf.name}"
        if pname not in self.engs:
            sem = self.es.enter_context(self.nc.semaphore(pname))
            pe = Eng(pname, None, sem, True)
            self.engs[pname] = pe
        return pname

    def op(self, ename, fn, reads=(), writes=(), dma_buf=None, exotic=False):
        eng = self.engs[ename]
        assert (dma_buf is None) or eng.dma
        waits = {}

        def need(pname, idx, pstrict=False):
            if (pname == ename and not eng.dma and not (pstrict or exotic)
                    and ename not in STRICT_ENGINES):
                return
            if pname == ename and (eng.dma or ename == "pe"):
                return
            if waits.get(pname, -1) < self._tok_val(pname, idx):
                waits[pname] = self._tok_val(pname, idx)

        seen = set()
        for b in reads:
            for bb in (b, *b.overlaps):
                if id(bb) in seen:
                    continue
                if bb.last_w:
                    need(*bb.last_w)
            seen.add(id(b))
        for b in writes:
            for bb in (b, *b.overlaps):
                if bb.last_w:
                    need(*bb.last_w)
                for r in bb.readers:
                    need(*r)
        for pname, val in waits.items():
            if eng.known.get(pname, -1) < val:
                eng.obj.wait_ge(self.engs[pname].sem, val)
                eng.known[pname] = val
        inst = fn(eng.obj)
        eng.n_ops += 1
        eng.last_inst = inst
        eng.insts.append(inst)
        if eng.dma:
            assert dma_buf is not None, "DMA ops must name their slot buffer"
            pname = self.buf_sem(dma_buf, ename)
            peng = self.engs[pname]
            peng.cum += 16
            inst.then_inc(peng.sem, 16)
            tok = (pname, peng.cum, False)
        else:
            tok = (ename, eng.n_ops - 1, exotic)
        for b in writes:
            b.last_w = tok
            b.readers = []
        for b in reads:
            b.readers.append(tok)
        return inst

    def _tok_val(self, pname, idx):
        peng = self.engs[pname]
        if peng.dma:
            return idx
        return self._value_for(peng, idx)

    def final_wait(self, ename):
        eng = self.engs[ename]
        for pname, peng in self.engs.items():
            if pname == ename or (peng.n_ops == 0 and not peng.dma):
                continue
            if peng.dma:
                if peng.cum == 0:
                    continue
                val = peng.cum
            else:
                val = self._value_for(peng, peng.n_ops - 1)
            if eng.known.get(pname, -1) < val:
                eng.obj.wait_ge(peng.sem, val)
                eng.known[pname] = val



def build_nc(n_tokens=8192, debug=0):
    assert n_tokens % T == 0
    nch = n_tokens // T
    nsub = n_tokens // SUB

    nc = bass.Bass("TRN2", target_bir_lowering=False)

    dx = nc.dram_tensor("x", [n_tokens, D], F32, kind="ExternalInput").ap()
    dy = nc.dram_tensor("y", [n_tokens, D], F32, kind="ExternalOutput").ap()
    dwq = nc.dram_tensor("wq", [128, 8 * D], F32, kind="ExternalInput").ap()
    dwk = nc.dram_tensor("wk", [128, 8 * D], F32, kind="ExternalInput").ap()
    dwv = nc.dram_tensor("wv", [128, 8 * D], F32, kind="ExternalInput").ap()
    dwo = nc.dram_tensor("wo", [128, 8 * D], F32, kind="ExternalInput").ap()
    dwa = nc.dram_tensor("wa", [128, 8 * H], F32, kind="ExternalInput").ap()
    dgw = nc.dram_tensor("gw", [128, 8 * E], F32, kind="ExternalInput").ap()
    dexpw = nc.dram_tensor("expw", [E, 128, 8 * D], F32, kind="ExternalInput").ap()
    dexpb = nc.dram_tensor("expb", [E, D], F32, kind="ExternalInput").ap()
    dwmod = nc.dram_tensor("wmod", [128, 48 * D], F32, kind="ExternalInput").ap()
    dc = nc.dram_tensor("cmod", [128, 8], F32, kind="ExternalInput").ap()
    dba = nc.dram_tensor("ba_bc", [128, H], F32, kind="ExternalInput").ap()
    dgb = nc.dram_tensor("gateb_bc", [128, E], F32, kind="ExternalInput").ap()
    dtri = nc.dram_tensor("tri", [128, 128], F32, kind="ExternalInput").ap()
    dident = nc.dram_tensor("identbf", [128, 128], BF16, kind="ExternalInput").ap()
    donesb = nc.dram_tensor("onesbf", [128, 1], BF16, kind="ExternalInput").ap()
    doneh = nc.dram_tensor("oneh127", [128, 1], F32, kind="ExternalInput").ap()
    dones64 = nc.dram_tensor("ones1x64", [1, 64], F32, kind="ExternalInput").ap()
    dones128 = nc.dram_tensor("ones1x128", [1, 128], F32, kind="ExternalInput").ap()
    dcvec = nc.dram_tensor("cvec", [128, 4], F32, kind="ExternalInput").ap()
    decache = nc.dram_tensor("ecache", [E, 128, 8 * D], BF16, kind="Internal").ap()
    if debug:
        ddbg = nc.dram_tensor("dbg", [debug, 128, D], F32, kind="ExternalOutput").ap()

    es = ExitStack()
    kb = KB(nc, es)
    S = lambda name, shape, dt: es.enter_context(nc.sbuf_tensor(name, shape, dt))
    P = lambda name, shape, dt: es.enter_context(nc.psum_tensor(name, shape, dt))

    xr = [S(f"xr{i}", [128, D], F32) for i in range(4)]
    wq_bf = S("wq_bf", [128, 8 * D], BF16)
    wk_bf = S("wk_bf", [128, 8 * D], BF16)
    wv_bf = S("wv_bf", [128, 8 * D], BF16)
    wo_bf = S("wo_bf", [128, 8 * D], BF16)
    wa_bf = S("wa_bf", [128, 8 * H], BF16)
    gw_bf = S("gw_bf", [128, 8 * E], BF16)
    expb_f = S("expb_f", [E, D], F32)
    expb_bf = S("expb_bf", [E, D], BF16)
    ering = [S(f"ering{i}", [128, 4096], BF16) for i in range(4)]
    wring = [S(f"wring{i}", [128, D], F32) for i in range(4)]
    sf1 = S("sf1", [128, D], F32)
    s1 = S("s1", [128, D], BF16)
    s2 = S("s2", [128, D], BF16)
    h1_bf = S("h1_bf", [128, D], BF16)
    h1_fm = S("h1_fm", [128, D], BF16)
    qt_tm = S("qt_tm", [128, D], BF16)
    kt_tm = S("kt_tm", [128, D], BF16)
    v_tm = S("v_tm", [128, D], BF16)
    qt_fm = S("qt_fm", [128, D], BF16)
    kt_fm = S("kt_fm", [128, D], BF16)
    smt = [S(f"smt{i}", [128, 128], BF16) for i in range(2)]
    ao_tm = S("ao_tm", [128, D], BF16)
    ao_fm = S("ao_fm", [128, D], BF16)
    h2_tm = S("h2_tm", [128, D], BF16)
    h2_fm = S("h2_fm", [128, SPC * D], BF16)
    Sst = S("Sst", [128, D], BF16)
    zst = S("zst", [128, H], BF16)
    kvbf = S("kvbf", [128, D], BF16)
    zvbf = S("zvbf", [128, H], BF16)
    u16 = S("u16", [128, H], F32)
    t16a = S("t16a", [128, H], F32)
    t16b = S("t16b", [128, H], F32)
    lna = S("lna", [128, H], F32)
    el = S("el", [128, H], F32)
    enl = S("enl", [128, H], F32)
    el_bf = S("el_bf", [128, H], BF16)
    enl_bf = S("enl_bf", [128, H], BF16)
    elcrow = S("elcrow", [1, H], F32)
    elc_bf = S("elc_bf", [128, H], BF16)
    den_sb = S("den_sb", [128, H], F32)
    rden = S("rden", [128, H], F32)
    stat = S("stat", [128, 16], F32)
    stat2 = S("stat2", [128, 4], F32)
    snap = S("snap", [128, 8], F32)
    g8 = S("g8", [128, 6 * E], F32)
    mrep = S("mrep", [128, 8], F32)
    m1c = S("m1c", [128, 4], F32)
    combc = S("combc", [128, SPC * E], F32)
    comb_bfx = S("comb_bfx", [128, SPC * E], BF16)
    combT_bf = S("combT_bf", [8, SPC * 128], BF16)
    c_sil = S("c_sil", [128, 8], F32)
    c_t8 = S("c_t8", [128, 16], F32)
    rowtmp = S("rowtmp", [1, D], F32)
    caT = S("caT", [128, DCH], F32)
    cbT = S("cbT", [128, DCH], F32)
    saT = S("saT", [128, DCH], F32)
    sbT = S("sbT", [128, DCH], F32)
    ga_exp = S("ga_exp", [128, D], F32)
    gb_exp = S("gb_exp", [128, D], F32)
    babc2 = S("babc2", [128, H], F32)
    gateb_t = S("gateb_t", [128, E], F32)
    tri_sb = S("tri_sb", [128, 128], F32)
    ident_sb = S("ident_sb", [128, 128], BF16)
    onesb_sb = S("onesb_sb", [128, 1], BF16)
    oneh_sb = S("oneh_sb", [128, 1], F32)
    ones64_sb = S("ones64_sb", [1, 64], F32)
    ones128_sb = S("ones128_sb", [1, 128], F32)
    cvec = S("cvec_sb", [128, 4], F32)

    P01 = P("P01", [128, D], F32)
    P23 = P("P23", [128, D], F32)
    P45 = P("P45", [128, D], F32)
    P6 = P("P6", [128, 512], F32)
    P7 = P("P7", [128, D], BF16)

    B = lambda n: Buf(n)
    b_xr = [B(f"xr{i}") for i in range(4)]
    b_wq, b_wk, b_wv, b_wo = B("wq"), B("wk"), B("wv"), B("wo")
    b_wa, b_gw = B("wa"), B("gw")
    b_expbf, b_expbbf = B("expb_f"), B("expb_bf")
    b_ering = [B(f"ering{i}") for i in range(4)]
    b_wring = [B(f"wring{i}") for i in range(4)]
    b_sf1, b_s1, b_s2 = B("sf1"), B("s1"), B("s2")
    b_h1bf = B("h1_bf")
    b_h1fm = [B(f"h1_fm{u}") for u in range(DCH)]
    b_qttm = [B(f"qt_tm{u}") for u in range(DCH)]
    b_kttm = [B(f"kt_tm{u}") for u in range(DCH)]
    b_vtm = B("v_tm")
    b_qtfm = [B("qt_fm0"), B("qt_fm1")]
    b_ktfm = [B("kt_fm0"), B("kt_fm1")]
    b_smt = [B("smt0"), B("smt1")]
    b_aotm = [B(f"ao_tm{u}") for u in range(DCH)]
    b_aofm = [B("ao_fm0"), B("ao_fm1")]
    b_h2tm = B("h2_tm")
    b_h2fm = [[B(f"h2_fm{s}_{u}") for u in range(DCH)] for s in range(SPC)]
    b_S, b_z = B("Sst"), B("zst")
    b_kvbf, b_zvbf = B("kvbf"), B("zvbf")
    b_u16, b_t16a, b_t16b, b_lna = B("u16"), B("t16a"), B("t16b"), B("lna")
    b_el, b_enl, b_elcrow, b_elcbf = B("el"), B("enl"), B("elcrow"), B("elc_bf")
    b_elbf, b_enlbf = B("el_bf"), B("enl_bf")
    b_den, b_rden, b_stat = B("den"), B("rden"), B("stat")
    b_stat2 = B("stat2")
    b_snap = B("snap")
    b_g8, b_mrep, b_m1c = B("g8"), B("mrep"), B("m1c")
    b_comb, b_combbfx, b_combT = B("combc"), B("comb_bfx"), B("combT_bf")
    b_csil, b_ct8, b_rowtmp = B("c_sil"), B("c_t8"), B("rowtmp")
    b_ca1, b_cb1, b_saf, b_sbf = B("ca1"), B("cb1"), B("sa_f"), B("sb_f")
    b_gaexp, b_gbexp = B("ga_exp"), B("gb_exp")
    b_babc2, b_gatebt = B("babc2"), B("gateb_t")
    b_tri, b_ident, b_onesb, b_oneh, b_ones64 = B("tri"), B("ident"), B("onesb"), B("oneh"), B("ones64")
    b_ones128 = B("ones128")
    b_cvec = B("cvec")
    b_ecache = [[B(f"ec{e}_{q}") for q in range(2)] for e in range(E)]
    b_y = B("y")

    b_pq, b_scA, b_scB, b_py, b_fe0 = B("pq"), B("scA"), B("scB"), B("py"), B("fe0")
    link_overlaps(b_pq, b_py, b_fe0)
    for bb in (b_pq, b_py, b_fe0):
        link_overlaps(bb, b_scA)
        link_overlaps(bb, b_scB)
    b_pk, b_kvS, b_fe1 = B("pk"), B("kvS"), B("fe1")
    link_overlaps(b_pk, b_kvS, b_fe1)
    b_pv, b_po, b_pbm, b_fe2, b_pmod = B("pv"), B("po"), B("pbm"), B("fe2"), B("pmod")
    link_overlaps(b_pv, b_po, b_pbm, b_fe2, b_pmod)
    b_p6 = B("p6")
    b_p7 = B("p7")
    b_modrow = B("modrow")
    link_overlaps(b_modrow, b_sf1)

    scA = P01[:, 0:128]
    scB = P01[:, 512:640]
    p6a = P6[:, 0:16]
    p6L = P6[:, 16:32]
    p6den = P6[:, 32:48]
    p6kvz = P6[:, 48:64]
    p6er = P6[0:1, 64:80]
    p6eb = P6[:, 80:96]
    p6g = P6[:, 96:104]
    p6t = P6[:, 104:112]

    with nc.Block():
        pe = kb.eng("pe", nc.tensor)
        dve = kb.eng("dve", nc.vector)
        act = kb.eng("act", nc.scalar)
        qs = kb.eng("qs", nc.sync, dma=True)
        qg = kb.eng("qg", nc.gpsimd, dma=True)
        pool = kb.eng("pool", nc.gpsimd, dma=False)
        qa = kb.eng("qa", nc.scalar, dma=True)

        OP = kb.op

        def warm(target_ap, rbufs, wbuf):
            OP("pe", lambda e: e.matmul(out=target_ap, lhsT=onesb_sb[:], rhs=onesb_sb[:],
                                        start=True, stop=True),
               reads=list(rbufs), writes=[wbuf])

        nc.scalar.add_instruction(mybir.InstLoadActFuncSet(
            name=nc.get_next_instruction_name(), act_func_set_id=6, ins=[], outs=[]))
        dbg_i = [0]
        dbg_f32 = S("dbg_f32", [128, D], F32) if debug else None
        b_dbgf = B("dbgf")
        b_dbgq = B("dbgq")

        def dump(ap, rbufs, ncols=D, cast=False, nparts=128):
            if not debug or dbg_i[0] >= debug:
                return
            i = dbg_i[0]
            dbg_i[0] += 1
            if cast:
                OP("act", lambda e: e.copy(out=dbg_f32[:, 0:ncols], in_=ap),
                   reads=list(rbufs), writes=[b_dbgf])
                OP("qg", lambda e, i=i: e.dma_start(out=ddbg[i, :, 0:ncols],
                                                    in_=dbg_f32[:, 0:ncols]),
                   reads=[b_dbgf], writes=[b_dbgq], dma_buf=b_dbgq)
            else:
                OP("qg", lambda e, i=i: e.dma_start(out=ddbg[i, 0:nparts, 0:ncols], in_=ap),
                   reads=list(rbufs) + [b_dbgq], writes=[b_dbgq], dma_buf=b_dbgq)
        kb.dump = dump

        OP("qs", lambda e: e.dma_start(out=tri_sb[:], in_=dtri), writes=[b_tri], dma_buf=b_tri)
        OP("qs", lambda e: e.dma_start(out=ident_sb[:], in_=dident), writes=[b_ident], dma_buf=b_ident)
        OP("qs", lambda e: e.dma_start(out=onesb_sb[:], in_=donesb), writes=[b_onesb], dma_buf=b_onesb)
        OP("qs", lambda e: e.dma_start(out=oneh_sb[:], in_=doneh), writes=[b_oneh], dma_buf=b_oneh)
        OP("qs", lambda e: e.dma_start(out=ones64_sb[:], in_=dones64), writes=[b_ones64], dma_buf=b_ones64)
        OP("qs", lambda e: e.dma_start(out=ones128_sb[:], in_=dones128), writes=[b_ones128], dma_buf=b_ones128)
        OP("qs", lambda e: e.dma_start(out=cvec[:], in_=dcvec), writes=[b_cvec], dma_buf=b_cvec)
        C_EPS, C_NHALF, C_NEG1 = cvec[:, 0:1], cvec[:, 1:2], cvec[:, 2:3]
        OP("qs", lambda e: e.dma_start(out=c_sil[:], in_=dc), writes=[b_csil], dma_buf=b_csil)
        OP("qs", lambda e: e.dma_start(out=babc2[:], in_=dba), writes=[b_babc2], dma_buf=b_babc2)
        OP("qs", lambda e: e.dma_start(out=gateb_t[:], in_=dgb), writes=[b_gatebt], dma_buf=b_gatebt)
        OP("qs", lambda e: e.dma_start(out=expb_f[:], in_=dexpb), writes=[b_expbf], dma_buf=b_expbf)
        for g in range(min(4, nsub)):
            OP("qg", lambda e, g=g: e.dma_start(out=xr[g][:], in_=dx[g * 128:(g + 1) * 128, :]),
               writes=[b_xr[g]], dma_buf=b_xr[g])

        OP("dve", lambda e: e.memset(Sst[:], 0.0), writes=[b_S])
        OP("dve", lambda e: e.memset(zst[:], 0.0), writes=[b_z])
        OP("dve", lambda e: e.memset(mrep[:], -1e30), writes=[b_mrep])
        OP("dve", lambda e: e.tensor_scalar(out=babc2[:], in0=babc2[:], scalar1=2.0,
                                            scalar2=None, op0=ALU.add),
           reads=[b_babc2], writes=[b_babc2])
        OP("act", lambda e: e.activation(out=c_t8[:, 0:8], in_=c_sil[:], func=AF.Exp,
                                         scale=C_NEG1),
           reads=[b_csil, b_cvec], writes=[b_ct8])
        OP("dve", lambda e: e.tensor_scalar(out=c_t8[:, 0:8], in0=c_t8[:, 0:8], scalar1=1.0,
                                            scalar2=None, op0=ALU.add),
           reads=[b_ct8], writes=[b_ct8])
        OP("dve", lambda e: e.reciprocal(out=c_t8[:, 8:16], in_=c_t8[:, 0:8]),
           reads=[b_ct8], writes=[b_ct8], exotic=True)
        OP("dve", lambda e: e.tensor_tensor(out=c_sil[:], in0=c_sil[:], in1=c_t8[:, 8:16],
                                            op=ALU.mult),
           reads=[b_csil, b_ct8], writes=[b_csil])

        modrow = sf1[0:1, 0:D]
        wslot = 0

        def stream_piece(dram_ap, ncols=D):
            nonlocal wslot
            sl = wslot % 4
            wslot += 1
            OP("qs", lambda e: e.dma_start(out=wring[sl][:, 0:ncols], in_=dram_ap),
               writes=[b_wring[sl]], dma_buf=b_wring[sl])
            return sl

        def mod_block(m):
            for j in range(DCH):
                sl = stream_piece(dwmod[:, (m * 8 + j) * D:(m * 8 + j + 1) * D])
                for hf in range(2):
                    OP("pe", lambda e, j=j, hf=hf, sl=sl: e.matmul(
                        out=P45[0:1, 512 * hf:512 * (hf + 1)],
                        lhsT=c_sil[:, j:j + 1], rhs=wring[sl][:, 512 * hf:512 * (hf + 1)],
                        start=(j == 0), stop=(j == DCH - 1)),
                       reads=[b_csil, b_wring[sl], b_pmod], writes=[b_pmod])
            OP("act", lambda e: e.copy(out=modrow, in_=P45[0:1, 0:D]),
               reads=[b_pmod], writes=[b_modrow])

        def bcast_row(dst_ap, dstbuf, bias=None, dtype_note=None):
            for hf in range(2):
                OP("pe", lambda e, hf=hf: e.matmul(
                    out=P23[:, 512 * hf:512 * (hf + 1)],
                    lhsT=ones128_sb[:],
                    rhs=modrow[:, 512 * hf:512 * (hf + 1)],
                    start=True, stop=True),
                   reads=[b_ones128, b_modrow, b_pk], writes=[b_pk])
            if bias is None:
                OP("act", lambda e: e.copy(out=dst_ap, in_=P23[:, 0:D]),
                   reads=[b_pk], writes=[dstbuf])
            else:
                OP("act", lambda e: e.activation(out=dst_ap, in_=P23[:, 0:D], func=AF.Copy,
                                                 bias=bias),
                   reads=[b_pk], writes=[dstbuf])

        def colT(dst_ap, dstbuf, bias=None):
            for b2 in range(DCH):
                OP("pe", lambda e, b2=b2: e.matmul(
                    out=p6t[:, b2:b2 + 1], lhsT=modrow[:, 128 * b2:128 * (b2 + 1)],
                    rhs=ones128_sb[0:1, 0:1], start=True, stop=True),
                   reads=[b_modrow, b_ones128, b_p6], writes=[b_p6])
            if bias is None:
                OP("act", lambda e: e.copy(out=dst_ap, in_=p6t[:]),
                   reads=[b_p6], writes=[dstbuf])
            else:
                OP("act", lambda e: e.activation(out=dst_ap, in_=p6t[:], func=AF.Copy,
                                                 bias=bias),
                   reads=[b_p6], writes=[dstbuf])

        def gate_transform():
            OP("act", lambda e: e.activation(out=rowtmp[:], in_=modrow, func=AF.Sign),
               reads=[b_modrow], writes=[b_rowtmp])
            OP("act", lambda e: e.activation(out=modrow, in_=modrow, func=AF.Abs),
               reads=[b_modrow], writes=[b_modrow])
            OP("dve", lambda e: e.tensor_scalar(out=modrow, in0=modrow, scalar1=0.25,
                                                scalar2=None, op0=ALU.max),
               reads=[b_modrow], writes=[b_modrow])
            OP("dve", lambda e: e.tensor_tensor(out=modrow, in0=modrow, in1=rowtmp[:], op=ALU.mult),
               reads=[b_modrow, b_rowtmp], writes=[b_modrow])

        mod_block(0)
        kb.dump(c_sil[:], [b_csil], ncols=8)
        kb.dump(modrow, [b_modrow], nparts=1)
        colT(saT[:], b_saf)
        mod_block(1)
        colT(caT[:], b_ca1, bias=1.0)
        mod_block(2)
        gate_transform()
        bcast_row(gb_exp[:], b_gbexp)
        mod_block(3)
        gate_transform()
        bcast_row(ga_exp[:], b_gaexp)
        mod_block(4)
        colT(sbT[:], b_sbf)
        mod_block(5)
        colT(cbT[:], b_cb1, bias=1.0)
        kb.dump(saT[:], [b_saf], ncols=DCH)
        kb.dump(caT[:], [b_ca1], ncols=DCH)
        kb.dump(gb_exp[:], [b_gbexp])
        kb.dump(ga_exp[:], [b_gaexp])
        kb.dump(sbT[:], [b_sbf], ncols=DCH)
        kb.dump(cbT[:], [b_cb1], ncols=DCH)

        def load_weight(dram, dst, dstbuf, ncols, fold=None, foldbuf=None):
            for j in range(ncols // D):
                sl = stream_piece(dram[:, j * D:(j + 1) * D])
                if fold is None:
                    OP("act", lambda e, j=j, sl=sl: e.copy(out=dst[:, j * D:(j + 1) * D],
                                                           in_=wring[sl][:]),
                       reads=[b_wring[sl]], writes=[dstbuf])
                else:
                    OP("dve", lambda e, j=j, sl=sl: e.tensor_tensor(
                        out=dst[:, j * D:(j + 1) * D], in0=wring[sl][:], in1=fold, op=ALU.mult),
                       reads=[b_wring[sl], foldbuf], writes=[dstbuf])

        load_weight(dwq, wq_bf, b_wq, 8 * D)
        load_weight(dwk, wk_bf, b_wk, 8 * D)
        load_weight(dwv, wv_bf, b_wv, 8 * D)
        sl = stream_piece(dwa, ncols=8 * H)
        OP("act", lambda e, sl=sl: e.copy(out=wa_bf[:], in_=wring[sl][:, 0:8 * H]),
           reads=[b_wring[sl]], writes=[b_wa])
        sl = stream_piece(dgw, ncols=8 * E)
        OP("act", lambda e, sl=sl: e.copy(out=gw_bf[:], in_=wring[sl][:, 0:8 * E]),
           reads=[b_wring[sl]], writes=[b_gw])
        load_weight(dwo, wo_bf, b_wo, 8 * D, fold=ga_exp[:], foldbuf=b_gaexp)
        OP("dve", lambda e: e.tensor_tensor(out=expb_bf[:], in0=expb_f[:], in1=gb_exp[0:E, :],
                                            op=ALU.mult),
           reads=[b_expbf, b_gbexp], writes=[b_expbbf])

        tr_cnt = [0]

        def transposes(src_ap, srcbufs, dst_ap, dstbuf):
            blk = srcbufs if isinstance(srcbufs, list) else [srcbufs] * DCH
            for u in range(DCH):
                OP("pe", lambda e, u=u: e.transpose(out=P7[:, 128 * u:128 * (u + 1)],
                                                    in_=src_ap[:, 128 * u:128 * (u + 1)],
                                                    identity=ident_sb[:]),
                   reads=[blk[u], b_ident], writes=[b_p7])
            for hf in range(2):
                OP("act", lambda e, hf=hf: e.copy(out=dst_ap[:, 512 * hf:512 * (hf + 1)],
                                                  in_=P7[:, 512 * hf:512 * (hf + 1)]),
                   reads=[b_p7], writes=[dstbuf[hf]])

        def proj(h_fm_ap, h_fm_buf, w_sb, w_buf, pdst, pbuf, nout):
            nhalf = max(1, nout // 512)
            step = min(nout, 512)
            blk = h_fm_buf if isinstance(h_fm_buf, list) else [h_fm_buf] * DCH
            for j in range(DCH):
                for hf in range(nhalf):
                    OP("pe", lambda e, j=j, hf=hf: e.matmul(
                        out=pdst[:, hf * step:(hf + 1) * step],
                        lhsT=h_fm_ap[:, 128 * j:128 * (j + 1)],
                        rhs=w_sb[:, j * nout + hf * step: j * nout + (hf + 1) * step],
                        start=(j == 0), stop=(j == DCH - 1)),
                       reads=[blk[j], w_buf, pbuf], writes=[pbuf])

        def ln_stats(x_ap, xbuf):
            OP("dve", lambda e: e.bn_stats(out=stat[:, 0:6], in_=x_ap[:, 0:512]),
               reads=[xbuf], writes=[b_stat], exotic=True)
            OP("dve", lambda e: e.bn_stats(out=stat[:, 6:12], in_=x_ap[:, 512:1024]),
               reads=[xbuf], writes=[b_stat], exotic=True)
            OP("dve", lambda e: e.bn_aggr(out=stat[:, 12:14], in_=stat[:, 0:12]),
               reads=[b_stat], writes=[b_stat], exotic=True)
            OP("act", lambda e: e.activation(out=stat2[:, 0:1], in_=stat[:, 13:14], func=AF.Ln,
                                             bias=C_EPS),
               reads=[b_stat, b_cvec], writes=[b_stat2])
            OP("act", lambda e: e.activation(out=stat2[:, 1:2], in_=stat2[:, 0:1], func=AF.Exp,
                                             scale=C_NHALF),
               reads=[b_stat2, b_cvec], writes=[b_stat2])

        def ln_norm(x_ap, xbuf, dst_tm, dstbuf):
            OP("dve", lambda e: e.tensor_scalar(out=dst_tm, in0=x_ap,
                                                scalar1=stat[:, 12:13],
                                                scalar2=stat2[:, 1:2],
                                                op0=ALU.subtract, op1=ALU.mult),
               reads=[xbuf, b_stat, b_stat2], writes=[dstbuf])

        def elu_scale(psrc, psbuf, elbf_ap, elbf_buf, dst, dstbufs, on_dve=False):
            if on_dve:
                OP("dve", lambda e: e.tensor_scalar(out=s1[:], in0=psrc, scalar1=0.0,
                                                    scalar2=None, op0=ALU.min),
                   reads=[psbuf], writes=[b_s1])
                OP("act", lambda e: e.activation(out=s2[:], in_=s1[:], func=AF.Exp),
                   reads=[b_s1], writes=[b_s2])
                OP("dve", lambda e: e.tensor_scalar(out=s1[:], in0=psrc, scalar1=0.0,
                                                    scalar2=None, op0=ALU.max),
                   reads=[psbuf], writes=[b_s1])
            else:
                OP("act", lambda e: e.activation(out=s1[:], in_=psrc, func=AF.Relu,
                                                 scale=C_NEG1),
                   reads=[psbuf, b_cvec], writes=[b_s1])
                OP("act", lambda e: e.activation(out=s2[:], in_=s1[:], func=AF.Exp,
                                                 scale=C_NEG1),
                   reads=[b_s1, b_cvec], writes=[b_s2])
                OP("act", lambda e: e.activation(out=s1[:], in_=psrc, func=AF.Relu),
                   reads=[psbuf], writes=[b_s1])
            OP("dve", lambda e: e.tensor_tensor(out=dst, in0=s2[:], in1=s1[:], op=ALU.add),
               reads=[b_s1, b_s2], writes=list(dstbufs))
            rep = elbf_ap.unsqueeze(2).to_broadcast([128, H, 64])
            d3 = dst.rearrange("p (h o) -> p h o", h=H)
            OP("dve", lambda e, rep=rep, d3=d3: e.tensor_tensor(out=d3, in0=d3, in1=rep,
                                                                op=ALU.mult),
               reads=list(dstbufs) + [elbf_buf], writes=list(dstbufs))

        ln1_done = set()
        h1t_done = set()
        stts_done = set()
        pe_backlog = []

        def emit_backlog():
            for f in pe_backlog:
                f()
            pe_backlog.clear()

        def ln1_parts(g):
            xa = xr[g % 4][:]
            xb = b_xr[g % 4]

            def p1():
                OP("dve", lambda e: e.bn_stats(out=stat[:, 0:6], in_=xa[:, 0:512]),
                   reads=[xb], writes=[b_stat], exotic=True)

            def p2():
                OP("dve", lambda e: e.bn_stats(out=stat[:, 6:12], in_=xa[:, 512:1024]),
                   reads=[xb], writes=[b_stat], exotic=True)

            def p3():
                OP("dve", lambda e: e.bn_aggr(out=stat[:, 12:14], in_=stat[:, 0:12]),
                   reads=[b_stat], writes=[b_stat], exotic=True)
                OP("act", lambda e: e.activation(out=stat2[:, 0:1], in_=stat[:, 13:14],
                                                 func=AF.Ln, bias=C_EPS),
                   reads=[b_stat, b_cvec], writes=[b_stat2])
                OP("act", lambda e: e.activation(out=stat2[:, 1:2], in_=stat2[:, 0:1],
                                                 func=AF.Exp, scale=C_NHALF),
                   reads=[b_stat2, b_cvec], writes=[b_stat2])

            def p4():
                ln_norm(xa, xb, h1_bf[:], b_h1bf)
                ln1_done.add(g)

            return [p1, p2, p3, p4]

        def ln1(g):
            for p in ln1_parts(g):
                p()

        def h1t(g):
            for u in range(DCH):
                OP("pe", lambda e, u=u: e.transpose(out=P7[:, 128 * u:128 * (u + 1)],
                                                    in_=h1_bf[:, 128 * u:128 * (u + 1)],
                                                    identity=ident_sb[:]),
                   reads=[b_h1bf, b_ident], writes=[b_p7])
            OP("dve", lambda e: e.tensor_copy(out=h1_bf[:], in_=P7[:]),
               reads=[b_p7], writes=[b_h1bf])
            h1t_done.add(g)

        def h1_stts(g):
            for u in range(DCH):
                sa_b = saT[:, u:u + 1].to_broadcast([128, 128])
                OP("dve", lambda e, u=u, sa_b=sa_b: e.scalar_tensor_tensor(
                    out=h1_fm[:, 128 * u:128 * (u + 1)],
                    in0=h1_bf[:, 128 * u:128 * (u + 1)],
                    scalar=caT[:, u:u + 1], in1=sa_b, op0=ALU.mult, op1=ALU.add),
                   reads=[b_h1bf, b_ca1, b_saf], writes=[b_h1fm[u]])
            stts_done.add(g)

        front_done = set()

        def attn_front(g):
            for j in range(DCH):
                OP("pe", lambda e, j=j: e.matmul(
                    out=p6a, lhsT=h1_fm[:, 128 * j:128 * (j + 1)],
                    rhs=wa_bf[:, j * H:(j + 1) * H],
                    start=(j == 0), stop=(j == DCH - 1)),
                   reads=[b_h1fm[j], b_wa, b_p6], writes=[b_p6])
            OP("dve", lambda e: e.tensor_tensor(out=u16[:], in0=p6a, in1=babc2[:], op=ALU.add),
               reads=[b_p6, b_babc2], writes=[b_u16])
            OP("act", lambda e: e.activation(out=t16a[:], in_=u16[:], func=AF.Abs),
               reads=[b_u16], writes=[b_t16a])
            OP("act", lambda e: e.activation(out=t16b[:], in_=t16a[:], func=AF.Exp, scale=C_NEG1),
               reads=[b_t16a, b_cvec], writes=[b_t16b])
            OP("act", lambda e: e.activation(out=t16a[:], in_=t16b[:], func=AF.Ln, bias=1.0),
               reads=[b_t16b], writes=[b_t16a])
            OP("dve", lambda e: e.tensor_scalar(out=t16b[:], in0=u16[:], scalar1=0.0,
                                                scalar2=None, op0=ALU.min),
               reads=[b_u16], writes=[b_t16b])
            OP("dve", lambda e: e.tensor_tensor(out=lna[:], in0=t16b[:], in1=t16a[:],
                                                op=ALU.subtract),
               reads=[b_t16a, b_t16b], writes=[b_lna])
            front_done.add(g)

        def attn_subchunk(g):
            xa = xr[g % 4][:]
            xb = b_xr[g % 4]
            if g not in ln1_done:
                ln1(g)
            if g not in h1t_done:
                h1t(g)
            if g not in stts_done:
                h1_stts(g)
            if g not in front_done:
                attn_front(g)
            proj(h1_fm[:], b_h1fm, wk_bf, b_wk, P23, b_pk, D)
            OP("pe", lambda e: e.matmul(out=p6L, lhsT=tri_sb[:], rhs=lna[:], start=True,
                                        stop=True),
               reads=[b_tri, b_lna, b_p6], writes=[b_p6])
            OP("act", lambda e: e.activation(out=el[:], in_=p6L, func=AF.Exp),
               reads=[b_p6], writes=[b_el])
            OP("act", lambda e: e.activation(out=enl[:], in_=p6L, func=AF.Exp, scale=C_NEG1),
               reads=[b_p6, b_cvec], writes=[b_enl])
            OP("act", lambda e: e.copy(out=el_bf[:], in_=el[:]), reads=[b_el], writes=[b_elbf])
            OP("act", lambda e: e.copy(out=enl_bf[:], in_=enl[:]), reads=[b_enl], writes=[b_enlbf])
            OP("pe", lambda e: e.matmul(out=p6er, lhsT=oneh_sb[:], rhs=el[:], start=True,
                                        stop=True),
               reads=[b_oneh, b_el, b_p6], writes=[b_p6])
            OP("act", lambda e: e.copy(out=elcrow[:], in_=p6er), reads=[b_p6], writes=[b_elcrow])
            OP("pe", lambda e: e.matmul(out=p6eb, lhsT=ones128_sb[:], rhs=elcrow[:], start=True,
                                        stop=True),
               reads=[b_ones128, b_elcrow, b_p6], writes=[b_p6])
            OP("act", lambda e: e.copy(out=elc_bf[:], in_=p6eb), reads=[b_p6], writes=[b_elcbf])
            proj(h1_fm[:], b_h1fm, wq_bf, b_wq, P01, b_pq, D)
            elu_scale(P23[:], b_pk, enl_bf[:], b_enlbf, kt_tm[:], b_kttm, on_dve=True)
            emit_backlog()
            proj(h1_fm[:], b_h1fm, wv_bf, b_wv, P45, b_pv, D)
            elu_scale(P01[:], b_pq, el_bf[:], b_elbf, qt_tm[:], b_qttm)
            transposes(kt_tm[:], b_kttm, kt_fm, b_ktfm)
            transposes(qt_tm[:], b_qttm, qt_fm, b_qtfm)
            OP("act", lambda e: e.copy(out=v_tm[:], in_=P45[:]), reads=[b_pv], writes=[b_vtm])

            def em_scores(h):
                u, half = h // 2, h % 2
                base = 64 * half
                qsl = qt_fm[base:base + 64, 128 * u:128 * (u + 1)]
                ksl = kt_fm[base:base + 64, 128 * u:128 * (u + 1)]
                sc = scA if h % 2 == 0 else scB
                bsc = b_scA if h % 2 == 0 else b_scB
                OP("pe", lambda e, sc=sc, ksl=ksl, qsl=qsl: e.matmul(
                    out=sc, lhsT=ksl, rhs=qsl, start=True, stop=True),
                   reads=[b_ktfm[u // 4], b_qtfm[u // 4], bsc], writes=[bsc])

            def em_mask(h):
                sc = scA if h % 2 == 0 else scB
                bsc = b_scA if h % 2 == 0 else b_scB
                OP("dve", lambda e, sc=sc, h=h: e.tensor_tensor(
                    out=smt[h % 2][:], in0=sc, in1=tri_sb[:], op=ALU.mult),
                   reads=[bsc, b_tri], writes=[b_smt[h % 2]])

            def em_rest(h):
                u, half = h // 2, h % 2
                base = 64 * half
                qsl = qt_fm[base:base + 64, 128 * u:128 * (u + 1)]
                OP("pe", lambda e, h=h: e.matmul(
                    out=P45[:, 64 * h:64 * (h + 1)], lhsT=smt[h % 2][:],
                    rhs=v_tm[:, 64 * h:64 * (h + 1)], start=True, stop=False),
                   reads=[b_smt[h % 2], b_vtm, b_po], writes=[b_po])
                OP("pe", lambda e, h=h, qsl=qsl, base=base: e.matmul(
                    out=P45[:, 64 * h:64 * (h + 1)], lhsT=qsl,
                    rhs=Sst[base:base + 64, 64 * h:64 * (h + 1)], start=False, stop=True),
                   reads=[b_qtfm[u // 4], b_S, b_po], writes=[b_po])
                OP("pe", lambda e, h=h: e.matmul(
                    out=p6den[:, h:h + 1], lhsT=smt[h % 2][:], rhs=onesb_sb[:],
                    start=True, stop=False),
                   reads=[b_smt[h % 2], b_onesb, b_p6], writes=[b_p6])
                OP("pe", lambda e, h=h, qsl=qsl, base=base: e.matmul(
                    out=p6den[:, h:h + 1], lhsT=qsl, rhs=zst[base:base + 64, h:h + 1],
                    start=False, stop=True),
                   reads=[b_qtfm[u // 4], b_z, b_p6], writes=[b_p6])
                OP("pe", lambda e, h=h: e.matmul(
                    out=P23[0:64, 64 * h:64 * (h + 1)],
                    lhsT=kt_tm[:, 64 * h:64 * (h + 1)],
                    rhs=v_tm[:, 64 * h:64 * (h + 1)], start=True, stop=True),
                   reads=[b_kttm[h // 2], b_vtm, b_kvS], writes=[b_kvS])
                OP("pe", lambda e, h=h: e.matmul(
                    out=p6kvz[0:64, h:h + 1], lhsT=kt_tm[:, 64 * h:64 * (h + 1)],
                    rhs=onesb_sb[:], start=True, stop=True),
                   reads=[b_kttm[h // 2], b_onesb, b_p6], writes=[b_p6])

            ln1_next = ln1_parts(g + 1) if g + 1 < nsub else None
            em_scores(0)
            for h in range(H):
                em_mask(h)
                if h + 1 < H:
                    em_scores(h + 1)
                em_rest(h)
                if h in (2, 5, 8, 11) and g + 1 < nsub:
                    ln1_next[(h - 2) // 3]()
            OP("dve", lambda e: e.tensor_scalar(out=den_sb[:], in0=p6den, scalar1=1e-6,
                                                scalar2=None, op0=ALU.add),
               reads=[b_p6], writes=[b_den])
            OP("dve", lambda e: e.reciprocal(out=rden[:], in_=den_sb[:]),
               reads=[b_den], writes=[b_rden], exotic=True)
            OP("act", lambda e: e.copy(out=kvbf[0:64, :], in_=P23[0:64, :]),
               reads=[b_kvS], writes=[b_kvbf])
            OP("act", lambda e: e.copy(out=zvbf[0:64, :], in_=p6kvz[0:64, :]),
               reads=[b_p6, b_den], writes=[b_zvbf])
            warm(P01[0:1, 0:1], [b_den], b_pq)
            if g + 1 < nsub:
                h1t(g + 1)
            warm(P01[0:1, 0:1], [b_rden], b_pq)
            warm(P01[0:1, 0:1], [b_kvbf], b_pq)
            OP("pool", lambda e: e.tensor_tensor(out=Sst[0:64, :], in0=Sst[0:64, :],
                                                 in1=kvbf[0:64, :], op=ALU.add),
               reads=[b_S, b_kvbf], writes=[b_S])
            elcrep = elc_bf[0:64, :].unsqueeze(2).to_broadcast([64, H, 64])
            OP("pool", lambda e, elcrep=elcrep: e.tensor_tensor(
                out=Sst[0:64, :].rearrange("p (h o) -> p h o", h=H),
                in0=Sst[0:64, :].rearrange("p (h o) -> p h o", h=H),
                in1=elcrep, op=ALU.mult),
               reads=[b_S, b_elcbf], writes=[b_S])
            OP("pool", lambda e: e.tensor_tensor(out=zst[0:64, :], in0=zst[0:64, :],
                                                 in1=zvbf[0:64, :], op=ALU.add),
               reads=[b_z, b_zvbf], writes=[b_z])
            OP("pool", lambda e: e.tensor_tensor(
                out=zst[0:64, :], in0=zst[0:64, :], in1=elc_bf[0:64, :], op=ALU.mult),
               reads=[b_z, b_elcbf], writes=[b_z])
            OP("pool", lambda e: e.tensor_copy(out=Sst[64:128, :], in_=Sst[0:64, :]),
               reads=[b_S], writes=[b_S])
            OP("pool", lambda e: e.tensor_copy(out=zst[64:128, :], in_=zst[0:64, :]),
               reads=[b_z], writes=[b_z])
            for i, u in enumerate(AO_SEQ):
                for h in (2 * u, 2 * u + 1):
                    if h < 8:
                        OP("act", lambda e, h=h: e.activation(
                            out=ao_tm[:, 64 * h:64 * (h + 1)],
                            in_=P45[:, 64 * h:64 * (h + 1)],
                            func=AF.Copy, scale=rden[:, h:h + 1]),
                           reads=[b_po, b_rden], writes=[b_aotm[u]])
                    else:
                        OP("dve", lambda e, h=h: e.tensor_scalar(
                            out=ao_tm[:, 64 * h:64 * (h + 1)],
                            in0=P45[:, 64 * h:64 * (h + 1)],
                            scalar1=rden[:, h:h + 1], scalar2=None, op0=ALU.mult),
                           reads=[b_po, b_rden], writes=[b_aotm[u]])
                OP("pe", lambda e, i=i, u=u: e.transpose(
                    out=P7[:, 128 * i:128 * (i + 1)], in_=ao_tm[:, 128 * u:128 * (u + 1)],
                    identity=ident_sb[:]),
                   reads=[b_aotm[u], b_ident, b_p7], writes=[b_p7])
                if i in (2, 5):
                    warm(P01[0:1, 0:1], [b_aotm[u]], b_py)
            if g + 1 < nsub:
                h1_stts(g + 1)
            for hf in range(2):
                OP("act", lambda e, hf=hf: e.copy(out=ao_fm[:, 512 * hf:512 * (hf + 1)],
                                                  in_=P7[:, 512 * hf:512 * (hf + 1)]),
                   reads=[b_p7], writes=[b_aofm[hf]])
            proj(ao_fm[:], [b_aofm[j // 4] for j in range(DCH)], wo_bf, b_wo, P01, b_py, D)
            OP("dve", lambda e, xa=xa: e.tensor_tensor(out=xa, in0=P01[:], in1=xa, op=ALU.add),
               reads=[b_py, xb], writes=[xb])
            s = g % SPC
            ln_stats(xa, xb)
            ln_norm(xa, xb, h2_tm[:], b_h2tm)
            if g + 1 < nsub:
                attn_front(g + 1)
            def h2t(s=s):
                for u in range(DCH):
                    OP("pe", lambda e, u=u, s=s: e.transpose(
                        out=P7[:, 128 * u:128 * (u + 1)],
                        in_=h2_tm[:, 128 * u:128 * (u + 1)], identity=ident_sb[:]),
                       reads=[b_h2tm, b_ident], writes=[b_p7])
                OP("act", lambda e, s=s: e.copy(out=h2_tm[:], in_=P7[:]),
                   reads=[b_p7], writes=[b_h2tm])
                for u in range(DCH):
                    sb_b = sbT[:, u:u + 1].to_broadcast([128, 128])
                    OP("dve", lambda e, u=u, s=s, sb_b=sb_b: e.scalar_tensor_tensor(
                        out=h2_fm[:, s * D + 128 * u:s * D + 128 * (u + 1)],
                        in0=h2_tm[:, 128 * u:128 * (u + 1)],
                        scalar=cbT[:, u:u + 1], in1=sb_b, op0=ALU.mult, op1=ALU.add),
                       reads=[b_h2tm, b_cb1, b_sbf], writes=[b_h2fm[s][u]])
            pe_backlog.append(h2t)

        piece_ctr = [0]
        fcvt = [0]

        def gate_comb(s):
            ga_, gb_, gc_, gd_, ge_, gf_ = (g8[:, i * E:(i + 1) * E] for i in range(6))
            OP("dve", lambda e: e.tensor_tensor(out=ga_, in0=p6g, in1=gateb_t[:], op=ALU.add),
               reads=[b_p6, b_gatebt], writes=[b_g8])
            OP("dve", lambda e: e.tensor_scalar(out=ga_, in0=ga_, scalar1=1e-4,
                                                scalar2=None, op0=ALU.max),
               reads=[b_g8], writes=[b_g8])
            OP("dve", lambda e: e.tensor_reduce(out=m1c[:, 0:1], in_=ga_, axis=AX.X,
                                                op=ALU.max),
               reads=[b_g8], writes=[b_m1c], exotic=True)
            OP("dve", lambda e: e.tensor_scalar(out=m1c[:, 1:2], in0=m1c[:, 0:1],
                                                scalar1=-1.0, scalar2=None, op0=ALU.mult),
               reads=[b_m1c], writes=[b_m1c])
            OP("act", lambda e: e.activation(out=gb_, in_=ga_, func=AF.Exp,
                                             bias=m1c[:, 1:2]),
               reads=[b_g8, b_m1c], writes=[b_g8])
            OP("dve", lambda e: e.tensor_reduce(out=m1c[:, 0:1], in_=gb_, axis=AX.X,
                                                op=ALU.add),
               reads=[b_g8], writes=[b_m1c], exotic=True)
            OP("dve", lambda e: e.reciprocal(out=m1c[:, 1:2], in_=m1c[:, 0:1]),
               reads=[b_m1c], writes=[b_m1c], exotic=True)
            OP("dve", lambda e: e.tensor_scalar(out=ga_, in0=gb_, scalar1=m1c[:, 1:2],
                                                scalar2=None, op0=ALU.mult),
               reads=[b_g8, b_m1c], writes=[b_g8])
            OP("dve", lambda e: e.tensor_reduce(out=mrep[:, 0:1], in_=ga_, axis=AX.X,
                                                op=ALU.max),
               reads=[b_g8], writes=[b_mrep], exotic=True)
            OP("dve", lambda e: e.match_replace(out=gb_, in_to_replace=mrep[:],
                                                in_values=ga_, imm_value=-1e30),
               reads=[b_g8, b_mrep], writes=[b_g8], exotic=True)
            OP("dve", lambda e: e.tensor_tensor(out=gc_, in0=ga_, in1=gb_, op=ALU.not_equal),
               reads=[b_g8], writes=[b_g8])
            OP("dve", lambda e: e.tensor_reduce(out=mrep[:, 0:1], in_=gb_, axis=AX.X,
                                                op=ALU.max),
               reads=[b_g8], writes=[b_mrep], exotic=True)
            OP("dve", lambda e: e.match_replace(out=gd_, in_to_replace=mrep[:],
                                                in_values=gb_, imm_value=-1e30),
               reads=[b_g8, b_mrep], writes=[b_g8], exotic=True)
            OP("dve", lambda e: e.tensor_tensor(out=ge_, in0=gb_, in1=gd_, op=ALU.not_equal),
               reads=[b_g8], writes=[b_g8])
            OP("dve", lambda e: e.tensor_tensor(out=gc_, in0=gc_, in1=ge_, op=ALU.add),
               reads=[b_g8], writes=[b_g8])
            OP("dve", lambda e: e.tensor_scalar(out=gf_, in0=ga_, scalar1=0.1,
                                                scalar2=None, op0=ALU.max),
               reads=[b_g8], writes=[b_g8])
            OP("dve", lambda e: e.tensor_tensor(out=gf_, in0=gf_, in1=gc_, op=ALU.mult),
               reads=[b_g8], writes=[b_g8])
            OP("dve", lambda e: e.tensor_reduce(out=m1c[:, 0:1], in_=gf_, axis=AX.X,
                                                op=ALU.add),
               reads=[b_g8], writes=[b_m1c], exotic=True)
            OP("dve", lambda e: e.reciprocal(out=m1c[:, 1:2], in_=m1c[:, 0:1]),
               reads=[b_m1c], writes=[b_m1c], exotic=True)
            OP("dve", lambda e, s=s: e.tensor_scalar(
                out=combc[:, s * E:(s + 1) * E], in0=gf_, scalar1=m1c[:, 1:2],
                scalar2=0.5, op0=ALU.mult, op1=ALU.mult),
               reads=[b_g8, b_m1c], writes=[b_comb])
            OP("dve", lambda e, s=s: e.tensor_copy(out=comb_bfx[:, s * E:(s + 1) * E],
                                                   in_=combc[:, s * E:(s + 1) * E]),
               reads=[b_comb], writes=[b_combbfx])

        def combT_biasmix(s, xa, xb):
            OP("pe", lambda e, s=s: e.transpose(out=P7[0:8, 0:128],
                                                in_=comb_bfx[:, s * E:(s + 1) * E],
                                                identity=ident_sb[:]),
               reads=[b_combbfx, b_ident, b_p7], writes=[b_p7])
            OP("act", lambda e, s=s: e.copy(out=combT_bf[:, s * 128:(s + 1) * 128],
                                            in_=P7[0:8, 0:128]),
               reads=[b_p7], writes=[b_combT])
            for hf in range(2):
                OP("pe", lambda e, hf=hf, s=s: e.matmul(
                    out=P45[:, 512 * hf:512 * (hf + 1)],
                    lhsT=combT_bf[:, s * 128:(s + 1) * 128],
                    rhs=expb_bf[:, 512 * hf:512 * (hf + 1)], start=True, stop=True),
                   reads=[b_combT, b_expbbf, b_pbm], writes=[b_pbm])
            OP("dve", lambda e, xa=xa: e.tensor_tensor(out=xa, in0=P45[:], in1=xa,
                                                       op=ALU.add),
               reads=[b_pbm, xb], writes=[xb])

        def ffn_chunk(c):
            g0 = c * SPC

            def gates(s):
                for j in range(DCH):
                    OP("pe", lambda e, j=j, s=s: e.matmul(
                        out=p6g, lhsT=h2_fm[:, s * D + 128 * j:s * D + 128 * (j + 1)],
                        rhs=gw_bf[:, j * E:(j + 1) * E],
                        start=(j == 0), stop=(j == DCH - 1)),
                       reads=[b_h2fm[s][j], b_gw, b_p6], writes=[b_p6])

            ptiles = (P01, P23, P45)
            pbufs = (b_fe0, b_fe1, b_fe2)

            def piece_dma(e_, q2):
                pc = piece_ctr[0]
                piece_ctr[0] += 1
                sl = pc % 4
                if c == 0:
                    for w_ in range(4):
                        j = 4 * q2 + w_
                        fsl = fcvt[0] % 4
                        fcvt[0] += 1
                        OP("qs", lambda e, e_=e_, j=j, fsl=fsl: e.dma_start(
                            out=wring[fsl][:], in_=dexpw[e_, :, j * D:(j + 1) * D]),
                           writes=[b_wring[fsl]], dma_buf=b_wring[fsl])
                        OP("dve", lambda e, sl=sl, w_=w_, fsl=fsl: e.tensor_tensor(
                            out=ering[sl][:, w_ * D:(w_ + 1) * D], in0=wring[fsl][:],
                            in1=gb_exp[:], op=ALU.mult),
                           reads=[b_wring[fsl], b_gbexp], writes=[b_ering[sl]])
                    OP("qg", lambda e, e_=e_, q2=q2, sl=sl: e.dma_start(
                        out=decache[e_, :, q2 * 4096:(q2 + 1) * 4096], in_=ering[sl][:]),
                       reads=[b_ering[sl]], writes=[b_ecache[e_][q2]],
                       dma_buf=b_ering[sl])
                else:
                    OP("qs", lambda e, e_=e_, q2=q2, sl=sl: e.dma_start(
                        out=ering[sl][:], in_=decache[e_, :, q2 * 4096:(q2 + 1) * 4096]),
                       reads=[b_ecache[e_][q2]], writes=[b_ering[sl]],
                       dma_buf=b_ering[sl])
                return sl

            def piece_mms(e_, q2, sl, subs, hfs=(0, 1)):
                for w_ in range(4):
                    j = 4 * q2 + w_
                    for s in subs:
                        pt = ptiles[(2 * e_ + s) % 3]
                        pb = pbufs[(2 * e_ + s) % 3]
                        for hf in hfs:
                            OP("pe", lambda e, j=j, s=s, w_=w_, hf=hf, sl=sl, pt=pt: e.matmul(
                                out=pt[:, 512 * hf:512 * (hf + 1)],
                                lhsT=h2_fm[:, s * D + 128 * j:s * D + 128 * (j + 1)],
                                rhs=ering[sl][:, w_ * D + 512 * hf:w_ * D + 512 * (hf + 1)],
                                start=(j == 0), stop=(j == DCH - 1)),
                               reads=[b_h2fm[s][j], b_ering[sl], pb], writes=[pb])

            def expert_mms(e_, sls, subs):
                for q2 in range(2):
                    piece_mms(e_, q2, sls[q2], subs)
                for s in subs:
                    g = g0 + s
                    xa = xr[g % 4][:]
                    xb = b_xr[g % 4]
                    pt = ptiles[(2 * e_ + s) % 3]
                    pb = pbufs[(2 * e_ + s) % 3]
                    OP("dve", lambda e, xa=xa, pt=pt, s=s, e_=e_: e.scalar_tensor_tensor(
                        out=xa, in0=pt[:], scalar=combc[:, s * E + e_:s * E + e_ + 1],
                        in1=xa, op0=ALU.mult, op1=ALU.add),
                       reads=[pb, b_comb, xb], writes=[xb])

            def stts(e_):
                pass

            gates(0)
            gate_comb(0)
            sls = [piece_dma(0, 0)]
            piece_mms(0, 0, sls[0], [0])
            warm(P23[0:1, 0:1], [b_stat2], b_fe1)
            warm(P23[0:1, 0:1], [b_h2tm], b_fe1)
            emit_backlog()
            gates(1)
            gate_comb(1)
            sls += [piece_dma(0, 1)]
            piece_mms(0, 0, sls[0], [1])
            piece_mms(0, 1, sls[1], [0, 1])
            for s in range(SPC):
                g = g0 + s
                combT_biasmix(s, xr[g % 4][:], b_xr[g % 4])
            for s in range(SPC):
                g = g0 + s
                xa = xr[g % 4][:]
                xb = b_xr[g % 4]
                pt = ptiles[s % 3]
                pb = pbufs[s % 3]
                OP("dve", lambda e, xa=xa, pt=pt, s=s: e.scalar_tensor_tensor(
                    out=xa, in0=pt[:], scalar=combc[:, s * E:s * E + 1],
                    in1=xa, op0=ALU.mult, op1=ALU.add),
                   reads=[pb, b_comb, xb], writes=[xb])
            for e_ in range(1, E):
                sls = [piece_dma(e_, q2) for q2 in range(2)]
                expert_mms(e_, sls, [0, 1])

        for c in range(nch):
            for s in range(SPC):
                attn_subchunk(c * SPC + s)
            ffn_chunk(c)
            for s in range(SPC):
                g = c * SPC + s
                OP("qg", lambda e, g=g: e.dma_start(
                    out=dy[g * 128:(g + 1) * 128, :], in_=xr[g % 4][:]),
                   reads=[b_xr[g % 4]], writes=[b_y], dma_buf=b_xr[g % 4])
            for s in range(SPC):
                g = (c + 2) * SPC + s
                if g < nsub:
                    OP("qg", lambda e, g=g: e.dma_start(
                        out=xr[g % 4][:], in_=dx[g * 128:(g + 1) * 128, :]),
                       writes=[b_xr[g % 4]], dma_buf=b_xr[g % 4])

        kb.final_wait("qs")

    es.close()
    return nc



def _wlayout(w):
    fdim = w.shape[1]
    return np.ascontiguousarray(
        w.reshape(8, 128, fdim).transpose(1, 0, 2).reshape(128, 8 * fdim))


def kernel(L0_attn_x, L0_attn_Wq, L0_attn_Wk, L0_attn_Wv, L0_attn_Wo, L0_attn_Wa,
           L0_attn_ba, L0_ffn_gate_W, L0_ffn_gate_b, L0_ffn_expert_W, L0_ffn_expert_b,
           L0_c, L0_wmod):
    n_tokens = L0_attn_x.shape[1]
    nc = build_nc(n_tokens)

    wq = _wlayout(np.asarray(L0_attn_Wq, np.float32))
    wk = _wlayout(np.asarray(L0_attn_Wk, np.float32))
    wv = _wlayout(np.asarray(L0_attn_Wv, np.float32))
    wo = _wlayout(np.asarray(L0_attn_Wo, np.float32))
    wo = np.ascontiguousarray(
        wo.reshape(128, 8, D)[:, list(AO_SEQ), :].reshape(128, 8 * D))
    wa = _wlayout(np.asarray(L0_attn_Wa, np.float32))
    gw = _wlayout(np.asarray(L0_ffn_gate_W, np.float32))
    expw = np.stack([_wlayout(np.asarray(L0_ffn_expert_W[e], np.float32)) for e in range(E)])
    expb = np.ascontiguousarray(np.asarray(L0_ffn_expert_b, np.float32))
    wm = np.asarray(L0_wmod, np.float32).reshape(1024, 6, 1024)[:, [0, 1, 5, 2, 3, 4], :]
    wm = np.ascontiguousarray(
        wm.reshape(8, 128, 6, 1024).transpose(1, 2, 0, 3).reshape(128, 48 * 1024))
    ba_bc = np.ascontiguousarray(
        np.broadcast_to(np.asarray(L0_attn_ba, np.float32), (128, H)))
    gateb_bc = np.ascontiguousarray(
        np.broadcast_to(np.asarray(L0_ffn_gate_b, np.float32), (128, E)))
    tri = np.triu(np.ones((128, 128), dtype=np.float32))
    identbf = np.eye(128, dtype=np.float32).astype(ml_dtypes.bfloat16)
    onesbf = np.ones((128, 1), dtype=np.float32).astype(ml_dtypes.bfloat16)
    oneh127 = np.zeros((128, 1), dtype=np.float32)
    oneh127[127, 0] = 1.0
    ones1x64 = np.ones((1, 64), dtype=np.float32)
    ones1x128 = np.ones((1, 128), dtype=np.float32)
    cvec_h = np.zeros((128, 4), dtype=np.float32)
    cvec_h[:, 0] = 1e-5
    cvec_h[:, 1] = -0.5
    cvec_h[:, 2] = -1.0

    shared = dict(wq=wq, wk=wk, wv=wv, wo=wo, wa=wa, gw=gw, expw=expw, expb=expb,
                  wmod=wm, ba_bc=ba_bc, gateb_bc=gateb_bc, tri=tri, identbf=identbf,
                  onesbf=onesbf, oneh127=oneh127, ones1x64=ones1x64,
                  ones1x128=ones1x128, cvec=cvec_h)
    in_maps = []
    xf = np.asarray(L0_attn_x, np.float32)
    for b in range(8):
        m = dict(shared)
        m["x"] = np.ascontiguousarray(xf[b, :n_tokens])
        m["cmod"] = np.ascontiguousarray(
            np.asarray(L0_c[b], np.float32).reshape(8, 128).T)
        in_maps.append(m)

    res = run_bass_kernel_spmd(nc, in_maps, core_ids=list(range(8)))
    out = np.stack([r["y"] for r in res.results])
    return out.astype(np.float32)
